# version 14
# baseline (speedup 1.0000x reference)
import os
os.environ.setdefault("TRN_ALLOW_ICI_ALL_TO_ALL", "1")

import math
from contextlib import ExitStack

import numpy as np
import ml_dtypes

import concourse.bass as bass
import concourse.mybir as mybir
from concourse.bass_utils import run_bass_kernel_spmd

F32 = mybir.dt.float32
BF16 = mybir.dt.bfloat16
AF = mybir.ActivationFunctionType
ALU = mybir.AluOpType
AX = mybir.AxisListType.X

NCORES = 8
B, T, C = 2, 8192, 1024
TOK = T * B // NCORES
HALO = 1024
NHEAD, HD = 16, 64
NEG = -1.0e30


ENGS = ("t", "s", "v", "g", "y")
NDMACH = 6


class TK:
    def __init__(self, nc, ctx):
        self.nc = nc
        self.sems = {}
        self.cnt = {}
        self.ops = {e: [] for e in ENGS}
        self.hw = {e: {} for e in ENGS}
        self.regions = {}
        self.rr = {e: 0 for e in ENGS}
        for e in ENGS:
            self.sems[e] = ctx.enter_context(nc.semaphore(f"sem_{e}"))
            self.cnt[e] = 0
            for i in range(NDMACH):
                k = f"{e}d{i}"
                self.sems[k] = ctx.enter_context(nc.semaphore(f"sem_{k}"))
                self.cnt[k] = 0

    def _reg(self, name):
        if name not in self.regions:
            self.regions[name] = {"w": [], "r": []}
        return self.regions[name]

    def op(self, eng, fn, reads=(), writes=(), dma=False):
        waits = {}

        def add(ev):
            if ev is None:
                return
            k, v = ev
            if waits.get(k, -1) < v:
                waits[k] = v

        for r in reads:
            for ev in self._reg(r)["w"]:
                add(ev)
        for w in writes:
            reg = self._reg(w)
            for ev in reg["w"]:
                add(ev)
            for ev in reg["r"]:
                add(ev)
        key = eng
        if dma:
            ch = self.rr[eng] % NDMACH
            self.rr[eng] += 1
            key = f"{eng}d{ch}"
            if self.cnt[key] > 0:
                add((key, self.cnt[key]))
        fw = []
        for k, v in waits.items():
            if k == eng == "t":
                continue
            if self.hw[eng].get(k, 0) >= v:
                continue
            self.hw[eng][k] = v
            fw.append((k, v))
        self.cnt[key] += 16 if dma else 1
        ev = (key, self.cnt[key])
        self.ops[eng].append((fw, fn, key))
        for w in writes:
            reg = self._reg(w)
            reg["w"] = [ev]
            reg["r"] = []
        for r in reads:
            self._reg(r)["r"].append(ev)
        return ev

    def barrier(self):
        snap = dict(self.cnt)
        for e in ENGS:
            fw = []
            for k, v in snap.items():
                if v > 0 and k != e:
                    self.hw[e][k] = max(self.hw[e].get(k, 0), v)
                    fw.append((k, v))
            self.cnt[e] += 1
            self.ops[e].append((fw, lambda eng: eng.nop(nofuse=True), e))
        bevs = [(e, self.cnt[e]) for e in ENGS]
        for reg in self.regions.values():
            if reg["w"] or reg["r"]:
                reg["w"] = list(bevs)
                reg["r"] = []

    def emit(self, block):
        def mk(name):
            def body(eng):
                for fw, fn, key in self.ops[name]:
                    for k, v in fw:
                        eng.wait_ge(self.sems[k], v)
                    inst = fn(eng)
                    inst.then_inc(self.sems[key], 16 if "d" in key else 1)
            return body
        block.tensor(mk("t"))
        block.scalar(mk("s"))
        block.vector(mk("v"))
        block.gpsimd(mk("g"))
        block.sync(mk("y"))


def cdiv(a, b):
    return (a + b - 1) // b


def window_of(h):
    return max(4, 1024 >> h)


def nkt_of(h):
    return cdiv(window_of(h) - 1 + 128, 128)



def emit_l0_attn(nc, tk, io, n_own=TOK, pfx="a_"):
    n_all = HALO + n_own
    n_q = n_own + 128
    nt_all = n_all // 128
    nqt = n_q // 128
    nqw = n_q + 384

    xo, xh = io["xo"], io["xh"]
    Wqk, bqk = io["Wqk"], io["bqk"]
    Wgate, bgate = io["Wgate"], io["bgate"]
    Wproj, bproj = io["Wproj"], io["bproj"]
    x1o, x1p = io["x1o"], io["x1p"]
    ldram = io["ldram"]

    ctx = ExitStack()
    with ctx:
        def sb(name, shape, dtype):
            h = ctx.enter_context(nc.sbuf_tensor(pfx + name, shape, dtype))
            return h

        identbf = sb("identbf", [128, 128], BF16)
        identf = sb("identf", [128, 128], F32)
        ones1 = sb("ones1", [1, 128], F32)
        sel = sb("sel", [16, 8 * 128], F32)
        stripE = sb("stripE", [128, 9 * 128], BF16)
        stripO = sb("stripO", [128, 9 * 128], BF16)
        xtile = sb("xtile", [128, 2 * C], F32)
        xlnt = sb("xlnt", [128, 2 * C], BF16)
        xlnT = sb("xlnT", [128, 8 * n_all], BF16)
        wq_s = sb("wq_s", [128, 2 * 1024], BF16)
        wk_s = sb("wk_s", [128, 2 * 1024], BF16)
        wproj_sb = sb("wproj_sb", [128, 8 * 1024], BF16)
        wgate_sb = sb("wgate_sb", [128, 8 * 16], BF16)
        bq8 = sb("bq8", [128, 16], F32)
        bgate_sb = sb("bgate_sb", [16, 1], F32)
        brow = sb("brow", [1, 1024], F32)
        bprojb = sb("bprojb", [128, C], F32)
        hvb = sb("hvb", [128, 1], F32)
        epsc = sb("epsc", [128, 1], F32)
        hvm = sb("hvm", [128, 1], F32)
        qpk = sb("qpk", [128, 2 * nqw], BF16)
        kpk = sb("kpk", [128, 2 * n_all], BF16)
        vb0 = sb("vb0", [128, nt_all * 65], BF16)
        vb1 = sb("vb1", [128, nt_all * 65], BF16)
        pT = sb("pT", [128, 2 * 4 * 128], BF16)
        gout = sb("gout", [128, 8 * n_q], BF16)
        gexp = sb("gexp", [16, n_q], F32)
        gsum = sb("gsum", [1, n_q], F32)
        lscr = sb("lscr", [16, 2 * n_q], F32)
        lfm = lscr[0:16, 0:n_q]
        st_scr = sb("st_scr", [128, 16 * nt_all + 4], F32)
        x0t = xtile[:, 0:C]
        x1t = xtile[:, C:2 * C]

        with nc.psum_tensor(pfx + "psT", [128, 1024 + 128], BF16) as psT, \
             nc.psum_tensor(pfx + "ps_b", [128, 512], F32) as ps_b, \
             nc.psum_tensor(pfx + "ps_g", [128, 512], F32) as ps_g:

            tk.op("g", lambda g: g.memset(identf[:], 1.0), writes=("identf",))
            tk.op("g", lambda g: g.affine_select(identf[:], identf[:], pattern=[[-1, 128]],
                                                 compare_op=ALU.is_equal, fill=0.0, base=0,
                                                 channel_multiplier=1),
                  reads=("identf",), writes=("identf",))
            tk.op("v", lambda v: v.tensor_copy(identbf[:], identf[:]),
                  reads=("identf",), writes=("identbf",))
            tk.op("g", lambda g: g.memset(ones1[:], 1.0), writes=("ones1",))
            tk.op("g", lambda g: g.memset(sel[:], 1.0), writes=("sel",))
            s3 = sel[:].rearrange("h (p c) -> h p c", p=8)
            tk.op("g", lambda g: g.affine_select(s3, s3, pattern=[[128, 8], [1, 128]],
                                                 compare_op=ALU.is_ge, fill=0.0, base=0,
                                                 channel_multiplier=-64),
                  reads=("sel",), writes=("sel",))
            tk.op("g", lambda g: g.affine_select(s3, s3, pattern=[[-128, 8], [-1, 128]],
                                                 compare_op=ALU.is_ge, fill=0.0, base=63,
                                                 channel_multiplier=64),
                  reads=("sel",), writes=("sel",))
            tk.op("g", lambda g: g.memset(epsc[:], 1e-5), writes=("epsc",))
            tk.op("g", lambda g: g.memset(vb0[:], 1.0), writes=("vb0",))
            tk.op("g", lambda g: g.memset(vb1[:], 1.0), writes=("vb1",))

            for kc in range(8):
                tk.op("g", lambda g, kc=kc: g.dma_start(
                    out=wproj_sb[:, kc * 1024:(kc + 1) * 1024], in_=Wproj[kc * 128:(kc + 1) * 128, :]),
                    writes=("wproj_sb",), dma=True)
                tk.op("g", lambda g, kc=kc: g.dma_start(
                    out=wgate_sb[:, kc * 16:(kc + 1) * 16], in_=Wgate[kc * 128:(kc + 1) * 128, :]),
                    writes=("wgate_sb",), dma=True)
            for cc in range(8):
                tk.op("y", lambda y, cc=cc: y.dma_start(
                    out=bq8[:, cc:cc + 1],
                    in_=bqk[cc * 128:(cc + 1) * 128].rearrange("(p o) -> p o", o=1)),
                    writes=("bq8",), dma=True)
                tk.op("y", lambda y, cc=cc: y.dma_start(
                    out=bq8[:, 8 + cc:8 + cc + 1],
                    in_=bqk[1024 + cc * 128:1024 + (cc + 1) * 128].rearrange("(p o) -> p o", o=1)),
                    writes=("bq8",), dma=True)
            tk.op("v", lambda v: v.tensor_scalar_mul(bq8[:, 0:8], bq8[:, 0:8], 0.125),
                  reads=("bq8",), writes=("bq8",))
            tk.op("y", lambda y: y.dma_start(out=bgate_sb[:], in_=bgate[:].rearrange("(n o) -> n o", o=1)),
                  writes=("bgate_sb",), dma=True)
            tk.op("y", lambda y: y.dma_start(out=brow[0:1, 0:1024],
                                             in_=bproj[:].rearrange("(o n) -> o n", o=1)),
                  writes=("brow",), dma=True)
            for i in range(2):
                tk.op("t", lambda t, i=i: t.matmul(ps_b[:, 0:512], ones1[0:1, :],
                                                   brow[0:1, i * 512:(i + 1) * 512],
                                                   start=True, stop=True),
                      reads=("ones1", "brow"), writes=("ps_b",))
                tk.op("s", lambda s, i=i: s.copy(bprojb[:, i * 512:(i + 1) * 512], ps_b[:, 0:512]),
                      reads=("ps_b",), writes=("bprojb",))

            for i in range(nt_all):
                slot = i % 2
                xsl = xtile[:, slot * C:(slot + 1) * C]
                xr = f"xt{slot}"
                src = xh[i * 128:(i + 1) * 128, :] if i < 8 else xo[(i - 8) * 128:(i - 7) * 128, :]
                tk.op("y", lambda y, src=src, xsl=xsl: y.dma_start(out=xsl, in_=src),
                      writes=(xr,), dma=True)
                if i == 0:
                    col = 16 * nt_all
                    tk.op("v", lambda v, xsl=xsl, col=col: v.reduce_max(
                        st_scr[:, col:col + 1], xsl, axis=AX, apply_absolute_value=True),
                        reads=(xr,), writes=("hvv",))
                    tk.op("s", lambda s, col=col: s.activation(
                        st_scr[0:1, col + 1:col + 2], st_scr[0:1, col:col + 1], AF.Sign),
                        reads=("hvv",), writes=("hvv",))
                    tk.op("t", lambda t, col=col: t.matmul(
                        ps_b[:, 0:1], ones1[0:1, :], st_scr[0:1, col + 1:col + 2],
                        start=True, stop=True),
                        reads=("ones1", "hvv"), writes=("ps_b",))
                    tk.op("s", lambda s: s.copy(hvb[:], ps_b[:, 0:1]),
                          reads=("ps_b",), writes=("hvb",))
                    tk.op("v", lambda v: v.tensor_scalar(hvm[:], hvb[:], -1.0, -NEG,
                                                         op0=ALU.add, op1=ALU.mult),
                          reads=("hvb",), writes=("hvm",))
                xl = xlnt[:, slot * C:(slot + 1) * C]
                xlr = f"xl{slot}"
                emit_ln_tile(tk, xsl, xr, xl, xlr, st_scr[:, 16 * i:16 * (i + 1)], epsc,
                             sfx=str(slot))
                for c in range(8):
                    pp = (i * 8 + c) % 2
                    psl = psT[:, pp * 1024: pp * 1024 + 128]
                    s_ap = xl[:, c * 128:(c + 1) * 128]
                    d_ap = xlnT[:, c * n_all + 128 * i: c * n_all + 128 * (i + 1)]
                    tk.op("t", lambda t, s_ap=s_ap, psl=psl: t.transpose(psl, s_ap, identbf[:]),
                          reads=(xlr, "identbf"), writes=(f"psT{pp}",))
                    tk.op("v", lambda v, d_ap=d_ap, psl=psl: v.tensor_copy(d_ap, psl),
                          reads=(f"psT{pp}",), writes=("xlnT",))

            toff = HALO - 128
            tk.op("g", lambda g: g.memset(st_scr[0:16, 16 * nt_all + 2:16 * nt_all + 3], 1.0),
                  writes=("ones16",))
            for i in range(cdiv(n_q, 512)):
                wd = min(512, n_q - i * 512)
                for kc in range(8):
                    tk.op("t", lambda t, i=i, wd=wd, kc=kc: t.matmul(
                        ps_g[0:16, 0:wd], wgate_sb[:, kc * 16:(kc + 1) * 16],
                        xlnT[:, kc * n_all + toff + i * 512: kc * n_all + toff + i * 512 + wd],
                        start=(kc == 0), stop=(kc == 7)),
                        reads=("wgate_sb", "xlnT"), writes=("ps_g",))
                tk.op("s", lambda s, i=i, wd=wd: s.activation(
                    gexp[:, i * 512:i * 512 + wd], ps_g[0:16, 0:wd], AF.Exp,
                    bias=bgate_sb[:], scale=1.0),
                    reads=("ps_g", "bgate_sb"), writes=("gexpr",))
                tk.op("t", lambda t, i=i, wd=wd: t.matmul(
                    ps_g[0:1, 0:wd], st_scr[0:16, 16 * nt_all + 2:16 * nt_all + 3],
                    gexp[:, i * 512:i * 512 + wd], start=True, stop=True),
                    reads=("ones16", "gexpr"), writes=("ps_g",))
                tk.op("s", lambda s, i=i, wd=wd: s.copy(gsum[:, i * 512:i * 512 + wd],
                                                        ps_g[0:1, 0:wd]),
                      reads=("ps_g",), writes=("gsumr",))

        tk.barrier()

        with nc.psum_tensor(pfx + "ps_qk", [128, 512], F32) as ps_qk, \
             nc.psum_tensor(pfx + "ps_qk2", [128, 512], F32) as ps_qk2, \
             nc.psum_tensor(pfx + "psTv", [128, 128], BF16) as psTv, \
             nc.psum_tensor(pfx + "ps_s0", [128, 4 * 128], F32) as ps_s0, \
             nc.psum_tensor(pfx + "ps_s1", [128, 4 * 128], F32) as ps_s1, \
             nc.psum_tensor(pfx + "ps_o0", [65, 128], F32) as ps_o0, \
             nc.psum_tensor(pfx + "ps_o1", [65, 128], F32) as ps_o1:
            ps_s = [ps_s0, ps_s1]
            ps_o = [ps_o0, ps_o1]

            def gen_prep(p):
                slot = p % 2
                wqr, wkr = f"wq{slot}", f"wk{slot}"
                qsrc = Wqk[:, p * 128:(p + 1) * 128].rearrange("(kc p2) c -> p2 kc c", p2=128)
                ksrc = Wqk[:, 1024 + p * 128:1024 + (p + 1) * 128].rearrange(
                    "(kc p2) c -> p2 kc c", p2=128)
                qdst = wq_s[:, slot * 1024:(slot + 1) * 1024].rearrange("q (kc c) -> q kc c", kc=8)
                kdst = wk_s[:, slot * 1024:(slot + 1) * 1024].rearrange("q (kc c) -> q kc c", kc=8)
                yield ("g", (lambda g, qsrc=qsrc, qdst=qdst: g.dma_start(out=qdst, in_=qsrc)),
                       (), (wqr,), True)
                yield ("g", (lambda g, ksrc=ksrc, kdst=kdst: g.dma_start(out=kdst, in_=ksrc)),
                       (), (wkr,), True)
                for i in range(cdiv(nqw, 512)):
                    wd = min(512, nqw - i * 512)
                    for kc in range(8):
                        yield ("t", (lambda t, i=i, wd=wd, kc=kc, slot=slot: t.matmul(
                            ps_qk[:, 0:wd],
                            wq_s[:, slot * 1024 + kc * 128: slot * 1024 + (kc + 1) * 128],
                            xlnT[:, kc * n_all + HALO - 512 + i * 512:
                                 kc * n_all + HALO - 512 + i * 512 + wd],
                            start=(kc == 0), stop=(kc == 7))),
                            (wqr, "xlnT"), ("ps_qk",), False)
                    yield ("s", (lambda s, i=i, wd=wd, p=p, slot=slot: s.activation(
                        qpk[:, slot * nqw + i * 512: slot * nqw + i * 512 + wd],
                        ps_qk[:, 0:wd], AF.Identity,
                        bias=bq8[:, p:p + 1], scale=0.125)),
                        ("ps_qk", "bq8"), (f"qpk{slot}",), False)
                for i in range(cdiv(n_all, 512)):
                    wd = min(512, n_all - i * 512)
                    for kc in range(8):
                        yield ("t", (lambda t, i=i, wd=wd, kc=kc, slot=slot: t.matmul(
                            ps_qk2[:, 0:wd],
                            wk_s[:, slot * 1024 + kc * 128: slot * 1024 + (kc + 1) * 128],
                            xlnT[:, kc * n_all + i * 512: kc * n_all + i * 512 + wd],
                            start=(kc == 0), stop=(kc == 7))),
                            (wkr, "xlnT"), ("ps_qk2",), False)
                    yield ("s", (lambda s, i=i, wd=wd, p=p, slot=slot: s.activation(
                        kpk[:, slot * n_all + i * 512: slot * n_all + i * 512 + wd],
                        ps_qk2[:, 0:wd], AF.Identity,
                        bias=bq8[:, 8 + p:8 + p + 1], scale=1.0)),
                        ("ps_qk2", "bq8"), (f"kpk{slot}",), False)

            def gen_tail(p):
                slot = p % 2
                for strip, stripn, h in ((stripE, "stripE", 2 * p), (stripO, "stripO", 2 * p + 1)):
                    nk = nkt_of(h)
                    w = window_of(h)
                    stripr = stripn
                    sl = strip[:, 0:nk * 128]
                    sl3 = sl.rearrange("q (r i) -> q r i", r=nk)
                    yield ("g", (lambda g, sl=sl: g.memset(sl, 0.0)), (), (stripr,), False)
                    yield ("g", (lambda g, sl3=sl3, nk=nk: g.affine_select(
                        sl3, sl3, pattern=[[128, nk], [1, 128]], compare_op=ALU.is_ge,
                        fill=NEG, base=0, channel_multiplier=-1)),
                        (stripr,), (stripr,), False)
                    yield ("g", (lambda g, sl3=sl3, nk=nk, w=w: g.affine_select(
                        sl3, sl3, pattern=[[-128, nk], [-1, 128]], compare_op=ALU.is_ge,
                        fill=NEG, base=w - 1, channel_multiplier=1)),
                        (stripr,), (stripr,), False)
                for i in range(nt_all):
                    yield ("t", (lambda t, i=i, slot=slot: t.transpose(
                        psTv[:], kpk[:, slot * n_all + i * 128: slot * n_all + (i + 1) * 128],
                        identbf[:])),
                        (f"kpk{slot}", "identbf"), ("psTv",), False)
                    yield ("v", (lambda v, i=i: v.tensor_copy(
                        vb0[:, i * 65: i * 65 + 64], psTv[:, 0:64])),
                        ("psTv",), ("vb0",), False)
                    yield ("v", (lambda v, i=i: v.tensor_copy(
                        vb1[:, i * 65: i * 65 + 64], psTv[:, 64:128])),
                        ("psTv",), ("vb1",), False)

            def gen_units(p, g0):
                gidx = g0
                for j in range(nqt):
                    qs = -128 + 128 * j
                    for sub in range(2):
                        h = 2 * p + sub
                        nk = nkt_of(h)
                        nke = min(nk, (qs + HALO) // 128 + 1)
                        wsh = window_of(h)
                        rl = list(range(nke))
                        parts = [rl[z:z + 4] for z in range(0, nke, 4)]
                        for pi, rpart in enumerate(parts):
                            yield dict(p=p, slot=p % 2, j=j, sub=sub, h=h, wsh=wsh,
                                       qs=qs, rpart=rpart, oslot=gidx % 2,
                                       av_first=(pi == 0),
                                       av_last=(pi == len(parts) - 1))
                        gidx += 1

            def emit_u(u, uslot):
                p, slot, j, sub = u["p"], u["slot"], u["j"], u["sub"]
                qs, wsh, rpart = u["qs"], u["wsh"], u["rpart"]
                qr, kr = f"qpk{slot}", f"kpk{slot}"
                strip = stripE if sub == 0 else stripO
                stripr = "stripE" if sub == 0 else "stripO"
                for idx, r in enumerate(rpart):
                    ks = qs - 128 * r
                    kcol = slot * n_all + ks + HALO
                    qcol = slot * nqw + qs + 512
                    tk.op("t", lambda t, idx=idx, kcol=kcol, qcol=qcol, sub=sub: t.matmul(
                        ps_s[uslot][:, idx * 128:(idx + 1) * 128],
                        kpk[sub * 64:(sub + 1) * 64, kcol:kcol + 128],
                        qpk[sub * 64:(sub + 1) * 64, qcol:qcol + 128],
                        start=True, stop=True),
                        reads=(kr, qr), writes=(f"pss{uslot}",))
                for idx, r in enumerate(rpart):
                    ks = qs - 128 * r
                    lo, hi = 128 * r - 127, 128 * r + 127
                    nw = not (lo >= 0 and hi <= wsh - 1)
                    nh = ks < 0
                    if not (nw or nh):
                        continue
                    ssl = strip[:, r * 128:(r + 1) * 128]
                    dst = ps_s[uslot][:, idx * 128:(idx + 1) * 128]
                    if nw and nh:
                        tk.op("v", lambda v, dst=dst, ssl=ssl: v.scalar_tensor_tensor(
                            dst, dst, hvm[:], ssl, op0=ALU.add, op1=ALU.add),
                            reads=(f"pss{uslot}", "hvm", stripr), writes=(f"pss{uslot}",))
                    elif nw:
                        tk.op("v", lambda v, dst=dst, ssl=ssl: v.tensor_add(dst, dst, ssl),
                              reads=(f"pss{uslot}", stripr), writes=(f"pss{uslot}",))
                    else:
                        tk.op("v", lambda v, dst=dst: v.tensor_scalar_add(dst, dst, hvm[:]),
                              reads=(f"pss{uslot}", "hvm"), writes=(f"pss{uslot}",))
                nr = len(rpart)
                tk.op("s", lambda s, nr=nr: s.activation(
                    pT[:, uslot * 512: uslot * 512 + nr * 128],
                    ps_s[uslot][:, 0:nr * 128], AF.Exp),
                    reads=(f"pss{uslot}",), writes=(f"pTu{uslot}",))

            def emit_av(u, uslot):
                p, slot, j, sub, h = u["p"], u["slot"], u["j"], u["sub"], u["h"]
                qs, rpart, oslot = u["qs"], u["rpart"], u["oslot"]
                vbuf = vb0 if sub == 0 else vb1
                vr = f"vb{sub}"
                nr = len(rpart)
                for idx, r in enumerate(rpart):
                    kidx = (qs - 128 * r + HALO) // 128
                    tk.op("t", lambda t, idx=idx, kidx=kidx, vbuf=vbuf, slot=slot,
                          oslot=oslot,
                          fst=(u["av_first"] and idx == 0),
                          lst=(u["av_last"] and idx == nr - 1): t.matmul(
                              ps_o[oslot][0:65, :],
                              vbuf[:, kidx * 65: kidx * 65 + 65],
                              pT[:, uslot * 512 + idx * 128: uslot * 512 + (idx + 1) * 128],
                              start=fst, stop=lst),
                          reads=(vr, f"pTu{uslot}"), writes=(f"pso{oslot}",))
                if u["av_last"]:
                    go = gout[sub * 64:(sub + 1) * 64,
                              p * n_q + j * 128: p * n_q + (j + 1) * 128]
                    tk.op("s", lambda s, go=go, oslot=oslot: s.copy(
                        go, ps_o[oslot][0:64, :]),
                        reads=(f"pso{oslot}",), writes=(f"gout{p}",))
                    lsl = lscr[0:1, sub * n_q + j * 128: sub * n_q + (j + 1) * 128]
                    tk.op("v", lambda v, lsl=lsl, oslot=oslot: v.tensor_copy(
                        lsl, ps_o[oslot][64:65, :]),
                        reads=(f"pso{oslot}",), writes=("lstr",))
                    if j == nqt - 1 and sub == 1:
                        tk.op("y", lambda y, p=p: y.dma_start(
                            out=ldram[0:1, 2 * p * n_q:(2 * p + 2) * n_q],
                            in_=lscr[0:1, 0:2 * n_q]),
                            reads=("lstr",), writes=("ldr",), dma=True)

            def emit_item(item):
                tk.op(item[0], item[1], reads=item[2], writes=item[3], dma=item[4])

            for item in gen_prep(0):
                emit_item(item)
            for item in gen_tail(0):
                emit_item(item)
            pending = None
            uslot = 0
            gcount = 0
            for p in range(8):
                nxt = gen_prep(p + 1) if p + 1 < 8 else iter(())
                units = list(gen_units(p, gcount))
                gcount += sum(1 for u in units if u["av_first"])
                nper = cdiv(160, len(units))
                for u in units:
                    emit_u(u, uslot)
                    if pending is not None:
                        emit_av(*pending)
                    pending = (u, uslot)
                    uslot ^= 1
                    for _ in range(nper):
                        item = next(nxt, None)
                        if item is None:
                            break
                        emit_item(item)
                for item in nxt:
                    emit_item(item)
                emit_av(*pending)
                pending = None
                if p + 1 < 8:
                    for item in gen_tail(p + 1):
                        emit_item(item)

        tk.barrier()

        with nc.psum_tensor(pfx + "ps_f", [128, 512], F32) as ps_f, \
             nc.psum_tensor(pfx + "ps_p", [128, 2 * 1024], F32) as ps_p:

            tk.op("y", lambda y: y.dma_start(
                out=lfm[:, :],
                in_=ldram[0:1, :].rearrange("o (h q) -> (o h) q", h=16)),
                reads=("ldr",), writes=("lfmr",), dma=True)
            for i in range(cdiv(n_q, 512)):
                wd = min(512, n_q - i * 512)
                tk.op("t", lambda t, i=i, wd=wd: t.matmul(
                    ps_f[0:16, 0:wd], ones1[0:1, 0:16], gsum[:, i * 512:i * 512 + wd],
                    start=True, stop=True),
                    reads=("ones1", "gsumr"), writes=("ps_f",))
                sl = lfm[:, i * 512:i * 512 + wd]
                tk.op("v", lambda v, sl=sl, i=i, wd=wd: v.scalar_tensor_tensor(
                    sl, sl, 1e-20, ps_f[0:16, 0:wd], op0=ALU.add, op1=ALU.mult),
                    reads=("lfmr", "ps_f"), writes=("lfmr",))
            tk.op("s", lambda s: s.activation(lfm[:, :], lfm[:, :], AF.Ln),
                  reads=("lfmr",), writes=("lfmr",))
            tk.op("s", lambda s: s.activation(lfm[:, :], lfm[:, :], AF.Exp, scale=-1.0),
                  reads=("lfmr",), writes=("lfmr",))
            tk.op("v", lambda v: v.tensor_mul(gexp[:, :], gexp[:, :], lfm[:, :]),
                  reads=("gexpr", "lfmr"), writes=("gexpr",))
            for p in range(8):
                for i in range(cdiv(n_q, 512)):
                    wd = min(512, n_q - i * 512)
                    tk.op("t", lambda t, p=p, i=i, wd=wd: t.matmul(
                        ps_f[:, 0:wd], sel[:, p * 128:(p + 1) * 128],
                        gexp[:, i * 512:i * 512 + wd], start=True, stop=True),
                        reads=("sel", "gexpr"), writes=("ps_f",))
                    gsl = gout[:, p * n_q + i * 512: p * n_q + i * 512 + wd]
                    tk.op("v", lambda v, gsl=gsl, wd=wd: v.tensor_mul(gsl, gsl, ps_f[:, 0:wd]),
                          reads=(f"gout{p}", "ps_f"), writes=(f"gout{p}",))
            for j in range(nqt):
                jp = j % 2
                for p in range(8):
                    for nn in range(2):
                        tk.op("t", lambda t, j=j, p=p, nn=nn, jp=jp: t.matmul(
                            ps_p[:, jp * 1024 + nn * 512: jp * 1024 + (nn + 1) * 512],
                            gout[:, p * n_q + j * 128: p * n_q + (j + 1) * 128],
                            wproj_sb[:, p * 1024 + nn * 512: p * 1024 + (nn + 1) * 512],
                            start=(p == 0), stop=(p == 7)),
                            reads=(f"gout{p}", "wproj_sb"), writes=(f"ps_p{jp}",))
                if j == 0:
                    src = xh[896:1024, :]
                else:
                    src = xo[(j - 1) * 128:j * 128, :]
                tk.op("y", lambda y, src=src: y.dma_start(out=x0t, in_=src),
                      writes=("x0t",), dma=True)
                tk.op("v", lambda v, jp=jp: v.tensor_add(
                    x1t, ps_p[:, jp * 1024:(jp + 1) * 1024], bprojb[:, :]),
                      reads=(f"ps_p{jp}", "bprojb"), writes=("x1t",))
                tk.op("v", lambda v: v.tensor_add(x1t, x1t, x0t),
                      reads=("x1t", "x0t"), writes=("x1t",))
                if j == 0:
                    tk.op("v", lambda v: v.tensor_scalar_mul(x1t, x1t, hvb[:]),
                          reads=("x1t", "hvb"), writes=("x1t",))
                    tk.op("y", lambda y: y.dma_start(out=x1p[0:1, :], in_=x1t[127:128, :]),
                          reads=("x1t",), writes=("x1p_out",), dma=True)
                else:
                    tk.op("y", lambda y, j=j: y.dma_start(
                        out=x1o[(j - 1) * 128:j * 128, :], in_=x1t),
                        reads=("x1t",), writes=("x1o_out",), dma=True)
    tk.barrier()


def emit_l0_ffn(nc, tk, io, n_own=TOK, pfx="b_"):
    nt = n_own // 128
    ncol = n_own + 1

    x1o, x1p = io["x1o"], io["x1p"]
    W_r, W_k, W_v = io["W_r"], io["W_k"], io["W_v"]
    mu_r, mu_k = io["mu_r"], io["mu_k"]
    x2o = io["x2o"]
    ctb1 = io["ctb1"]

    ctx = ExitStack()
    with ctx:
        def sb(name, shape, dtype):
            return ctx.enter_context(nc.sbuf_tensor(pfx + name, shape, dtype))

        identbf = sb("identbf", [128, 128], BF16)
        identf = sb("identf", [128, 128], F32)
        epsc = sb("epsc", [128, 1], F32)
        xtile = sb("xtile", [128, 2 * C], F32)
        xlnt = sb("xlnt", [128, 2 * C], BF16)
        xlnT = sb("xlnT", [128, 8 * ncol], BF16)
        xrT = sb("xrT", [128, 2 * 8 * 512], BF16)
        xkT = sb("xkT", [128, 2 * 8 * 512], BF16)
        dT = sb("dT", [128, 2 * 512], BF16)
        wrT = sb("wrT", [128, 8 * C], BF16)
        wkT = sb("wkT", [128, 8 * C], BF16)
        wvT = sb("wvT", [128, 8 * C], BF16)
        wld = sb("wld", [128, C], BF16)
        mur = sb("mur", [128, 16], F32)
        kact = sb("kact", [128, 8 * 512], BF16)
        krelu = sb("krelu", [128, 2 * 512], F32)
        r_tm = sb("r_tm", [128, 8 * C], F32)
        st_scr = sb("st_scr", [128, 16 * (nt + 1)], F32)
        xlt3 = xlnt
        st43 = sb("st43", [128, 2 * 16], F32)
        xT3 = sb("xT3", [128, 2 * 1024], BF16)

        with nc.psum_tensor(pfx + "psT", [128, 1024 + 128], BF16) as psT:
            tk.op("g", lambda g: g.memset(identf[:], 1.0), writes=("identf",))
            tk.op("g", lambda g: g.affine_select(identf[:], identf[:], pattern=[[-1, 128]],
                                                 compare_op=ALU.is_equal, fill=0.0, base=0,
                                                 channel_multiplier=1),
                  reads=("identf",), writes=("identf",))
            tk.op("v", lambda v: v.tensor_copy(identbf[:], identf[:]),
                  reads=("identf",), writes=("identbf",))
            tk.op("g", lambda g: g.memset(epsc[:], 1e-5), writes=("epsc",))
            for cc in range(8):
                tk.op("y", lambda y, cc=cc: y.dma_start(
                    out=mur[:, cc:cc + 1],
                    in_=mu_r[cc * 128:(cc + 1) * 128].rearrange("(p o) -> p o", o=1)),
                    writes=("mur",), dma=True)
                tk.op("y", lambda y, cc=cc: y.dma_start(
                    out=mur[:, 8 + cc:8 + cc + 1],
                    in_=mu_k[cc * 128:(cc + 1) * 128].rearrange("(p o) -> p o", o=1)),
                    writes=("mur",), dma=True)
            for W, wT, wtr in ((W_r, wrT, "wrT"), (W_k, wkT, "wkT"), (W_v, wvT, "wvT")):
                for dc in range(8):
                    tk.op("g", lambda g, W=W, dc=dc: g.dma_start(
                        out=wld[:, :], in_=W[dc * 128:(dc + 1) * 128, :]),
                        writes=("wld",), dma=True)
                    for cc in range(8):
                        pp = (dc * 8 + cc) % 2
                        psl = psT[:, pp * 1024: pp * 1024 + 128]
                        s_ap = wld[:, cc * 128:(cc + 1) * 128]
                        d_ap = wT[:, cc * 1024 + dc * 128: cc * 1024 + (dc + 1) * 128]
                        tk.op("t", lambda t, s_ap=s_ap, psl=psl: t.transpose(psl, s_ap, identbf[:]),
                              reads=("wld", "identbf"), writes=(f"psT{pp}",))
                        tk.op("v", lambda v, d_ap=d_ap, psl=psl: v.tensor_copy(d_ap, psl),
                              reads=(f"psT{pp}",), writes=(wtr,))
            for i in range(nt + 1):
                slot = i % 2
                xsl = xtile[:, slot * C:(slot + 1) * C]
                xr_ = f"xt{slot}"
                if i < nt:
                    rows = 128
                    src = x1o[i * 128:(i + 1) * 128, :]
                else:
                    rows = 1
                    src = x1p[0:1, :]
                xsl_r = xsl[0:rows, :] if rows == 1 else xsl
                tk.op("y", lambda y, src=src, xsl_r=xsl_r: y.dma_start(out=xsl_r, in_=src),
                      writes=(xr_,), dma=True)
                xl = xlnt[:, slot * C:(slot + 1) * C]
                xlr = f"xl{slot}"
                xs = xsl[0:rows, :]
                xlo = xl[0:rows, :]
                emit_ln_tile(tk, xs, xr_, xlo, xlr, st_scr[:, 16 * i:16 * (i + 1)], epsc,
                             rows=rows, sfx=str(slot))
                for c in range(8):
                    pp = (i * 8 + c) % 2
                    s_ap = xl[0:rows, c * 128:(c + 1) * 128]
                    col0 = c * ncol + (1 + 128 * i if i < nt else 0)
                    d_ap = xlnT[:, col0:col0 + rows]
                    psl = psT[:, pp * 1024: pp * 1024 + rows]
                    tk.op("t", lambda t, s_ap=s_ap, rows=rows, psl=psl: t.transpose(
                        psl, s_ap, identbf[0:rows, 0:rows]),
                          reads=(xlr, "identbf"), writes=(f"psT{pp}",))
                    tk.op("v", lambda v, d_ap=d_ap, psl=psl: v.tensor_copy(d_ap, psl),
                          reads=(f"psT{pp}",), writes=("xlnT",))

        tk.barrier()

        with nc.psum_tensor(pfx + "ps_r", [128, 1024], F32) as ps_r, \
             nc.psum_tensor(pfx + "ps_k", [128, 2 * 512], F32) as ps_k, \
             nc.psum_tensor(pfx + "ps_kv", [128, 1024], F32) as ps_kv, \
             nc.psum_tensor(pfx + "psT3", [128, 1024 + 128], BF16) as psT3:
            for ch in range(cdiv(n_own, 512)):
                t0 = ch * 512
                cw = min(512, n_own - t0)
                cp = ch % 2
                for cc in range(8):
                    cur = xlnT[:, cc * ncol + 1 + t0: cc * ncol + 1 + t0 + cw]
                    prev = xlnT[:, cc * ncol + t0: cc * ncol + t0 + cw]
                    dsl = dT[:, cp * 512: cp * 512 + cw]
                    tk.op("v", lambda v, cur=cur, prev=prev, dsl=dsl: v.tensor_sub(
                        dsl, cur, prev),
                        reads=("xlnT",), writes=(f"dT{cp}",))
                    tk.op("v", lambda v, cc=cc, prev=prev, dsl=dsl, cp=cp, cw=cw:
                          v.scalar_tensor_tensor(
                              xrT[:, (cp * 8 + cc) * 512: (cp * 8 + cc) * 512 + cw], dsl,
                              mur[:, cc:cc + 1], prev, op0=ALU.mult, op1=ALU.add),
                          reads=(f"dT{cp}", "mur", "xlnT"), writes=(f"xrT{cp}",))
                    tk.op("v", lambda v, cc=cc, prev=prev, dsl=dsl, cp=cp, cw=cw:
                          v.scalar_tensor_tensor(
                              xkT[:, (cp * 8 + cc) * 512: (cp * 8 + cc) * 512 + cw], dsl,
                              mur[:, 8 + cc:8 + cc + 1], prev, op0=ALU.mult, op1=ALU.add),
                          reads=(f"dT{cp}", "mur", "xlnT"), writes=(f"xkT{cp}",))
                for dc in range(8):
                    kp = dc % 2
                    for cc in range(8):
                        tk.op("t", lambda t, dc=dc, cc=cc, cp=cp, cw=cw, kp=kp: t.matmul(
                            ps_k[:, kp * 512: kp * 512 + cw],
                            wkT[:, cc * 1024 + dc * 128: cc * 1024 + (dc + 1) * 128],
                            xkT[:, (cp * 8 + cc) * 512: (cp * 8 + cc) * 512 + cw],
                            start=(cc == 0), stop=(cc == 7)),
                            reads=("wkT", f"xkT{cp}"), writes=(f"ps_k{kp}",))
                    tk.op("s", lambda s, cw=cw, kp=kp: s.activation(
                        krelu[:, kp * 512: kp * 512 + cw], ps_k[:, kp * 512: kp * 512 + cw],
                        AF.Relu),
                        reads=(f"ps_k{kp}",), writes=(f"krelu{kp}",))
                    tk.op("s", lambda s, dc=dc, cw=cw, kp=kp: s.activation(
                        kact[:, dc * 512:dc * 512 + cw], krelu[:, kp * 512: kp * 512 + cw],
                        AF.Square),
                        reads=(f"krelu{kp}",), writes=("kact",))
                for jt in range(cw // 128):
                    tcol = t0 + jt * 128
                    for cc in range(8):
                        for nn in range(2):
                            tk.op("t", lambda t, cc=cc, nn=nn, tcol=tcol, cp=cp, t0=t0: t.matmul(
                                ps_r[:, nn * 512:(nn + 1) * 512],
                                xrT[:, (cp * 8 + cc) * 512 + tcol - t0:
                                    (cp * 8 + cc) * 512 + tcol - t0 + 128],
                                wrT[:, cc * 1024 + nn * 512: cc * 1024 + (nn + 1) * 512],
                                start=(cc == 0), stop=(cc == 7)),
                                reads=(f"xrT{cp}", "wrT"), writes=("ps_r",))
                    tk.op("s", lambda s, jt=jt, cp=cp: s.activation(
                        r_tm[:, (cp * 4 + jt) * C:(cp * 4 + jt + 1) * C], ps_r[:, :],
                        AF.Sigmoid),
                        reads=("ps_r",), writes=(f"rtm{cp}{jt}",))
                for jt in range(cw // 128):
                    tcol = t0 + jt * 128
                    for dc in range(8):
                        for nn in range(2):
                            tk.op("t", lambda t, dc=dc, nn=nn, tcol=tcol, t0=t0: t.matmul(
                                ps_kv[:, nn * 512:(nn + 1) * 512],
                                kact[:, dc * 512 + tcol - t0: dc * 512 + tcol - t0 + 128],
                                wvT[:, dc * 1024 + nn * 512: dc * 1024 + (nn + 1) * 512],
                                start=(dc == 0), stop=(dc == 7)),
                                reads=("kact", "wvT"), writes=("ps_kv",))
                    xsl = xtile[:, (jt % 2) * C:(jt % 2 + 1) * C]
                    xr_ = f"xt{jt % 2}"
                    rr = f"rtm{cp}{jt}"
                    rsl = r_tm[:, (cp * 4 + jt) * C:(cp * 4 + jt + 1) * C]
                    tk.op("y", lambda y, tcol=tcol, xsl=xsl: y.dma_start(
                        out=xsl, in_=x1o[tcol:tcol + 128, :]),
                        writes=(xr_,), dma=True)
                    tk.op("v", lambda v, rsl=rsl: v.tensor_mul(rsl, rsl, ps_kv[:, :]),
                        reads=(rr, "ps_kv"), writes=(rr,))
                    tk.op("v", lambda v, rsl=rsl, xsl=xsl: v.tensor_add(rsl, rsl, xsl),
                        reads=(rr, xr_), writes=(rr,))
                    tk.op("y", lambda y, rsl=rsl, tcol=tcol: y.dma_start(
                        out=x2o[tcol:tcol + 128, :], in_=rsl),
                        reads=(rr,), writes=("x2_out",), dma=True)
                    ti = ch * 4 + jt
                    ts = ti % 2
                    x2sl = rsl
                    xlo3 = xlt3[:, ts * C:(ts + 1) * C]
                    emit_ln_tile(tk, x2sl, rr, xlo3, f"xl3{ts}",
                                 st43[:, ts * 16:(ts + 1) * 16], epsc, sfx=f"3x{ts}")
                    for c in range(8):
                        pp = c % 2
                        psl = psT3[:, pp * 1024: pp * 1024 + 128]
                        s_ap = xlo3[:, c * 128:(c + 1) * 128]
                        d_ap = xT3[:, ts * 1024 + c * 128: ts * 1024 + (c + 1) * 128]
                        tk.op("t", lambda t, s_ap=s_ap, psl=psl: t.transpose(psl, s_ap, identbf[:]),
                              reads=(f"xl3{ts}", "identbf"), writes=(f"psT3{pp}",))
                        tk.op("v", lambda v, d_ap=d_ap, psl=psl: v.tensor_copy(d_ap, psl),
                              reads=(f"psT3{pp}",), writes=(f"xT3{ts}",))
                    hf, to = (0, ti) if ti < 8 else (1, ti - 8)
                    gdst = ctb1[hf].rearrange("(kc p) t -> p kc t", p=128)[
                        :, :, to * 128:(to + 1) * 128]
                    gsrc = xT3[:, ts * 1024:(ts + 1) * 1024].rearrange("p (kc t) -> p kc t", kc=8)
                    tk.op("y", lambda y, gdst=gdst, gsrc=gsrc: y.dma_start(out=gdst, in_=gsrc),
                          reads=(f"xT3{ts}",), writes=(f"g1d{hf}",), dma=True)
    tk.barrier()


def emit_ln_tile(tk, xs, xr_, xlo, xlr, stats, epsc, rows=128, sfx=""):
    st = "st" + sfx
    b01 = stats[0:rows, 0:12]
    mv = stats[0:rows, 12:14]
    mn = stats[0:rows, 12:13]
    vr = stats[0:rows, 13:14]
    sC = stats[0:rows, 14:15]
    sD = stats[0:rows, 15:16]
    tk.op("v", lambda v: v.bn_stats(stats[0:rows, 0:6], xs[:, 0:C // 2]),
          reads=(xr_,), writes=(st,))
    tk.op("v", lambda v: v.bn_stats(stats[0:rows, 6:12], xs[:, C // 2:C]),
          reads=(xr_,), writes=(st,))
    tk.op("v", lambda v: v.bn_aggr(mv, b01), reads=(st,), writes=(st,))
    tk.op("s", lambda s: s.activation(sC, vr, AF.Sqrt, bias=epsc[0:rows, :]),
          reads=(st, "epsc"), writes=(st,))
    tk.op("v", lambda v: v.reciprocal(sC, sC), reads=(st,), writes=(st,))
    tk.op("v", lambda v: v.tensor_mul(sD, mn, sC), reads=(st,), writes=(st,))
    tk.op("v", lambda v: v.tensor_scalar(xlo, xs, sC, sD, op0=ALU.mult, op1=ALU.subtract),
          reads=(xr_, st), writes=(xlr,))



def emit_l1_attn(nc, tk, io, seq=T, nb=B, pfx="c_"):
    ntt = seq // 128
    nqc = seq // 512
    GS = 3

    Wq2, bq2 = io["Wq2"], io["bq2"]
    Wk2, bk2 = io["Wk2"], io["bk2"]
    Wv2, bv2 = io["Wv2"], io["bv2"]
    x2src = io["x2src"]
    ctb1 = io["ctb1"]
    g1g = io["g1g"]
    ctb2 = io["ctb2"]
    rcv = io["rcv"]

    def g1read(kc, gt0, w):
        r = gt0 // TOK
        off = gt0 % TOK
        hf = 0 if off < TOK // 2 else 1
        off -= hf * (TOK // 2)
        return (g1g[hf][(r * 8 + kc) * 128:(r * 8 + kc + 1) * 128, off:off + w],
                f"g1g{hf}r")

    ctx = ExitStack()
    with ctx:
        def sb(name, shape, dtype):
            return ctx.enter_context(nc.sbuf_tensor(pfx + name, shape, dtype))

        identbf = sb("identbf", [128, 128], BF16)
        identf = sb("identf", [128, 128], F32)
        ones1 = sb("ones1", [1, 128], F32)
        onesc = sb("onesc", [128, 1], BF16)
        wload = sb("wload", [64, C], BF16)
        wqT = sb("wqT", [128, 8 * 128], BF16)
        wkT = sb("wkT", [128, 8 * 128], BF16)
        wvT = sb("wvT", [128, 8 * 128], BF16)
        bqc = sb("bqc", [128, 2], F32)
        bvc = sb("bvc", [128, 1], F32)
        g1sb = sb("g1sb", [128, 2 * 8 * 512], BF16)
        qh = [sb(f"qh{b}", [128, seq], BF16) for b in range(nb)]
        kh = [sb(f"kh{b}", [128, seq], BF16) for b in range(nb)]
        vt = [[sb(f"vt{b}{h}", [128, ntt * 65], BF16) for h in range(2)]
              for b in range(nb)]
        vfm = sb("vfm", [128, 2 * 512], BF16)
        pT = sb("pT", [128, 2 * GS * 512], BF16)
        cstr = sb("cstr", [128, 4 * 512], BF16)
        sqs = sb("sqs", [128, 512], BF16)
        nmx = sb("nmx", [1, nb * 2 * 2 * nqc], F32)
        negC = sb("negC", [128, nb * 2], F32)
        oout = sb("oout", [65, 2 * 512], F32)
        oot2 = sb("oot2", [128, 2 * 260], F32)
        st4 = sb("st4", [128, 4], F32)

        with nc.psum_tensor(pfx + "psT", [128, 1024 + 64], BF16) as psT:
            tk.op("g", lambda g: g.memset(identf[:], 1.0), writes=("identf",))
            tk.op("g", lambda g: g.affine_select(identf[:], identf[:], pattern=[[-1, 128]],
                                                 compare_op=ALU.is_equal, fill=0.0, base=0,
                                                 channel_multiplier=1),
                  reads=("identf",), writes=("identf",))
            tk.op("v", lambda v: v.tensor_copy(identbf[:], identf[:]),
                  reads=("identf",), writes=("identbf",))
            tk.op("g", lambda g: g.memset(ones1[:], 1.0), writes=("ones1",))
            tk.op("g", lambda g: g.memset(onesc[:], 1.0), writes=("onesc",))
            tk.op("g", lambda g: g.memset(cstr[:], 0.0), writes=("cstr",))
            c3 = cstr[:].rearrange("p (d i) -> p d i", d=4)
            tk.op("g", lambda g, c3=c3: g.affine_select(
                c3, c3, pattern=[[-128, 4], [1, 512]], compare_op=ALU.is_ge,
                fill=NEG, base=0, channel_multiplier=-1),
                reads=("cstr",), writes=("cstr",))
            for b in range(nb):
                for h in range(2):
                    tk.op("g", lambda g, b=b, h=h: g.memset(vt[b][h][:], 1.0),
                          writes=(f"vt{b}{h}",))

            for W, wT, wtr in ((Wq2, wqT, "wqT"), (Wk2, wkT, "wkT"), (Wv2, wvT, "wvT")):
                for hh in range(2):
                    tk.op("g", lambda g, W=W, hh=hh: g.dma_start(
                        out=wload[:, :], in_=W[hh, :, :]), writes=("wload",), dma=True)
                    for kc in range(8):
                        pp = kc % 2
                        psl = psT[:, pp * 1024: pp * 1024 + 64]
                        s_ap = wload[:, kc * 128:(kc + 1) * 128]
                        d_ap = wT[:, kc * 128 + hh * 64: kc * 128 + (hh + 1) * 64]
                        tk.op("t", lambda t, s_ap=s_ap, psl=psl: t.transpose(
                            psl, s_ap, identbf[0:64, 0:64]),
                            reads=("wload", "identbf"), writes=(f"psT{pp}",))
                        tk.op("v", lambda v, d_ap=d_ap, psl=psl: v.tensor_copy(d_ap, psl),
                              reads=(f"psT{pp}",), writes=(wtr,))
            tk.op("y", lambda y: y.dma_start(
                out=bqc[:, 0:1], in_=bq2[:, :].rearrange("h (d o) -> (h d) o", o=1)),
                writes=("bqc",), dma=True)
            tk.op("y", lambda y: y.dma_start(
                out=bqc[:, 1:2], in_=bk2[:, :].rearrange("h (d o) -> (h d) o", o=1)),
                writes=("bqc",), dma=True)
            tk.op("v", lambda v: v.tensor_scalar_mul(bqc[:, 0:1], bqc[:, 0:1], 0.125),
                  reads=("bqc",), writes=("bqc",))
            tk.op("y", lambda y: y.dma_start(
                out=bvc[:, 0:1], in_=bv2[:, :].rearrange("h (d o) -> (h d) o", o=1)),
                writes=("bvc",), dma=True)

            for hf in range(2):
                tk.op("g", lambda g, hf=hf: g.collective_compute(
                    "AllGather", ALU.bypass, replica_groups=[list(range(NCORES))],
                    ins=[ctb1[hf][:, :]], outs=[g1g[hf][:, :]]),
                    reads=(f"g1d{hf}",), writes=(f"g1g{hf}r",))

        tk.barrier()

        with nc.psum_tensor(pfx + "ps_q", [128, 1024], F32) as ps_q, \
             nc.psum_tensor(pfx + "psT2", [128, 1024 + 128], BF16) as psT2, \
             nc.psum_tensor(pfx + "ps_n", [64, 1024], F32) as ps_n:

            for b in range(nb):
                order = [i for i in range(nqc) if (i * 512) % TOK < TOK // 2] + \
                        [i for i in range(nqc) if (i * 512) % TOK >= TOK // 2]
                for pos, i in enumerate(order):
                    slot = pos % 2
                    gsl = g1sb[:, slot * 4096:(slot + 1) * 4096]
                    for kc in range(8):
                        gr = f"g1s{slot}k{kc}"
                        gsrc, greg = g1read(kc, b * seq + i * 512, 512)
                        eng = "y" if kc < 4 else "g"
                        tk.op(eng, lambda e, gsrc=gsrc, kc=kc, gsl=gsl: e.dma_start(
                            out=gsl[:, kc * 512:(kc + 1) * 512], in_=gsrc),
                            reads=(greg,), writes=(gr,), dma=True)
                    grs = tuple(f"g1s{slot}k{kc}" for kc in range(8))
                    for kc in range(8):
                        tk.op("t", lambda t, kc=kc, gsl=gsl: t.matmul(
                            ps_q[:, 0:512], wqT[:, kc * 128:(kc + 1) * 128],
                            gsl[:, kc * 512:(kc + 1) * 512],
                            start=(kc == 0), stop=(kc == 7)),
                            reads=("wqT", grs[kc]), writes=("psq0",))
                    tk.op("v", lambda v, b=b, i=i: v.tensor_scalar(
                        qh[b][:, i * 512:(i + 1) * 512], ps_q[:, 0:512],
                        0.125, bqc[:, 0:1], op0=ALU.mult, op1=ALU.add),
                        reads=("psq0", "bqc"), writes=(f"qh{b}",))
                    for kc in range(8):
                        tk.op("t", lambda t, kc=kc, gsl=gsl: t.matmul(
                            ps_q[:, 512:1024], wkT[:, kc * 128:(kc + 1) * 128],
                            gsl[:, kc * 512:(kc + 1) * 512],
                            start=(kc == 0), stop=(kc == 7)),
                            reads=("wkT", grs[kc]), writes=("psq1",))
                    tk.op("v", lambda v, b=b, i=i: v.tensor_scalar(
                        kh[b][:, i * 512:(i + 1) * 512], ps_q[:, 512:1024],
                        bqc[:, 1:2], None, op0=ALU.add),
                        reads=("psq1", "bqc"), writes=(f"kh{b}",))
                    for kc in range(8):
                        tk.op("t", lambda t, kc=kc, gsl=gsl: t.matmul(
                            ps_q[:, 0:512], wvT[:, kc * 128:(kc + 1) * 128],
                            gsl[:, kc * 512:(kc + 1) * 512],
                            start=(kc == 0), stop=(kc == 7)),
                            reads=("wvT", grs[kc]), writes=("psq0",))
                    tk.op("v", lambda v, slot=slot: v.tensor_scalar(
                        vfm[:, slot * 512:(slot + 1) * 512], ps_q[:, 0:512],
                        bvc[:, 0:1], None, op0=ALU.add),
                        reads=("psq0", "bvc"), writes=(f"vfm{slot}",))
                    for jt in range(4):
                        ti = i * 4 + jt
                        pp = jt % 2
                        psl = psT2[:, pp * 1024: pp * 1024 + 128]
                        tk.op("t", lambda t, slot=slot, jt=jt, psl=psl: t.transpose(
                            psl, vfm[:, slot * 512 + jt * 128: slot * 512 + (jt + 1) * 128],
                            identbf[:]),
                            reads=(f"vfm{slot}", "identbf"), writes=(f"psT2{pp}",))
                        tk.op("v", lambda v, b=b, ti=ti, psl=psl: v.tensor_copy(
                            vt[b][0][:, ti * 65: ti * 65 + 64], psl[:, 0:64]),
                            reads=(f"psT2{pp}",), writes=(f"vt{b}0",))
                        tk.op("v", lambda v, b=b, ti=ti, psl=psl: v.tensor_copy(
                            vt[b][1][:, ti * 65: ti * 65 + 64], psl[:, 64:128]),
                            reads=(f"psT2{pp}",), writes=(f"vt{b}1",))
                    for srcbuf, srcr, base in ((qh[b], f"qh{b}", 0), (kh[b], f"kh{b}", 1)):
                        sl = srcbuf[:, i * 512:(i + 1) * 512]
                        tk.op("v", lambda v, sl=sl: v.tensor_mul(sqs[:, :], sl, sl),
                              reads=(srcr,), writes=("sqs",))
                        tk.op("t", lambda t: t.matmul(
                            ps_n[0:1, 0:512], onesc[0:64, :], sqs[0:64, :],
                            start=True, stop=True),
                            reads=("onesc", "sqs"), writes=("ps_n",))
                        tk.op("t", lambda t: t.matmul(
                            ps_n[0:1, 512:1024], onesc[64:128, :], sqs[64:128, :],
                            start=True, stop=True),
                            reads=("onesc", "sqs"), writes=("ps_n",))
                        for hh in range(2):
                            col = ((b * 2 + hh) * 2 + base) * nqc + i
                            tk.op("v", lambda v, hh=hh, col=col: v.reduce_max(
                                nmx[0:1, col:col + 1], ps_n[0:1, hh * 512:(hh + 1) * 512],
                                axis=AX),
                                reads=("ps_n",), writes=("nmx",))
            for b in range(nb):
                for hh in range(2):
                    u = b * 2 + hh
                    qm = nmx[0:1, (u * 2 + 0) * nqc: (u * 2 + 0) * nqc + nqc]
                    km = nmx[0:1, (u * 2 + 1) * nqc: (u * 2 + 1) * nqc + nqc]
                    tk.op("v", lambda v, qm=qm: v.reduce_max(st4[0:1, 0:1], qm, axis=AX),
                          reads=("nmx",), writes=("stb",))
                    tk.op("v", lambda v, km=km: v.reduce_max(st4[0:1, 1:2], km, axis=AX),
                          reads=("nmx",), writes=("stb",))
                    tk.op("v", lambda v: v.tensor_mul(st4[0:1, 0:1], st4[0:1, 0:1],
                                                      st4[0:1, 1:2]),
                          reads=("stb",), writes=("stb",))
                    tk.op("s", lambda s: s.activation(st4[0:1, 0:1], st4[0:1, 0:1], AF.Sqrt),
                          reads=("stb",), writes=("stb",))
                    tk.op("t", lambda t: t.matmul(ps_q[:, 0:1], ones1[0:1, :],
                                                  st4[0:1, 0:1], start=True, stop=True),
                          reads=("ones1", "stb"), writes=("psq0",))
                    tk.op("v", lambda v, u=u: v.tensor_scalar_mul(
                        negC[:, u:u + 1], ps_q[:, 0:1], -1.0),
                        reads=("psq0",), writes=("negC",))

        tk.barrier()

        ps_s = [ctx.enter_context(nc.psum_tensor(pfx + f"ps_s{s}", [128, GS * 512], F32))
                for s in range(2)]
        ps_o = [ctx.enter_context(nc.psum_tensor(pfx + f"ps_o{s}", [128, 512], F32))
                for s in range(2)]

        def emit_unit(b, hh, qc, g0, kts, gslot, first_av, last_av):
            kAP = kh[b]
            qAP = qh[b]
            u = b * 2 + hh
            for j, kt in enumerate(kts):
                tk.op("t", lambda t, j=j, kt=kt, qc=qc, hh=hh, kAP=kAP, qAP=qAP: t.matmul(
                    ps_s[gslot][:, j * 512:(j + 1) * 512],
                    kAP[hh * 64:(hh + 1) * 64, kt * 128:(kt + 1) * 128],
                    qAP[hh * 64:(hh + 1) * 64, qc * 512:(qc + 1) * 512],
                    start=True, stop=True),
                    reads=(f"kh{b}", f"qh{b}"), writes=(f"pss{gslot}",))
            for j, kt in enumerate(kts):
                if kt >= 4 * qc:
                    jj = kt - 4 * qc
                    tk.op("v", lambda v, j=j, jj=jj: v.tensor_add(
                        ps_s[gslot][:, j * 512:(j + 1) * 512],
                        ps_s[gslot][:, j * 512:(j + 1) * 512],
                        cstr[:, jj * 512:(jj + 1) * 512]),
                        reads=(f"pss{gslot}", "cstr"), writes=(f"pss{gslot}",))
            nk = len(kts)
            tk.op("s", lambda s, nk=nk, u=u: s.activation(
                pT[:, gslot * GS * 512: gslot * GS * 512 + nk * 512],
                ps_s[gslot][:, 0:nk * 512], AF.Exp, bias=negC[:, u:u + 1]),
                reads=(f"pss{gslot}", "negC"), writes=(f"pTg{gslot}",))

        def emit_av(b, hh, qc, g0, kts, gslot, first_av, last_av):
            oslot = qc % 2
            for j, kt in enumerate(kts):
                tk.op("t", lambda t, j=j, kt=kt, b=b, hh=hh,
                      fst=(first_av and j == 0), lst=(last_av and j == len(kts) - 1): t.matmul(
                          ps_o[oslot][0:65, :], vt[b][hh][:, kt * 65:(kt + 1) * 65],
                          pT[:, gslot * GS * 512 + j * 512: gslot * GS * 512 + (j + 1) * 512],
                          start=fst, stop=lst),
                      reads=(f"vt{b}{hh}", f"pTg{gslot}"), writes=(f"pso{oslot}",))
            if last_av:
                tk.op("v", lambda v, oslot=oslot: v.tensor_copy(
                    oout[0:65, oslot * 512:(oslot + 1) * 512], ps_o[oslot][0:65, :]),
                    reads=(f"pso{oslot}",), writes=(f"oouts{oslot}",))
                for cc in range(4):
                    tk.op("t", lambda t, oslot=oslot, cc=cc: t.transpose(
                        ps_o[oslot][:, cc * 65:(cc + 1) * 65],
                        oout[0:65, oslot * 512 + cc * 128: oslot * 512 + (cc + 1) * 128],
                        identf[0:65, 0:65]),
                        reads=(f"oouts{oslot}", "identf"), writes=(f"pso{oslot}",))
                tk.op("v", lambda v, oslot=oslot: v.tensor_copy(
                    oot2[:, oslot * 260:(oslot + 1) * 260], ps_o[oslot][:, 0:260]),
                    reads=(f"pso{oslot}",), writes=(f"oot2s{oslot}",))
                for cc in range(4):
                    gt = b * seq + qc * 512 + cc * 128
                    jd = gt // TOK
                    off = gt % TOK
                    tk.op("y", lambda y, jd=jd, off=off, hh=hh, oslot=oslot, cc=cc:
                          y.dma_start(
                              out=ctb2[jd * TOK + off: jd * TOK + off + 128,
                                       hh * 65:(hh + 1) * 65],
                              in_=oot2[:, oslot * 260 + cc * 65: oslot * 260 + (cc + 1) * 65]),
                          reads=(f"oot2s{oslot}",), writes=(f"outd{(qc + hh) % 4}",),
                          dma=True)

        units = []
        for b in range(nb):
            for hh in range(2):
                for qc in range(nqc):
                    ktmax = 4 * (qc + 1)
                    g0s = list(range(0, ktmax, GS))
                    for gidx, g0 in enumerate(g0s):
                        kts = list(range(g0, min(g0 + GS, ktmax)))
                        units.append((b, hh, qc, g0, kts, gidx == 0,
                                      gidx == len(g0s) - 1))
        for gi, (b, hh, qc, g0, kts, fa, la) in enumerate(units):
            emit_unit(b, hh, qc, g0, kts, gi % 2, fa, la)
            if gi >= 1:
                pb, phh, pqc, pg0, pkts, pfa, pla = units[gi - 1]
                emit_av(pb, phh, pqc, pg0, pkts, (gi - 1) % 2, pfa, pla)
        b, hh, qc, g0, kts, fa, la = units[-1]
        emit_av(b, hh, qc, g0, kts, (len(units) - 1) % 2, fa, la)

        tk.op("g", lambda g: g.collective_compute(
            "AllToAll", ALU.bypass, replica_groups=[list(range(NCORES))],
            ins=[ctb2[:, :]], outs=[rcv[:, :]]),
            reads=("outd0", "outd1", "outd2", "outd3"), writes=("rcvd",))
    tk.barrier()



def emit_l1_ffn(nc, tk, io, n_own=TOK, pfx="d_"):
    M, INNER = 64, 16

    x2s = io["x2s"]
    Win, bin_ = io["Win"], io["bin_"]
    W1, b1, W2, b2 = io["W1"], io["b1"], io["W2"], io["b2"]
    Wout, bout = io["Wout"], io["bout"]
    x4o = io["x4o"]
    rcv = io["rcv"]

    ctx = ExitStack()
    with ctx:
        def sb(name, shape, dtype):
            return ctx.enter_context(nc.sbuf_tensor(pfx + name, shape, dtype))

        identbf = sb("identbf", [128, 128], BF16)
        identf = sb("identf", [128, 128], F32)
        ones1 = sb("ones1", [1, 128], F32)
        epsc = sb("epsc", [128, 1], F32)
        win_sb = sb("win_sb", [128, 8 * M], BF16)
        wout_sb = sb("wout_sb", [M, C], BF16)
        w12 = sb("w12", [M, 3 * INNER + 2], F32)
        boutb = sb("boutb", [128, C], F32)
        brow = sb("brow", [1, C], F32)
        x3b = sb("x3b", [128, 8 * C], F32)
        atile = sb("atile", [128, 2 * 8 * 130], F32)
        xlt = sb("xlt", [128, 2 * C], BF16)
        xT = sb("xT", [128, 2 * 8 * 512], BF16)
        s_f = sb("s_f", [M, 2 * 512], F32)
        h_f = sb("h_f", [M, 4 * 512], F32)
        g_f = sb("g_f", [M, 2 * 512], F32)
        g_bf = sb("g_bf", [M, 2 * 512], BF16)
        o_t = sb("o_t", [128, 2 * C], F32)
        linv = sb("linv", [128, 2 * 16], F32)
        st4 = sb("st4", [128, 2 * 16], F32)

        with nc.psum_tensor(pfx + "ps_s", [64, 2 * 512], F32) as ps_s, \
             nc.psum_tensor(pfx + "ps_out", [128, 2 * 1024], F32) as ps_out, \
             nc.psum_tensor(pfx + "ps_b", [128, 512], F32) as ps_b, \
             nc.psum_tensor(pfx + "psT", [128, 128], BF16) as psT:

            tk.op("g", lambda g: g.memset(identf[:], 1.0), writes=("identf",))
            tk.op("g", lambda g: g.affine_select(identf[:], identf[:], pattern=[[-1, 128]],
                                                 compare_op=ALU.is_equal, fill=0.0, base=0,
                                                 channel_multiplier=1),
                  reads=("identf",), writes=("identf",))
            tk.op("v", lambda v: v.tensor_copy(identbf[:], identf[:]),
                  reads=("identf",), writes=("identbf",))
            tk.op("g", lambda g: g.memset(ones1[:], 1.0), writes=("ones1",))
            tk.op("g", lambda g: g.memset(epsc[:], 1e-5), writes=("epsc",))
            tk.op("g", lambda g: g.dma_start(out=win_sb[:].rearrange("p (kc m) -> p kc m", kc=8),
                                             in_=Win[:, :].rearrange("(kc p) m -> p kc m", p=128)),
                  writes=("win_sb",), dma=True)
            tk.op("g", lambda g: g.dma_start(out=wout_sb[:, :], in_=Wout[:, :]),
                  writes=("wout_sb",), dma=True)
            tk.op("y", lambda y: y.dma_start(out=w12[:, 0:INNER], in_=W1[:, :]),
                  writes=("w12",), dma=True)
            tk.op("y", lambda y: y.dma_start(out=w12[:, INNER:2 * INNER], in_=b1[:, :]),
                  writes=("w12",), dma=True)
            tk.op("y", lambda y: y.dma_start(out=w12[:, 2 * INNER:3 * INNER], in_=W2[:, :]),
                  writes=("w12",), dma=True)
            tk.op("y", lambda y: y.dma_start(
                out=w12[:, 3 * INNER:3 * INNER + 1],
                in_=bin_[:].rearrange("(m o) -> m o", o=1)),
                writes=("w12",), dma=True)
            tk.op("y", lambda y: y.dma_start(
                out=w12[:, 3 * INNER + 1:3 * INNER + 2],
                in_=b2[:].rearrange("(m o) -> m o", o=1)),
                writes=("w12",), dma=True)
            tk.op("y", lambda y: y.dma_start(out=brow[0:1, :],
                                             in_=bout[:].rearrange("(o n) -> o n", o=1)),
                  writes=("brow",), dma=True)
            for i in range(2):
                tk.op("t", lambda t, i=i: t.matmul(ps_b[:, :], ones1[0:1, :],
                                                   brow[0:1, i * 512:(i + 1) * 512],
                                                   start=True, stop=True),
                      reads=("ones1", "brow"), writes=("ps_b",))
                tk.op("s", lambda s, i=i: s.copy(boutb[:, i * 512:(i + 1) * 512], ps_b[:, :]),
                      reads=("ps_b",), writes=("boutb",))

            binc = w12[:, 3 * INNER:3 * INNER + 1]
            b2c = w12[:, 3 * INNER + 1:3 * INNER + 2]

            for ch in range(cdiv(n_own, 512)):
                cs = ch % 2
                t0 = ch * 512
                cw = min(512, n_own - t0)
                for jt in range(cw // 128):
                    ti = 4 * ch + jt
                    ts = ti % 2
                    tc = t0 + jt * 128
                    x3s = x3b[:, (cs * 4 + jt) * C:(cs * 4 + jt + 1) * C]
                    x3r = f"x3{cs}{jt}"
                    tk.op("y", lambda y, tc=tc, x3s=x3s: y.dma_start(
                        out=x3s, in_=x2s[tc:tc + 128, :]), writes=(x3r,), dma=True)
                    ats = tuple(f"at{ts}{r}" for r in range(8))
                    for r in range(8):
                        asl = atile[:, (ts * 8 + r) * 130:(ts * 8 + r + 1) * 130]
                        eng = "y" if r % 2 == 0 else "g"
                        tk.op(eng, lambda e, r=r, tc=tc, asl=asl: e.dma_start(
                            out=asl, in_=rcv[r * TOK + tc: r * TOK + tc + 128, :]),
                            reads=("rcvd",), writes=(ats[r],), dma=True)
                    a3 = atile[:, ts * 1040:(ts + 1) * 1040].rearrange(
                        "p (r c) -> p r c", c=130)
                    for hh in range(2):
                        dst = linv[:, ts * 16 + hh * 8: ts * 16 + (hh + 1) * 8].rearrange(
                            "p (r o) -> p r o", o=1)
                        lsrc = a3[:, :, hh * 65 + 64: hh * 65 + 65]
                        tk.op("v", lambda v, dst=dst, lsrc=lsrc: v.reciprocal(dst, lsrc),
                              reads=ats, writes=(f"linv{ts}",))
                    for r in range(8):
                        asl = atile[:, (ts * 8 + r) * 130:(ts * 8 + r + 1) * 130]
                        for hh in range(2):
                            lcol = linv[:, ts * 16 + hh * 8 + r: ts * 16 + hh * 8 + r + 1]
                            tk.op("v", lambda v, r=r, hh=hh, asl=asl, lcol=lcol, x3s=x3s:
                                  v.scalar_tensor_tensor(
                                      x3s[:, (2 * r + hh) * 64:(2 * r + hh + 1) * 64],
                                      asl[:, hh * 65: hh * 65 + 64], lcol,
                                      x3s[:, (2 * r + hh) * 64:(2 * r + hh + 1) * 64],
                                      op0=ALU.mult, op1=ALU.add),
                                  reads=(ats[r], f"linv{ts}", x3r), writes=(x3r,))
                    xlo = xlt[:, ts * C:(ts + 1) * C]
                    xlr = f"xlt{ts}"
                    emit_ln_tile(tk, x3s, x3r, xlo, xlr,
                                 st4[:, ts * 16:(ts + 1) * 16], epsc, sfx=str(ts))
                    for c in range(8):
                        s_ap = xlo[:, c * 128:(c + 1) * 128]
                        d_ap = xT[:, cs * 4096 + c * 512 + jt * 128:
                                  cs * 4096 + c * 512 + (jt + 1) * 128]
                        tk.op("t", lambda t, s_ap=s_ap: t.transpose(psT[:], s_ap, identbf[:]),
                              reads=(xlr, "identbf"), writes=("psT",))
                        tk.op("s", lambda s, d_ap=d_ap: s.copy(d_ap, psT[:]),
                              reads=("psT",), writes=(f"xT{cs}",))
                for kc in range(8):
                    tk.op("t", lambda t, kc=kc, cw=cw, cs=cs: t.matmul(
                        ps_s[:, cs * 512: cs * 512 + cw], win_sb[:, kc * M:(kc + 1) * M],
                        xT[:, cs * 4096 + kc * 512: cs * 4096 + kc * 512 + cw],
                        start=(kc == 0), stop=(kc == 7)),
                        reads=("win_sb", f"xT{cs}"), writes=(f"ps_s{cs}",))
                tk.op("s", lambda s, cw=cw, cs=cs: s.activation(
                    s_f[:, cs * 512: cs * 512 + cw], ps_s[:, cs * 512: cs * 512 + cw],
                    AF.Identity, bias=binc, scale=1.0),
                    reads=(f"ps_s{cs}", "w12"), writes=(f"s_f{cs}",))
                for i in range(INNER):
                    hs = cs * 2 + (i % 2)
                    w1c = w12[:, i:i + 1]
                    b1c = w12[:, INNER + i:INNER + i + 1]
                    w2c = w12[:, 2 * INNER + i:2 * INNER + i + 1]
                    tk.op("s", lambda s, w1c=w1c, b1c=b1c, cw=cw, cs=cs, hs=hs: s.activation(
                        h_f[:, hs * 512: hs * 512 + cw], s_f[:, cs * 512: cs * 512 + cw],
                        AF.Relu, bias=b1c, scale=w1c),
                        reads=(f"s_f{cs}", "w12"), writes=(f"h_f{hs}",))
                    if i == 0:
                        tk.op("v", lambda v, w2c=w2c, cw=cw, cs=cs, hs=hs: v.tensor_scalar(
                            g_f[:, cs * 512: cs * 512 + cw], h_f[:, hs * 512: hs * 512 + cw],
                            w2c, None, op0=ALU.mult),
                            reads=(f"h_f{hs}", "w12"), writes=(f"g_f{cs}",))
                    else:
                        tk.op("v", lambda v, w2c=w2c, cw=cw, cs=cs, hs=hs: v.scalar_tensor_tensor(
                            g_f[:, cs * 512: cs * 512 + cw], h_f[:, hs * 512: hs * 512 + cw],
                            w2c, g_f[:, cs * 512: cs * 512 + cw],
                            op0=ALU.mult, op1=ALU.add),
                            reads=(f"h_f{hs}", f"g_f{cs}", "w12"), writes=(f"g_f{cs}",))
                tk.op("v", lambda v, cw=cw, cs=cs: v.tensor_scalar(
                    g_bf[:, cs * 512: cs * 512 + cw], g_f[:, cs * 512: cs * 512 + cw],
                    b2c, None, op0=ALU.add),
                    reads=(f"g_f{cs}", "w12"), writes=(f"g_bf{cs}",))
                for jt in range(cw // 128):
                    ts = jt % 2
                    tc = t0 + jt * 128
                    x3s = x3b[:, (cs * 4 + jt) * C:(cs * 4 + jt + 1) * C]
                    x3r = f"x3{cs}{jt}"
                    for nn in range(2):
                        tk.op("t", lambda t, jt=jt, nn=nn, cs=cs: t.matmul(
                            ps_out[:, cs * 1024 + nn * 512: cs * 1024 + (nn + 1) * 512],
                            g_bf[:, cs * 512 + jt * 128: cs * 512 + (jt + 1) * 128],
                            wout_sb[:, nn * 512:(nn + 1) * 512],
                            start=True, stop=True),
                            reads=(f"g_bf{cs}", "wout_sb"), writes=(f"ps_out{cs}",))
                    osl = o_t[:, ts * C:(ts + 1) * C]
                    orr = f"o_t{ts}"
                    tk.op("v", lambda v, osl=osl, cs=cs: v.tensor_add(
                        osl, ps_out[:, cs * 1024:(cs + 1) * 1024], boutb[:, :]),
                        reads=(f"ps_out{cs}", "boutb"), writes=(orr,))
                    tk.op("v", lambda v, osl=osl, x3s=x3s: v.tensor_add(osl, osl, x3s),
                          reads=(orr, x3r), writes=(orr,))
                    tk.op("y", lambda y, tc=tc, osl=osl: y.dma_start(
                        out=x4o[tc:tc + 128, :], in_=osl),
                        reads=(orr,), writes=("x4_out",), dma=True)
            tk.op("y", lambda y: y.nop(nofuse=True), reads=("x4_out",))
    tk.barrier()



def build_megakernel():
    nc = bass.Bass("TRN2", target_bir_lowering=False)
    M, INNER = 64, 16
    io = dict(
        xo=nc.dram_tensor("xo", [TOK, C], F32, kind="ExternalInput"),
        xh=nc.dram_tensor("xh", [HALO, C], F32, kind="ExternalInput"),
        Wqk=nc.dram_tensor("Wqk", [C, 2 * C], F32, kind="ExternalInput"),
        bqk=nc.dram_tensor("bqk", [2 * C], F32, kind="ExternalInput"),
        Wgate=nc.dram_tensor("Wgate", [C, NHEAD], F32, kind="ExternalInput"),
        bgate=nc.dram_tensor("bgate", [NHEAD], F32, kind="ExternalInput"),
        Wproj=nc.dram_tensor("Wproj", [C, C], F32, kind="ExternalInput"),
        bproj=nc.dram_tensor("bproj", [C], F32, kind="ExternalInput"),
        W_r=nc.dram_tensor("W_r", [C, C], F32, kind="ExternalInput"),
        W_k=nc.dram_tensor("W_k", [C, C], F32, kind="ExternalInput"),
        W_v=nc.dram_tensor("W_v", [C, C], F32, kind="ExternalInput"),
        mu_r=nc.dram_tensor("mu_r", [C], F32, kind="ExternalInput"),
        mu_k=nc.dram_tensor("mu_k", [C], F32, kind="ExternalInput"),
        Wq2=nc.dram_tensor("Wq2", [2, HD, C], F32, kind="ExternalInput"),
        bq2=nc.dram_tensor("bq2", [2, HD], F32, kind="ExternalInput"),
        Wk2=nc.dram_tensor("Wk2", [2, HD, C], F32, kind="ExternalInput"),
        bk2=nc.dram_tensor("bk2", [2, HD], F32, kind="ExternalInput"),
        Wv2=nc.dram_tensor("Wv2", [2, HD, C], F32, kind="ExternalInput"),
        bv2=nc.dram_tensor("bv2", [2, HD], F32, kind="ExternalInput"),
        Win=nc.dram_tensor("Win", [C, M], F32, kind="ExternalInput"),
        bin_=nc.dram_tensor("bin_", [M], F32, kind="ExternalInput"),
        W1=nc.dram_tensor("W1", [M, INNER], F32, kind="ExternalInput"),
        b1=nc.dram_tensor("b1", [M, INNER], F32, kind="ExternalInput"),
        W2=nc.dram_tensor("W2", [M, INNER], F32, kind="ExternalInput"),
        b2=nc.dram_tensor("b2", [M], F32, kind="ExternalInput"),
        Wout=nc.dram_tensor("Wout", [M, C], F32, kind="ExternalInput"),
        bout=nc.dram_tensor("bout", [C], F32, kind="ExternalInput"),
        x4o=nc.dram_tensor("xout", [TOK, C], F32, kind="ExternalOutput"),
    )
    x1d = nc.dram_tensor("x1d", [TOK + 1, C], F32)
    x2d = nc.dram_tensor("x2d", [TOK, C], F32)
    io["ldram"] = nc.dram_tensor("ldram", [1, 16 * (TOK + 128)], F32)
    io["x1o"] = x1d[1:TOK + 1, :]
    io["x1p"] = x1d[0:1, :]
    io["x2o"] = x2d
    io["x2src"] = x2d
    io["x2s"] = x2d
    io["ctb1"] = [nc.dram_tensor(f"ctb1{h}", [8 * 128, TOK // 2], BF16) for h in range(2)]
    io["g1g"] = [nc.dram_tensor(f"g1g{h}", [NCORES * 8 * 128, TOK // 2], BF16,
                                addr_space="Shared") for h in range(2)]
    io["ctb2"] = nc.dram_tensor("ctb2", [8 * TOK, 130], F32)
    io["rcv"] = nc.dram_tensor("rcv", [8 * TOK, 130], F32)

    ctx = ExitStack()
    with ctx:
        tk = TK(nc, ctx)
        block = ctx.enter_context(nc.Block())
        emit_l0_attn(nc, tk, io, n_own=TOK, pfx="a_")
        emit_l0_ffn(nc, tk, io, n_own=TOK, pfx="b_")
        emit_l1_attn(nc, tk, io, seq=T, nb=B, pfx="c_")
        emit_l1_ffn(nc, tk, io, n_own=TOK, pfx="d_")
        tk.emit(block)
    return nc


def kernel(L0_attn_x, L0_attn_Wqk, L0_attn_bqk, L0_attn_Wgate, L0_attn_bgate,
           L0_attn_Wproj, L0_attn_bproj, L0_ffn_W_r, L0_ffn_W_k, L0_ffn_W_v,
           L0_ffn_mu_r, L0_ffn_mu_k, L1_attn_Wq, L1_attn_bq, L1_attn_Wk,
           L1_attn_bk, L1_attn_Wv, L1_attn_bv, L1_ffn_Win, L1_ffn_bin_,
           L1_ffn_W1, L1_ffn_b1, L1_ffn_W2, L1_ffn_b2, L1_ffn_Wout, L1_ffn_bout):
    cores = list(range(NCORES))
    xf = np.ascontiguousarray(np.asarray(L0_attn_x, dtype=np.float32).reshape(B * T, C))

    def cc(a):
        return np.ascontiguousarray(a, dtype=np.float32)

    nc = build_megakernel()
    maps = []
    for c in cores:
        q = c % 4
        sl = slice(2 * c, 2 * c + 2)
        maps.append(dict(
            xo=cc(xf[c * TOK:(c + 1) * TOK]),
            xh=(np.zeros((HALO, C), np.float32) if q == 0
                else cc(xf[c * TOK - HALO:c * TOK])),
            Wqk=cc(L0_attn_Wqk), bqk=cc(L0_attn_bqk),
            Wgate=cc(L0_attn_Wgate), bgate=cc(L0_attn_bgate),
            Wproj=cc(L0_attn_Wproj), bproj=cc(L0_attn_bproj),
            W_r=cc(L0_ffn_W_r), W_k=cc(L0_ffn_W_k), W_v=cc(L0_ffn_W_v),
            mu_r=cc(L0_ffn_mu_r), mu_k=cc(L0_ffn_mu_k),
            Wq2=cc(L1_attn_Wq[sl]), bq2=cc(L1_attn_bq[sl]),
            Wk2=cc(L1_attn_Wk[sl]), bk2=cc(L1_attn_bk[sl]),
            Wv2=cc(L1_attn_Wv[sl]), bv2=cc(L1_attn_bv[sl]),
            Win=cc(L1_ffn_Win), bin_=cc(L1_ffn_bin_),
            W1=cc(L1_ffn_W1), b1=cc(L1_ffn_b1),
            W2=cc(L1_ffn_W2), b2=cc(L1_ffn_b2),
            Wout=cc(L1_ffn_Wout), bout=cc(L1_ffn_bout)))
    res = run_bass_kernel_spmd(nc, maps, core_ids=cores)
    out = np.concatenate([np.asarray(r["xout"]) for r in res.results], axis=0)
    return out.reshape(B, T, C)


# version 18
# speedup vs baseline: 1.0226x; 1.0226x over previous
import os
os.environ.setdefault("TRN_ALLOW_ICI_ALL_TO_ALL", "1")

import math
from contextlib import ExitStack

import numpy as np
import ml_dtypes

import concourse.bass as bass
import concourse.mybir as mybir
from concourse.bass_utils import run_bass_kernel_spmd

F32 = mybir.dt.float32
BF16 = mybir.dt.bfloat16
AF = mybir.ActivationFunctionType
ALU = mybir.AluOpType
AX = mybir.AxisListType.X

NCORES = 8
B, T, C = 2, 8192, 1024
TOK = T * B // NCORES
HALO = 1024
NHEAD, HD = 16, 64
NEG = -1.0e30


ENGS = ("t", "s", "v", "g", "y")
NDMACH = 6


class TK:
    def __init__(self, nc, ctx):
        self.nc = nc
        self.sems = {}
        self.cnt = {}
        self.ops = {e: [] for e in ENGS}
        self.hw = {e: {} for e in ENGS}
        self.regions = {}
        self.rr = {e: 0 for e in ENGS}
        for e in ENGS:
            self.sems[e] = ctx.enter_context(nc.semaphore(f"sem_{e}"))
            self.cnt[e] = 0
            for i in range(NDMACH):
                k = f"{e}d{i}"
                self.sems[k] = ctx.enter_context(nc.semaphore(f"sem_{k}"))
                self.cnt[k] = 0

    def _reg(self, name):
        if name not in self.regions:
            self.regions[name] = {"w": [], "r": []}
        return self.regions[name]

    def op(self, eng, fn, reads=(), writes=(), dma=False):
        waits = {}

        def add(ev):
            if ev is None:
                return
            k, v = ev
            if waits.get(k, -1) < v:
                waits[k] = v

        for r in reads:
            for ev in self._reg(r)["w"]:
                add(ev)
        for w in writes:
            reg = self._reg(w)
            for ev in reg["w"]:
                add(ev)
            for ev in reg["r"]:
                add(ev)
        key = eng
        if dma:
            ch = self.rr[eng] % NDMACH
            self.rr[eng] += 1
            key = f"{eng}d{ch}"
            if self.cnt[key] > 0:
                add((key, self.cnt[key]))
        fw = []
        for k, v in waits.items():
            if k == eng == "t":
                continue
            if self.hw[eng].get(k, 0) >= v:
                continue
            self.hw[eng][k] = v
            fw.append((k, v))
        self.cnt[key] += 16 if dma else 1
        ev = (key, self.cnt[key])
        self.ops[eng].append((fw, fn, key))
        for w in writes:
            reg = self._reg(w)
            reg["w"] = [ev]
            reg["r"] = []
        for r in reads:
            self._reg(r)["r"].append(ev)
        return ev

    def barrier(self):
        snap = dict(self.cnt)
        for e in ENGS:
            fw = []
            for k, v in snap.items():
                if v > 0 and k != e:
                    self.hw[e][k] = max(self.hw[e].get(k, 0), v)
                    fw.append((k, v))
            self.cnt[e] += 1
            self.ops[e].append((fw, lambda eng: eng.nop(nofuse=True), e))
        bevs = [(e, self.cnt[e]) for e in ENGS]
        for reg in self.regions.values():
            if reg["w"] or reg["r"]:
                reg["w"] = list(bevs)
                reg["r"] = []

    def emit(self, block):
        def mk(name):
            def body(eng):
                for fw, fn, key in self.ops[name]:
                    for k, v in fw:
                        eng.wait_ge(self.sems[k], v)
                    inst = fn(eng)
                    inst.then_inc(self.sems[key], 16 if "d" in key else 1)
            return body
        block.tensor(mk("t"))
        block.scalar(mk("s"))
        block.vector(mk("v"))
        block.gpsimd(mk("g"))
        block.sync(mk("y"))


def cdiv(a, b):
    return (a + b - 1) // b


def window_of(h):
    return max(4, 1024 >> h)


def nkt_of(h):
    return cdiv(window_of(h) - 1 + 128, 128)



def emit_l0_attn(nc, tk, io, n_own=TOK, pfx="a_"):
    n_all = HALO + n_own
    n_q = n_own + 128
    nt_all = n_all // 128
    nqt = n_q // 128
    nqw = n_q + 384

    xo, xh = io["xo"], io["xh"]
    Wqk, bqk = io["Wqk"], io["bqk"]
    Wgate, bgate = io["Wgate"], io["bgate"]
    Wproj, bproj = io["Wproj"], io["bproj"]
    x1o, x1p = io["x1o"], io["x1p"]
    ldram = io["ldram"]

    ctx = ExitStack()
    with ctx:
        def sb(name, shape, dtype):
            h = ctx.enter_context(nc.sbuf_tensor(pfx + name, shape, dtype))
            return h

        identbf = sb("identbf", [128, 128], BF16)
        identf = sb("identf", [128, 128], F32)
        ones1 = sb("ones1", [1, 128], F32)
        sel = sb("sel", [16, 8 * 128], F32)
        stripE = sb("stripE", [128, 9 * 128], BF16)
        stripO = sb("stripO", [128, 9 * 128], BF16)
        xtile = sb("xtile", [128, 2 * C], F32)
        xlnt = sb("xlnt", [128, 2 * C], BF16)
        xlnT = sb("xlnT", [128, 8 * n_all], BF16)
        wq_s = sb("wq_s", [128, 2 * 1024], BF16)
        wk_s = sb("wk_s", [128, 2 * 1024], BF16)
        wproj_sb = sb("wproj_sb", [128, 8 * 1024], BF16)
        wgate_sb = sb("wgate_sb", [128, 8 * 16], BF16)
        bq8 = sb("bq8", [128, 16], F32)
        bgate_sb = sb("bgate_sb", [16, 1], F32)
        brow = sb("brow", [1, 1024], F32)
        bprojb = sb("bprojb", [128, C], F32)
        hvb = sb("hvb", [128, 1], F32)
        epsc = sb("epsc", [128, 1], F32)
        hvm = sb("hvm", [128, 1], F32)
        qpk = sb("qpk", [128, 2 * nqw], BF16)
        kpk = sb("kpk", [128, 2 * n_all], BF16)
        vb0 = sb("vb0", [128, nt_all * 65], BF16)
        vb1 = sb("vb1", [128, nt_all * 65], BF16)
        pT = sb("pT", [128, 2 * 4 * 128], BF16)
        gout = sb("gout", [128, 8 * n_q], BF16)
        gexp = sb("gexp", [16, n_q], F32)
        gsum = sb("gsum", [1, n_q], F32)
        lscr = sb("lscr", [16, 2 * n_q], F32)
        lfm = lscr[0:16, 0:n_q]
        st_scr = sb("st_scr", [128, 16 * nt_all + 4], F32)
        x0t = xtile[:, 0:C]
        x1t = xtile[:, C:2 * C]

        with nc.psum_tensor(pfx + "psT", [128, 1024 + 128], BF16) as psT, \
             nc.psum_tensor(pfx + "ps_b", [128, 512], F32) as ps_b, \
             nc.psum_tensor(pfx + "ps_g", [128, 512], F32) as ps_g:

            tk.op("g", lambda g: g.memset(identf[:], 1.0), writes=("identf",))
            tk.op("g", lambda g: g.affine_select(identf[:], identf[:], pattern=[[-1, 128]],
                                                 compare_op=ALU.is_equal, fill=0.0, base=0,
                                                 channel_multiplier=1),
                  reads=("identf",), writes=("identf",))
            tk.op("v", lambda v: v.tensor_copy(identbf[:], identf[:]),
                  reads=("identf",), writes=("identbf",))
            tk.op("g", lambda g: g.memset(ones1[:], 1.0), writes=("ones1",))
            tk.op("g", lambda g: g.memset(sel[:], 1.0), writes=("sel",))
            s3 = sel[:].rearrange("h (p c) -> h p c", p=8)
            tk.op("g", lambda g: g.affine_select(s3, s3, pattern=[[128, 8], [1, 128]],
                                                 compare_op=ALU.is_ge, fill=0.0, base=0,
                                                 channel_multiplier=-64),
                  reads=("sel",), writes=("sel",))
            tk.op("g", lambda g: g.affine_select(s3, s3, pattern=[[-128, 8], [-1, 128]],
                                                 compare_op=ALU.is_ge, fill=0.0, base=63,
                                                 channel_multiplier=64),
                  reads=("sel",), writes=("sel",))
            tk.op("g", lambda g: g.memset(epsc[:], 1e-5), writes=("epsc",))
            tk.op("g", lambda g: g.memset(vb0[:], 1.0), writes=("vb0",))
            tk.op("g", lambda g: g.memset(vb1[:], 1.0), writes=("vb1",))

            for kc in range(8):
                tk.op("g", lambda g, kc=kc: g.dma_start(
                    out=wproj_sb[:, kc * 1024:(kc + 1) * 1024], in_=Wproj[kc * 128:(kc + 1) * 128, :]),
                    writes=("wproj_sb",), dma=True)
                tk.op("g", lambda g, kc=kc: g.dma_start(
                    out=wgate_sb[:, kc * 16:(kc + 1) * 16], in_=Wgate[kc * 128:(kc + 1) * 128, :]),
                    writes=("wgate_sb",), dma=True)
            for cc in range(8):
                tk.op("y", lambda y, cc=cc: y.dma_start(
                    out=bq8[:, cc:cc + 1],
                    in_=bqk[cc * 128:(cc + 1) * 128].rearrange("(p o) -> p o", o=1)),
                    writes=("bq8",), dma=True)
                tk.op("y", lambda y, cc=cc: y.dma_start(
                    out=bq8[:, 8 + cc:8 + cc + 1],
                    in_=bqk[1024 + cc * 128:1024 + (cc + 1) * 128].rearrange("(p o) -> p o", o=1)),
                    writes=("bq8",), dma=True)
            tk.op("v", lambda v: v.tensor_scalar_mul(bq8[:, 0:8], bq8[:, 0:8], 0.125),
                  reads=("bq8",), writes=("bq8",))
            tk.op("y", lambda y: y.dma_start(out=bgate_sb[:], in_=bgate[:].rearrange("(n o) -> n o", o=1)),
                  writes=("bgate_sb",), dma=True)
            tk.op("y", lambda y: y.dma_start(out=brow[0:1, 0:1024],
                                             in_=bproj[:].rearrange("(o n) -> o n", o=1)),
                  writes=("brow",), dma=True)
            for i in range(2):
                tk.op("t", lambda t, i=i: t.matmul(ps_b[:, 0:512], ones1[0:1, :],
                                                   brow[0:1, i * 512:(i + 1) * 512],
                                                   start=True, stop=True),
                      reads=("ones1", "brow"), writes=("ps_b",))
                tk.op("s", lambda s, i=i: s.copy(bprojb[:, i * 512:(i + 1) * 512], ps_b[:, 0:512]),
                      reads=("ps_b",), writes=("bprojb",))

            for i in range(nt_all):
                slot = i % 2
                xsl = xtile[:, slot * C:(slot + 1) * C]
                xr = f"xt{slot}"
                src = xh[i * 128:(i + 1) * 128, :] if i < 8 else xo[(i - 8) * 128:(i - 7) * 128, :]
                tk.op("y", lambda y, src=src, xsl=xsl: y.dma_start(out=xsl, in_=src),
                      writes=(xr,), dma=True)
                if i == 0:
                    col = 16 * nt_all
                    tk.op("v", lambda v, xsl=xsl, col=col: v.reduce_max(
                        st_scr[:, col:col + 1], xsl, axis=AX, apply_absolute_value=True),
                        reads=(xr,), writes=("hvv",))
                    tk.op("s", lambda s, col=col: s.activation(
                        st_scr[0:1, col + 1:col + 2], st_scr[0:1, col:col + 1], AF.Sign),
                        reads=("hvv",), writes=("hvv",))
                    tk.op("t", lambda t, col=col: t.matmul(
                        ps_b[:, 0:1], ones1[0:1, :], st_scr[0:1, col + 1:col + 2],
                        start=True, stop=True),
                        reads=("ones1", "hvv"), writes=("ps_b",))
                    tk.op("s", lambda s: s.copy(hvb[:], ps_b[:, 0:1]),
                          reads=("ps_b",), writes=("hvb",))
                    tk.op("v", lambda v: v.tensor_scalar(hvm[:], hvb[:], -1.0, -NEG,
                                                         op0=ALU.add, op1=ALU.mult),
                          reads=("hvb",), writes=("hvm",))
                xl = xlnt[:, slot * C:(slot + 1) * C]
                xlr = f"xl{slot}"
                emit_ln_tile(tk, xsl, xr, xl, xlr, st_scr[:, 16 * i:16 * (i + 1)], epsc,
                             sfx=str(slot))
                for c in range(8):
                    pp = (i * 8 + c) % 2
                    psl = psT[:, pp * 1024: pp * 1024 + 128]
                    s_ap = xl[:, c * 128:(c + 1) * 128]
                    d_ap = xlnT[:, c * n_all + 128 * i: c * n_all + 128 * (i + 1)]
                    tk.op("t", lambda t, s_ap=s_ap, psl=psl: t.transpose(psl, s_ap, identbf[:]),
                          reads=(xlr, "identbf"), writes=(f"psT{pp}",))
                    tk.op("v", lambda v, d_ap=d_ap, psl=psl: v.tensor_copy(d_ap, psl),
                          reads=(f"psT{pp}",), writes=("xlnT",))

            toff = HALO - 128
            tk.op("g", lambda g: g.memset(st_scr[0:16, 16 * nt_all + 2:16 * nt_all + 3], 1.0),
                  writes=("ones16",))
            for i in range(cdiv(n_q, 512)):
                wd = min(512, n_q - i * 512)
                for kc in range(8):
                    tk.op("t", lambda t, i=i, wd=wd, kc=kc: t.matmul(
                        ps_g[0:16, 0:wd], wgate_sb[:, kc * 16:(kc + 1) * 16],
                        xlnT[:, kc * n_all + toff + i * 512: kc * n_all + toff + i * 512 + wd],
                        start=(kc == 0), stop=(kc == 7)),
                        reads=("wgate_sb", "xlnT"), writes=("ps_g",))
                tk.op("s", lambda s, i=i, wd=wd: s.activation(
                    gexp[:, i * 512:i * 512 + wd], ps_g[0:16, 0:wd], AF.Exp,
                    bias=bgate_sb[:], scale=1.0),
                    reads=("ps_g", "bgate_sb"), writes=("gexpr",))
                tk.op("t", lambda t, i=i, wd=wd: t.matmul(
                    ps_g[0:1, 0:wd], st_scr[0:16, 16 * nt_all + 2:16 * nt_all + 3],
                    gexp[:, i * 512:i * 512 + wd], start=True, stop=True),
                    reads=("ones16", "gexpr"), writes=("ps_g",))
                tk.op("s", lambda s, i=i, wd=wd: s.copy(gsum[:, i * 512:i * 512 + wd],
                                                        ps_g[0:1, 0:wd]),
                      reads=("ps_g",), writes=("gsumr",))

        tk.barrier()

        with nc.psum_tensor(pfx + "ps_qk", [128, 512], F32) as ps_qk, \
             nc.psum_tensor(pfx + "ps_qk2", [128, 512], F32) as ps_qk2, \
             nc.psum_tensor(pfx + "psTv", [128, 128], BF16) as psTv, \
             nc.psum_tensor(pfx + "ps_s0", [128, 4 * 128], F32) as ps_s0, \
             nc.psum_tensor(pfx + "ps_s1", [128, 4 * 128], F32) as ps_s1, \
             nc.psum_tensor(pfx + "ps_o0", [65, 128], F32) as ps_o0, \
             nc.psum_tensor(pfx + "ps_o1", [65, 128], F32) as ps_o1:
            ps_s = [ps_s0, ps_s1]
            ps_o = [ps_o0, ps_o1]

            def gen_prep(p):
                slot = p % 2
                wqr, wkr = f"wq{slot}", f"wk{slot}"
                qsrc = Wqk[:, p * 128:(p + 1) * 128].rearrange("(kc p2) c -> p2 kc c", p2=128)
                ksrc = Wqk[:, 1024 + p * 128:1024 + (p + 1) * 128].rearrange(
                    "(kc p2) c -> p2 kc c", p2=128)
                qdst = wq_s[:, slot * 1024:(slot + 1) * 1024].rearrange("q (kc c) -> q kc c", kc=8)
                kdst = wk_s[:, slot * 1024:(slot + 1) * 1024].rearrange("q (kc c) -> q kc c", kc=8)
                yield ("g", (lambda g, qsrc=qsrc, qdst=qdst: g.dma_start(out=qdst, in_=qsrc)),
                       (), (wqr,), True)
                yield ("g", (lambda g, ksrc=ksrc, kdst=kdst: g.dma_start(out=kdst, in_=ksrc)),
                       (), (wkr,), True)
                for i in range(cdiv(nqw, 512)):
                    wd = min(512, nqw - i * 512)
                    for kc in range(8):
                        yield ("t", (lambda t, i=i, wd=wd, kc=kc, slot=slot: t.matmul(
                            ps_qk[:, 0:wd],
                            wq_s[:, slot * 1024 + kc * 128: slot * 1024 + (kc + 1) * 128],
                            xlnT[:, kc * n_all + HALO - 512 + i * 512:
                                 kc * n_all + HALO - 512 + i * 512 + wd],
                            start=(kc == 0), stop=(kc == 7))),
                            (wqr, "xlnT"), ("ps_qk",), False)
                    yield ("s", (lambda s, i=i, wd=wd, p=p, slot=slot: s.activation(
                        qpk[:, slot * nqw + i * 512: slot * nqw + i * 512 + wd],
                        ps_qk[:, 0:wd], AF.Identity,
                        bias=bq8[:, p:p + 1], scale=0.125)),
                        ("ps_qk", "bq8"), (f"qpk{slot}",), False)
                for i in range(cdiv(n_all, 512)):
                    wd = min(512, n_all - i * 512)
                    for kc in range(8):
                        yield ("t", (lambda t, i=i, wd=wd, kc=kc, slot=slot: t.matmul(
                            ps_qk2[:, 0:wd],
                            wk_s[:, slot * 1024 + kc * 128: slot * 1024 + (kc + 1) * 128],
                            xlnT[:, kc * n_all + i * 512: kc * n_all + i * 512 + wd],
                            start=(kc == 0), stop=(kc == 7))),
                            (wkr, "xlnT"), ("ps_qk2",), False)
                    yield ("s", (lambda s, i=i, wd=wd, p=p, slot=slot: s.activation(
                        kpk[:, slot * n_all + i * 512: slot * n_all + i * 512 + wd],
                        ps_qk2[:, 0:wd], AF.Identity,
                        bias=bq8[:, 8 + p:8 + p + 1], scale=1.0)),
                        ("ps_qk2", "bq8"), (f"kpk{slot}",), False)

            def gen_tail(p):
                slot = p % 2
                for strip, stripn, h in ((stripE, "stripE", 2 * p), (stripO, "stripO", 2 * p + 1)):
                    nk = nkt_of(h)
                    w = window_of(h)
                    stripr = stripn
                    sl = strip[:, 0:nk * 128]
                    sl3 = sl.rearrange("q (r i) -> q r i", r=nk)
                    yield ("g", (lambda g, sl=sl: g.memset(sl, 0.0)), (), (stripr,), False)
                    yield ("g", (lambda g, sl3=sl3, nk=nk: g.affine_select(
                        sl3, sl3, pattern=[[128, nk], [1, 128]], compare_op=ALU.is_ge,
                        fill=NEG, base=0, channel_multiplier=-1)),
                        (stripr,), (stripr,), False)
                    yield ("g", (lambda g, sl3=sl3, nk=nk, w=w: g.affine_select(
                        sl3, sl3, pattern=[[-128, nk], [-1, 128]], compare_op=ALU.is_ge,
                        fill=NEG, base=w - 1, channel_multiplier=1)),
                        (stripr,), (stripr,), False)
                for i in range(nt_all):
                    yield ("t", (lambda t, i=i, slot=slot: t.transpose(
                        psTv[:], kpk[:, slot * n_all + i * 128: slot * n_all + (i + 1) * 128],
                        identbf[:])),
                        (f"kpk{slot}", "identbf"), ("psTv",), False)
                    yield ("v", (lambda v, i=i: v.tensor_copy(
                        vb0[:, i * 65: i * 65 + 64], psTv[:, 0:64])),
                        ("psTv",), ("vb0",), False)
                    yield ("v", (lambda v, i=i: v.tensor_copy(
                        vb1[:, i * 65: i * 65 + 64], psTv[:, 64:128])),
                        ("psTv",), ("vb1",), False)

            def gen_units(p, g0):
                gidx = g0
                for j in range(nqt):
                    qs = -128 + 128 * j
                    for sub in range(2):
                        h = 2 * p + sub
                        nk = nkt_of(h)
                        nke = min(nk, (qs + HALO) // 128 + 1)
                        wsh = window_of(h)
                        rl = list(range(nke))
                        parts = [rl[z:z + 4] for z in range(0, nke, 4)]
                        for pi, rpart in enumerate(parts):
                            yield dict(p=p, slot=p % 2, j=j, sub=sub, h=h, wsh=wsh,
                                       qs=qs, rpart=rpart, oslot=gidx % 2,
                                       av_first=(pi == 0),
                                       av_last=(pi == len(parts) - 1))
                        gidx += 1

            def emit_u(u, uslot):
                p, slot, j, sub = u["p"], u["slot"], u["j"], u["sub"]
                qs, wsh, rpart = u["qs"], u["wsh"], u["rpart"]
                qr, kr = f"qpk{slot}", f"kpk{slot}"
                strip = stripE if sub == 0 else stripO
                stripr = "stripE" if sub == 0 else "stripO"
                for idx, r in enumerate(rpart):
                    ks = qs - 128 * r
                    kcol = slot * n_all + ks + HALO
                    qcol = slot * nqw + qs + 512
                    tk.op("t", lambda t, idx=idx, kcol=kcol, qcol=qcol, sub=sub: t.matmul(
                        ps_s[uslot][:, idx * 128:(idx + 1) * 128],
                        kpk[sub * 64:(sub + 1) * 64, kcol:kcol + 128],
                        qpk[sub * 64:(sub + 1) * 64, qcol:qcol + 128],
                        start=True, stop=True),
                        reads=(kr, qr), writes=(f"pss{uslot}",))
                for idx, r in enumerate(rpart):
                    ks = qs - 128 * r
                    lo, hi = 128 * r - 127, 128 * r + 127
                    nw = not (lo >= 0 and hi <= wsh - 1)
                    nh = ks < 0
                    if not (nw or nh):
                        continue
                    ssl = strip[:, r * 128:(r + 1) * 128]
                    dst = ps_s[uslot][:, idx * 128:(idx + 1) * 128]
                    if nw and nh:
                        tk.op("v", lambda v, dst=dst, ssl=ssl: v.scalar_tensor_tensor(
                            dst, dst, hvm[:], ssl, op0=ALU.add, op1=ALU.add),
                            reads=(f"pss{uslot}", "hvm", stripr), writes=(f"pss{uslot}",))
                    elif nw:
                        tk.op("v", lambda v, dst=dst, ssl=ssl: v.tensor_add(dst, dst, ssl),
                              reads=(f"pss{uslot}", stripr), writes=(f"pss{uslot}",))
                    else:
                        tk.op("v", lambda v, dst=dst: v.tensor_scalar_add(dst, dst, hvm[:]),
                              reads=(f"pss{uslot}", "hvm"), writes=(f"pss{uslot}",))
                nr = len(rpart)
                tk.op("s", lambda s, nr=nr: s.activation(
                    pT[:, uslot * 512: uslot * 512 + nr * 128],
                    ps_s[uslot][:, 0:nr * 128], AF.Exp),
                    reads=(f"pss{uslot}",), writes=(f"pTu{uslot}",))

            def emit_av(u, uslot):
                p, slot, j, sub, h = u["p"], u["slot"], u["j"], u["sub"], u["h"]
                qs, rpart, oslot = u["qs"], u["rpart"], u["oslot"]
                vbuf = vb0 if sub == 0 else vb1
                vr = f"vb{sub}"
                nr = len(rpart)
                for idx, r in enumerate(rpart):
                    kidx = (qs - 128 * r + HALO) // 128
                    tk.op("t", lambda t, idx=idx, kidx=kidx, vbuf=vbuf, slot=slot,
                          oslot=oslot,
                          fst=(u["av_first"] and idx == 0),
                          lst=(u["av_last"] and idx == nr - 1): t.matmul(
                              ps_o[oslot][0:65, :],
                              vbuf[:, kidx * 65: kidx * 65 + 65],
                              pT[:, uslot * 512 + idx * 128: uslot * 512 + (idx + 1) * 128],
                              start=fst, stop=lst),
                          reads=(vr, f"pTu{uslot}"), writes=(f"pso{oslot}",))
                if u["av_last"]:
                    go = gout[sub * 64:(sub + 1) * 64,
                              p * n_q + j * 128: p * n_q + (j + 1) * 128]
                    tk.op("s", lambda s, go=go, oslot=oslot: s.copy(
                        go, ps_o[oslot][0:64, :]),
                        reads=(f"pso{oslot}",), writes=(f"gout{p}",))
                    lsl = lscr[0:1, sub * n_q + j * 128: sub * n_q + (j + 1) * 128]
                    tk.op("v", lambda v, lsl=lsl, oslot=oslot: v.tensor_copy(
                        lsl, ps_o[oslot][64:65, :]),
                        reads=(f"pso{oslot}",), writes=("lstr",))
                    if j == nqt - 1 and sub == 1:
                        tk.op("y", lambda y, p=p: y.dma_start(
                            out=ldram[0:1, 2 * p * n_q:(2 * p + 2) * n_q],
                            in_=lscr[0:1, 0:2 * n_q]),
                            reads=("lstr",), writes=("ldr",), dma=True)

            def emit_item(item):
                tk.op(item[0], item[1], reads=item[2], writes=item[3], dma=item[4])

            for item in gen_prep(0):
                emit_item(item)
            for item in gen_tail(0):
                emit_item(item)
            pending = None
            uslot = 0
            gcount = 0
            for p in range(8):
                nxt = gen_prep(p + 1) if p + 1 < 8 else iter(())
                units = list(gen_units(p, gcount))
                gcount += sum(1 for u in units if u["av_first"])
                nper = cdiv(160, len(units))
                for u in units:
                    emit_u(u, uslot)
                    if pending is not None:
                        emit_av(*pending)
                    pending = (u, uslot)
                    uslot ^= 1
                    for _ in range(nper):
                        item = next(nxt, None)
                        if item is None:
                            break
                        emit_item(item)
                for item in nxt:
                    emit_item(item)
                emit_av(*pending)
                pending = None
                if p + 1 < 8:
                    for item in gen_tail(p + 1):
                        emit_item(item)

        tk.barrier()

        with nc.psum_tensor(pfx + "ps_f", [128, 512], F32) as ps_f, \
             nc.psum_tensor(pfx + "ps_p", [128, 2 * 1024], F32) as ps_p:

            tk.op("y", lambda y: y.dma_start(
                out=lfm[:, :],
                in_=ldram[0:1, :].rearrange("o (h q) -> (o h) q", h=16)),
                reads=("ldr",), writes=("lfmr",), dma=True)
            for i in range(cdiv(n_q, 512)):
                wd = min(512, n_q - i * 512)
                tk.op("t", lambda t, i=i, wd=wd: t.matmul(
                    ps_f[0:16, 0:wd], ones1[0:1, 0:16], gsum[:, i * 512:i * 512 + wd],
                    start=True, stop=True),
                    reads=("ones1", "gsumr"), writes=("ps_f",))
                sl = lfm[:, i * 512:i * 512 + wd]
                tk.op("v", lambda v, sl=sl, i=i, wd=wd: v.scalar_tensor_tensor(
                    sl, sl, 1e-20, ps_f[0:16, 0:wd], op0=ALU.add, op1=ALU.mult),
                    reads=("lfmr", "ps_f"), writes=("lfmr",))
            tk.op("s", lambda s: s.activation(lfm[:, :], lfm[:, :], AF.Ln),
                  reads=("lfmr",), writes=("lfmr",))
            tk.op("s", lambda s: s.activation(lfm[:, :], lfm[:, :], AF.Exp, scale=-1.0),
                  reads=("lfmr",), writes=("lfmr",))
            tk.op("v", lambda v: v.tensor_mul(gexp[:, :], gexp[:, :], lfm[:, :]),
                  reads=("gexpr", "lfmr"), writes=("gexpr",))
            for p in range(8):
                for i in range(cdiv(n_q, 512)):
                    wd = min(512, n_q - i * 512)
                    tk.op("t", lambda t, p=p, i=i, wd=wd: t.matmul(
                        ps_f[:, 0:wd], sel[:, p * 128:(p + 1) * 128],
                        gexp[:, i * 512:i * 512 + wd], start=True, stop=True),
                        reads=("sel", "gexpr"), writes=("ps_f",))
                    gsl = gout[:, p * n_q + i * 512: p * n_q + i * 512 + wd]
                    tk.op("v", lambda v, gsl=gsl, wd=wd: v.tensor_mul(gsl, gsl, ps_f[:, 0:wd]),
                          reads=(f"gout{p}", "ps_f"), writes=(f"gout{p}",))
            for j in range(nqt):
                jp = j % 2
                for p in range(8):
                    for nn in range(2):
                        tk.op("t", lambda t, j=j, p=p, nn=nn, jp=jp: t.matmul(
                            ps_p[:, jp * 1024 + nn * 512: jp * 1024 + (nn + 1) * 512],
                            gout[:, p * n_q + j * 128: p * n_q + (j + 1) * 128],
                            wproj_sb[:, p * 1024 + nn * 512: p * 1024 + (nn + 1) * 512],
                            start=(p == 0), stop=(p == 7)),
                            reads=(f"gout{p}", "wproj_sb"), writes=(f"ps_p{jp}",))
                if j == 0:
                    src = xh[896:1024, :]
                else:
                    src = xo[(j - 1) * 128:j * 128, :]
                tk.op("y", lambda y, src=src: y.dma_start(out=x0t, in_=src),
                      writes=("x0t",), dma=True)
                tk.op("v", lambda v, jp=jp: v.tensor_add(
                    x1t, ps_p[:, jp * 1024:(jp + 1) * 1024], bprojb[:, :]),
                      reads=(f"ps_p{jp}", "bprojb"), writes=("x1t",))
                tk.op("v", lambda v: v.tensor_add(x1t, x1t, x0t),
                      reads=("x1t", "x0t"), writes=("x1t",))
                if j == 0:
                    tk.op("v", lambda v: v.tensor_scalar_mul(x1t, x1t, hvb[:]),
                          reads=("x1t", "hvb"), writes=("x1t",))
                    tk.op("y", lambda y: y.dma_start(out=x1p[0:1, :], in_=x1t[127:128, :]),
                          reads=("x1t",), writes=("x1p_out",), dma=True)
                else:
                    tk.op("y", lambda y, j=j: y.dma_start(
                        out=x1o[(j - 1) * 128:j * 128, :], in_=x1t),
                        reads=("x1t",), writes=("x1o_out",), dma=True)
    tk.barrier()


def emit_l0_ffn(nc, tk, io, n_own=TOK, pfx="b_"):
    nt = n_own // 128
    ncol = n_own + 1

    x1o, x1p = io["x1o"], io["x1p"]
    W_r, W_k, W_v = io["W_r"], io["W_k"], io["W_v"]
    mu_r, mu_k = io["mu_r"], io["mu_k"]
    x2o = io["x2o"]
    ctb1 = io["ctb1"]

    ctx = ExitStack()
    with ctx:
        def sb(name, shape, dtype):
            return ctx.enter_context(nc.sbuf_tensor(pfx + name, shape, dtype))

        identbf = sb("identbf", [128, 128], BF16)
        identf = sb("identf", [128, 128], F32)
        epsc = sb("epsc", [128, 1], F32)
        xtile = sb("xtile", [128, 2 * C], F32)
        xlnt = sb("xlnt", [128, 2 * C], BF16)
        xlnT = sb("xlnT", [128, 8 * ncol], BF16)
        xrT = sb("xrT", [128, 2 * 8 * 512], BF16)
        xkT = sb("xkT", [128, 2 * 8 * 512], BF16)
        dT = sb("dT", [128, 2 * 512], BF16)
        wrT = sb("wrT", [128, 8 * C], BF16)
        wkT = sb("wkT", [128, 8 * C], BF16)
        wvT = sb("wvT", [128, 8 * C], BF16)
        wld = sb("wld", [128, 2 * C], BF16)
        mur = sb("mur", [128, 16], F32)
        kact = sb("kact", [128, 8 * 512], BF16)
        krelu = sb("krelu", [128, 2 * 512], F32)
        r_tm = sb("r_tm", [128, 8 * C], F32)
        st_scr = sb("st_scr", [128, 16 * (nt + 1)], F32)
        xlt3 = xlnt
        st43 = sb("st43", [128, 2 * 16], F32)
        xT3 = sb("xT3", [128, 2 * 1024], BF16)

        with nc.psum_tensor(pfx + "psT", [128, 1024 + 128], BF16) as psT:
            tk.op("g", lambda g: g.memset(identf[:], 1.0), writes=("identf",))
            tk.op("g", lambda g: g.affine_select(identf[:], identf[:], pattern=[[-1, 128]],
                                                 compare_op=ALU.is_equal, fill=0.0, base=0,
                                                 channel_multiplier=1),
                  reads=("identf",), writes=("identf",))
            tk.op("v", lambda v: v.tensor_copy(identbf[:], identf[:]),
                  reads=("identf",), writes=("identbf",))
            tk.op("g", lambda g: g.memset(epsc[:], 1e-5), writes=("epsc",))
            for cc in range(8):
                tk.op("y", lambda y, cc=cc: y.dma_start(
                    out=mur[:, cc:cc + 1],
                    in_=mu_r[cc * 128:(cc + 1) * 128].rearrange("(p o) -> p o", o=1)),
                    writes=("mur",), dma=True)
                tk.op("y", lambda y, cc=cc: y.dma_start(
                    out=mur[:, 8 + cc:8 + cc + 1],
                    in_=mu_k[cc * 128:(cc + 1) * 128].rearrange("(p o) -> p o", o=1)),
                    writes=("mur",), dma=True)
            for wi, (W, wT, wtr) in enumerate(((W_r, wrT, "wrT"), (W_k, wkT, "wkT"),
                                               (W_v, wvT, "wvT"))):
                for dc in range(8):
                    ws = (wi * 8 + dc) % 2
                    tk.op("g", lambda g, W=W, dc=dc, ws=ws: g.dma_start(
                        out=wld[:, ws * C:(ws + 1) * C], in_=W[dc * 128:(dc + 1) * 128, :]),
                        writes=(f"wld{ws}",), dma=True)
                    for cc in range(8):
                        pp = (dc * 8 + cc) % 2
                        psl = psT[:, pp * 1024: pp * 1024 + 128]
                        s_ap = wld[:, ws * C + cc * 128: ws * C + (cc + 1) * 128]
                        d_ap = wT[:, cc * 1024 + dc * 128: cc * 1024 + (dc + 1) * 128]
                        tk.op("t", lambda t, s_ap=s_ap, psl=psl: t.transpose(psl, s_ap, identbf[:]),
                              reads=(f"wld{ws}", "identbf"), writes=(f"psT{pp}",))
                        tk.op("v", lambda v, d_ap=d_ap, psl=psl: v.tensor_copy(d_ap, psl),
                              reads=(f"psT{pp}",), writes=(wtr,))
            for i in range(nt + 1):
                slot = i % 2
                xsl = xtile[:, slot * C:(slot + 1) * C]
                xr_ = f"xt{slot}"
                if i < nt:
                    rows = 128
                    src = x1o[i * 128:(i + 1) * 128, :]
                else:
                    rows = 1
                    src = x1p[0:1, :]
                xsl_r = xsl[0:rows, :] if rows == 1 else xsl
                tk.op("y", lambda y, src=src, xsl_r=xsl_r: y.dma_start(out=xsl_r, in_=src),
                      writes=(xr_,), dma=True)
                xl = xlnt[:, slot * C:(slot + 1) * C]
                xlr = f"xl{slot}"
                xs = xsl[0:rows, :]
                xlo = xl[0:rows, :]
                emit_ln_tile(tk, xs, xr_, xlo, xlr, st_scr[:, 16 * i:16 * (i + 1)], epsc,
                             rows=rows, sfx=str(slot))
                for c in range(8):
                    pp = (i * 8 + c) % 2
                    s_ap = xl[0:rows, c * 128:(c + 1) * 128]
                    col0 = c * ncol + (1 + 128 * i if i < nt else 0)
                    d_ap = xlnT[:, col0:col0 + rows]
                    psl = psT[:, pp * 1024: pp * 1024 + rows]
                    tk.op("t", lambda t, s_ap=s_ap, rows=rows, psl=psl: t.transpose(
                        psl, s_ap, identbf[0:rows, 0:rows]),
                          reads=(xlr, "identbf"), writes=(f"psT{pp}",))
                    tk.op("v", lambda v, d_ap=d_ap, psl=psl: v.tensor_copy(d_ap, psl),
                          reads=(f"psT{pp}",), writes=("xlnT",))

        tk.barrier()

        with nc.psum_tensor(pfx + "ps_r", [128, 1024], F32) as ps_r, \
             nc.psum_tensor(pfx + "ps_k", [128, 2 * 512], F32) as ps_k, \
             nc.psum_tensor(pfx + "ps_kv", [128, 1024], F32) as ps_kv, \
             nc.psum_tensor(pfx + "psT3", [128, 1024 + 128], BF16) as psT3:
            for ch in range(cdiv(n_own, 512)):
                t0 = ch * 512
                cw = min(512, n_own - t0)
                cp = ch % 2
                for cc in range(8):
                    cur = xlnT[:, cc * ncol + 1 + t0: cc * ncol + 1 + t0 + cw]
                    prev = xlnT[:, cc * ncol + t0: cc * ncol + t0 + cw]
                    dsl = dT[:, cp * 512: cp * 512 + cw]
                    tk.op("v", lambda v, cur=cur, prev=prev, dsl=dsl: v.tensor_sub(
                        dsl, cur, prev),
                        reads=("xlnT",), writes=(f"dT{cp}",))
                    tk.op("v", lambda v, cc=cc, prev=prev, dsl=dsl, cp=cp, cw=cw:
                          v.scalar_tensor_tensor(
                              xrT[:, (cp * 8 + cc) * 512: (cp * 8 + cc) * 512 + cw], dsl,
                              mur[:, cc:cc + 1], prev, op0=ALU.mult, op1=ALU.add),
                          reads=(f"dT{cp}", "mur", "xlnT"), writes=(f"xrT{cp}",))
                    tk.op("v", lambda v, cc=cc, prev=prev, dsl=dsl, cp=cp, cw=cw:
                          v.scalar_tensor_tensor(
                              xkT[:, (cp * 8 + cc) * 512: (cp * 8 + cc) * 512 + cw], dsl,
                              mur[:, 8 + cc:8 + cc + 1], prev, op0=ALU.mult, op1=ALU.add),
                          reads=(f"dT{cp}", "mur", "xlnT"), writes=(f"xkT{cp}",))
                for dc in range(8):
                    kp = dc % 2
                    for cc in range(8):
                        tk.op("t", lambda t, dc=dc, cc=cc, cp=cp, cw=cw, kp=kp: t.matmul(
                            ps_k[:, kp * 512: kp * 512 + cw],
                            wkT[:, cc * 1024 + dc * 128: cc * 1024 + (dc + 1) * 128],
                            xkT[:, (cp * 8 + cc) * 512: (cp * 8 + cc) * 512 + cw],
                            start=(cc == 0), stop=(cc == 7)),
                            reads=("wkT", f"xkT{cp}"), writes=(f"ps_k{kp}",))
                    tk.op("s", lambda s, cw=cw, kp=kp: s.activation(
                        krelu[:, kp * 512: kp * 512 + cw], ps_k[:, kp * 512: kp * 512 + cw],
                        AF.Relu),
                        reads=(f"ps_k{kp}",), writes=(f"krelu{kp}",))
                    tk.op("s", lambda s, dc=dc, cw=cw, kp=kp: s.activation(
                        kact[:, dc * 512:dc * 512 + cw], krelu[:, kp * 512: kp * 512 + cw],
                        AF.Square),
                        reads=(f"krelu{kp}",), writes=("kact",))
                for jt in range(cw // 128):
                    tcol = t0 + jt * 128
                    for cc in range(8):
                        for nn in range(2):
                            tk.op("t", lambda t, cc=cc, nn=nn, tcol=tcol, cp=cp, t0=t0: t.matmul(
                                ps_r[:, nn * 512:(nn + 1) * 512],
                                xrT[:, (cp * 8 + cc) * 512 + tcol - t0:
                                    (cp * 8 + cc) * 512 + tcol - t0 + 128],
                                wrT[:, cc * 1024 + nn * 512: cc * 1024 + (nn + 1) * 512],
                                start=(cc == 0), stop=(cc == 7)),
                                reads=(f"xrT{cp}", "wrT"), writes=("ps_r",))
                    tk.op("s", lambda s, jt=jt, cp=cp: s.activation(
                        r_tm[:, (cp * 4 + jt) * C:(cp * 4 + jt + 1) * C], ps_r[:, :],
                        AF.Sigmoid),
                        reads=("ps_r",), writes=(f"rtm{cp}{jt}",))
                for jt in range(cw // 128):
                    tcol = t0 + jt * 128
                    for dc in range(8):
                        for nn in range(2):
                            tk.op("t", lambda t, dc=dc, nn=nn, tcol=tcol, t0=t0: t.matmul(
                                ps_kv[:, nn * 512:(nn + 1) * 512],
                                kact[:, dc * 512 + tcol - t0: dc * 512 + tcol - t0 + 128],
                                wvT[:, dc * 1024 + nn * 512: dc * 1024 + (nn + 1) * 512],
                                start=(dc == 0), stop=(dc == 7)),
                                reads=("kact", "wvT"), writes=("ps_kv",))
                    xsl = xtile[:, (jt % 2) * C:(jt % 2 + 1) * C]
                    xr_ = f"xt{jt % 2}"
                    rr = f"rtm{cp}{jt}"
                    rsl = r_tm[:, (cp * 4 + jt) * C:(cp * 4 + jt + 1) * C]
                    tk.op("y", lambda y, tcol=tcol, xsl=xsl: y.dma_start(
                        out=xsl, in_=x1o[tcol:tcol + 128, :]),
                        writes=(xr_,), dma=True)
                    tk.op("v", lambda v, rsl=rsl: v.tensor_mul(rsl, rsl, ps_kv[:, :]),
                        reads=(rr, "ps_kv"), writes=(rr,))
                    tk.op("v", lambda v, rsl=rsl, xsl=xsl: v.tensor_add(rsl, rsl, xsl),
                        reads=(rr, xr_), writes=(rr,))
                    tk.op("y", lambda y, rsl=rsl, tcol=tcol: y.dma_start(
                        out=x2o[tcol:tcol + 128, :], in_=rsl),
                        reads=(rr,), writes=("x2_out",), dma=True)
                    ti = ch * 4 + jt
                    ts = ti % 2
                    x2sl = rsl
                    xlo3 = xlt3[:, ts * C:(ts + 1) * C]
                    emit_ln_tile(tk, x2sl, rr, xlo3, f"xl3{ts}",
                                 st43[:, ts * 16:(ts + 1) * 16], epsc, sfx=f"3x{ts}")
                    for c in range(8):
                        pp = c % 2
                        psl = psT3[:, pp * 1024: pp * 1024 + 128]
                        s_ap = xlo3[:, c * 128:(c + 1) * 128]
                        d_ap = xT3[:, ts * 1024 + c * 128: ts * 1024 + (c + 1) * 128]
                        tk.op("t", lambda t, s_ap=s_ap, psl=psl: t.transpose(psl, s_ap, identbf[:]),
                              reads=(f"xl3{ts}", "identbf"), writes=(f"psT3{pp}",))
                        tk.op("v", lambda v, d_ap=d_ap, psl=psl: v.tensor_copy(d_ap, psl),
                              reads=(f"psT3{pp}",), writes=(f"xT3{ts}",))
                    gdst = ctb1.rearrange("(kc p) t -> p kc t", p=128)[:, :, ti * 128:(ti + 1) * 128]
                    gsrc = xT3[:, ts * 1024:(ts + 1) * 1024].rearrange("p (kc t) -> p kc t", kc=8)
                    tk.op("y", lambda y, gdst=gdst, gsrc=gsrc: y.dma_start(out=gdst, in_=gsrc),
                          reads=(f"xT3{ts}",), writes=("g1d",), dma=True)
    tk.barrier()


def emit_ln_tile(tk, xs, xr_, xlo, xlr, stats, epsc, rows=128, sfx=""):
    st = "st" + sfx
    b01 = stats[0:rows, 0:12]
    mv = stats[0:rows, 12:14]
    mn = stats[0:rows, 12:13]
    vr = stats[0:rows, 13:14]
    sC = stats[0:rows, 14:15]
    sD = stats[0:rows, 15:16]
    tk.op("v", lambda v: v.bn_stats(stats[0:rows, 0:6], xs[:, 0:C // 2]),
          reads=(xr_,), writes=(st,))
    tk.op("v", lambda v: v.bn_stats(stats[0:rows, 6:12], xs[:, C // 2:C]),
          reads=(xr_,), writes=(st,))
    tk.op("v", lambda v: v.bn_aggr(mv, b01), reads=(st,), writes=(st,))
    tk.op("s", lambda s: s.activation(sC, vr, AF.Sqrt, bias=epsc[0:rows, :]),
          reads=(st, "epsc"), writes=(st,))
    tk.op("v", lambda v: v.reciprocal(sC, sC), reads=(st,), writes=(st,))
    tk.op("v", lambda v: v.tensor_mul(sD, mn, sC), reads=(st,), writes=(st,))
    tk.op("v", lambda v: v.tensor_scalar(xlo, xs, sC, sD, op0=ALU.mult, op1=ALU.subtract),
          reads=(xr_, st), writes=(xlr,))



def emit_l1_attn(nc, tk, io, seq=T, nb=B, pfx="c_"):
    ntt = seq // 128
    nqc = seq // 512
    GS = 3

    Wq2, bq2 = io["Wq2"], io["bq2"]
    Wk2, bk2 = io["Wk2"], io["bk2"]
    Wv2, bv2 = io["Wv2"], io["bv2"]
    x2src = io["x2src"]
    ctb1 = io["ctb1"]
    g1g = io["g1g"]
    ctb2 = io["ctb2"]
    rcv = io["rcv"]

    def g1read(kc, gt0, w):
        r = gt0 // TOK
        off = gt0 % TOK
        return g1g[(r * 8 + kc) * 128:(r * 8 + kc + 1) * 128, off:off + w]

    ctx = ExitStack()
    with ctx:
        def sb(name, shape, dtype):
            return ctx.enter_context(nc.sbuf_tensor(pfx + name, shape, dtype))

        identbf = sb("identbf", [128, 128], BF16)
        identf = sb("identf", [128, 128], F32)
        ones1 = sb("ones1", [1, 128], F32)
        onesc = sb("onesc", [128, 1], BF16)
        wload = sb("wload", [64, C], BF16)
        wqT = sb("wqT", [128, 8 * 128], BF16)
        wkT = sb("wkT", [128, 8 * 128], BF16)
        wvT = sb("wvT", [128, 8 * 128], BF16)
        bqc = sb("bqc", [128, 2], F32)
        bvc = sb("bvc", [128, 1], F32)
        g1sb = sb("g1sb", [128, 2 * 8 * 512], BF16)
        qh = [sb(f"qh{b}", [128, seq], BF16) for b in range(nb)]
        kh = [sb(f"kh{b}", [128, seq], BF16) for b in range(nb)]
        vt = [[sb(f"vt{b}{h}", [128, ntt * 65], BF16) for h in range(2)]
              for b in range(nb)]
        vfm = sb("vfm", [128, 2 * 512], BF16)
        pT = sb("pT", [128, 2 * GS * 512], BF16)
        cstr = sb("cstr", [128, 4 * 512], BF16)
        sqs = sb("sqs", [128, 512], BF16)
        nmx = sb("nmx", [1, nb * 2 * 2 * nqc], F32)
        negC = sb("negC", [128, nb * 2], F32)
        oout = sb("oout", [65, 2 * 512], F32)
        oot2 = sb("oot2", [128, 2 * 260], F32)
        st4 = sb("st4", [128, 4], F32)

        with nc.psum_tensor(pfx + "psT", [128, 1024 + 64], BF16) as psT:
            tk.op("g", lambda g: g.memset(identf[:], 1.0), writes=("identf",))
            tk.op("g", lambda g: g.affine_select(identf[:], identf[:], pattern=[[-1, 128]],
                                                 compare_op=ALU.is_equal, fill=0.0, base=0,
                                                 channel_multiplier=1),
                  reads=("identf",), writes=("identf",))
            tk.op("v", lambda v: v.tensor_copy(identbf[:], identf[:]),
                  reads=("identf",), writes=("identbf",))
            tk.op("g", lambda g: g.memset(ones1[:], 1.0), writes=("ones1",))
            tk.op("g", lambda g: g.memset(onesc[:], 1.0), writes=("onesc",))
            tk.op("g", lambda g: g.memset(cstr[:], 0.0), writes=("cstr",))
            c3 = cstr[:].rearrange("p (d i) -> p d i", d=4)
            tk.op("g", lambda g, c3=c3: g.affine_select(
                c3, c3, pattern=[[-128, 4], [1, 512]], compare_op=ALU.is_ge,
                fill=NEG, base=0, channel_multiplier=-1),
                reads=("cstr",), writes=("cstr",))
            for b in range(nb):
                for h in range(2):
                    tk.op("g", lambda g, b=b, h=h: g.memset(vt[b][h][:], 1.0),
                          writes=(f"vt{b}{h}",))

            for W, wT, wtr in ((Wq2, wqT, "wqT"), (Wk2, wkT, "wkT"), (Wv2, wvT, "wvT")):
                for hh in range(2):
                    tk.op("g", lambda g, W=W, hh=hh: g.dma_start(
                        out=wload[:, :], in_=W[hh, :, :]), writes=("wload",), dma=True)
                    for kc in range(8):
                        pp = kc % 2
                        psl = psT[:, pp * 1024: pp * 1024 + 64]
                        s_ap = wload[:, kc * 128:(kc + 1) * 128]
                        d_ap = wT[:, kc * 128 + hh * 64: kc * 128 + (hh + 1) * 64]
                        tk.op("t", lambda t, s_ap=s_ap, psl=psl: t.transpose(
                            psl, s_ap, identbf[0:64, 0:64]),
                            reads=("wload", "identbf"), writes=(f"psT{pp}",))
                        tk.op("v", lambda v, d_ap=d_ap, psl=psl: v.tensor_copy(d_ap, psl),
                              reads=(f"psT{pp}",), writes=(wtr,))
            tk.op("y", lambda y: y.dma_start(
                out=bqc[:, 0:1], in_=bq2[:, :].rearrange("h (d o) -> (h d) o", o=1)),
                writes=("bqc",), dma=True)
            tk.op("y", lambda y: y.dma_start(
                out=bqc[:, 1:2], in_=bk2[:, :].rearrange("h (d o) -> (h d) o", o=1)),
                writes=("bqc",), dma=True)
            tk.op("v", lambda v: v.tensor_scalar_mul(bqc[:, 0:1], bqc[:, 0:1], 0.125),
                  reads=("bqc",), writes=("bqc",))
            tk.op("y", lambda y: y.dma_start(
                out=bvc[:, 0:1], in_=bv2[:, :].rearrange("h (d o) -> (h d) o", o=1)),
                writes=("bvc",), dma=True)

            tk.op("g", lambda g: g.collective_compute(
                "AllGather", ALU.bypass, replica_groups=[list(range(NCORES))],
                ins=[ctb1[:, :]], outs=[g1g[:, :]]),
                reads=("g1d",), writes=("g1gd",))

        tk.barrier()

        with nc.psum_tensor(pfx + "ps_q", [128, 1024], F32) as ps_q, \
             nc.psum_tensor(pfx + "psT2", [128, 1024 + 128], BF16) as psT2, \
             nc.psum_tensor(pfx + "ps_n", [64, 1024], F32) as ps_n:

            for b in range(nb):
                for i in range(nqc):
                    slot = i % 2
                    gsl = g1sb[:, slot * 4096:(slot + 1) * 4096]
                    for kc in range(8):
                        gr = f"g1s{slot}k{kc}"
                        gsrc = g1read(kc, b * seq + i * 512, 512)
                        eng = "y" if kc < 4 else "g"
                        tk.op(eng, lambda e, gsrc=gsrc, kc=kc, gsl=gsl: e.dma_start(
                            out=gsl[:, kc * 512:(kc + 1) * 512], in_=gsrc),
                            reads=("g1gd",), writes=(gr,), dma=True)
                    grs = tuple(f"g1s{slot}k{kc}" for kc in range(8))
                    for kc in range(8):
                        tk.op("t", lambda t, kc=kc, gsl=gsl: t.matmul(
                            ps_q[:, 0:512], wqT[:, kc * 128:(kc + 1) * 128],
                            gsl[:, kc * 512:(kc + 1) * 512],
                            start=(kc == 0), stop=(kc == 7)),
                            reads=("wqT", grs[kc]), writes=("psq0",))
                    tk.op("s", lambda s, b=b, i=i: s.activation(
                        qh[b][:, i * 512:(i + 1) * 512], ps_q[:, 0:512],
                        AF.Identity, bias=bqc[:, 0:1], scale=0.125),
                        reads=("psq0", "bqc"), writes=(f"qh{b}",))
                    for kc in range(8):
                        tk.op("t", lambda t, kc=kc, gsl=gsl: t.matmul(
                            ps_q[:, 512:1024], wkT[:, kc * 128:(kc + 1) * 128],
                            gsl[:, kc * 512:(kc + 1) * 512],
                            start=(kc == 0), stop=(kc == 7)),
                            reads=("wkT", grs[kc]), writes=("psq1",))
                    tk.op("s", lambda s, b=b, i=i: s.activation(
                        kh[b][:, i * 512:(i + 1) * 512], ps_q[:, 512:1024],
                        AF.Identity, bias=bqc[:, 1:2], scale=1.0),
                        reads=("psq1", "bqc"), writes=(f"kh{b}",))
                    for kc in range(8):
                        tk.op("t", lambda t, kc=kc, gsl=gsl: t.matmul(
                            ps_q[:, 0:512], wvT[:, kc * 128:(kc + 1) * 128],
                            gsl[:, kc * 512:(kc + 1) * 512],
                            start=(kc == 0), stop=(kc == 7)),
                            reads=("wvT", grs[kc]), writes=("psq0",))
                    tk.op("s", lambda s, slot=slot: s.activation(
                        vfm[:, slot * 512:(slot + 1) * 512], ps_q[:, 0:512],
                        AF.Identity, bias=bvc[:, 0:1], scale=1.0),
                        reads=("psq0", "bvc"), writes=(f"vfm{slot}",))
                    for jt in range(4):
                        ti = i * 4 + jt
                        pp = jt % 2
                        psl = psT2[:, pp * 1024: pp * 1024 + 128]
                        tk.op("t", lambda t, slot=slot, jt=jt, psl=psl: t.transpose(
                            psl, vfm[:, slot * 512 + jt * 128: slot * 512 + (jt + 1) * 128],
                            identbf[:]),
                            reads=(f"vfm{slot}", "identbf"), writes=(f"psT2{pp}",))
                        tk.op("v", lambda v, b=b, ti=ti, psl=psl: v.tensor_copy(
                            vt[b][0][:, ti * 65: ti * 65 + 64], psl[:, 0:64]),
                            reads=(f"psT2{pp}",), writes=(f"vt{b}0",))
                        tk.op("v", lambda v, b=b, ti=ti, psl=psl: v.tensor_copy(
                            vt[b][1][:, ti * 65: ti * 65 + 64], psl[:, 64:128]),
                            reads=(f"psT2{pp}",), writes=(f"vt{b}1",))
                    for srcbuf, srcr, base in ((qh[b], f"qh{b}", 0), (kh[b], f"kh{b}", 1)):
                        sl = srcbuf[:, i * 512:(i + 1) * 512]
                        tk.op("v", lambda v, sl=sl: v.tensor_mul(sqs[:, :], sl, sl),
                              reads=(srcr,), writes=("sqs",))
                        tk.op("t", lambda t: t.matmul(
                            ps_n[0:1, 0:512], onesc[0:64, :], sqs[0:64, :],
                            start=True, stop=True),
                            reads=("onesc", "sqs"), writes=("ps_n",))
                        tk.op("t", lambda t: t.matmul(
                            ps_n[0:1, 512:1024], onesc[64:128, :], sqs[64:128, :],
                            start=True, stop=True),
                            reads=("onesc", "sqs"), writes=("ps_n",))
                        for hh in range(2):
                            col = ((b * 2 + hh) * 2 + base) * nqc + i
                            tk.op("v", lambda v, hh=hh, col=col: v.reduce_max(
                                nmx[0:1, col:col + 1], ps_n[0:1, hh * 512:(hh + 1) * 512],
                                axis=AX),
                                reads=("ps_n",), writes=("nmx",))
            for b in range(nb):
                for hh in range(2):
                    u = b * 2 + hh
                    qm = nmx[0:1, (u * 2 + 0) * nqc: (u * 2 + 0) * nqc + nqc]
                    km = nmx[0:1, (u * 2 + 1) * nqc: (u * 2 + 1) * nqc + nqc]
                    tk.op("v", lambda v, qm=qm: v.reduce_max(st4[0:1, 0:1], qm, axis=AX),
                          reads=("nmx",), writes=("stb",))
                    tk.op("v", lambda v, km=km: v.reduce_max(st4[0:1, 1:2], km, axis=AX),
                          reads=("nmx",), writes=("stb",))
                    tk.op("v", lambda v: v.tensor_mul(st4[0:1, 0:1], st4[0:1, 0:1],
                                                      st4[0:1, 1:2]),
                          reads=("stb",), writes=("stb",))
                    tk.op("s", lambda s: s.activation(st4[0:1, 0:1], st4[0:1, 0:1], AF.Sqrt),
                          reads=("stb",), writes=("stb",))
                    tk.op("t", lambda t: t.matmul(ps_q[:, 0:1], ones1[0:1, :],
                                                  st4[0:1, 0:1], start=True, stop=True),
                          reads=("ones1", "stb"), writes=("psq0",))
                    tk.op("v", lambda v, u=u: v.tensor_scalar_mul(
                        negC[:, u:u + 1], ps_q[:, 0:1], -1.0),
                        reads=("psq0",), writes=("negC",))

        tk.barrier()

        ps_s = [ctx.enter_context(nc.psum_tensor(pfx + f"ps_s{s}", [128, GS * 512], F32))
                for s in range(2)]
        ps_o = [ctx.enter_context(nc.psum_tensor(pfx + f"ps_o{s}", [128, 512], F32))
                for s in range(2)]

        def emit_unit(b, hh, qc, g0, kts, gslot, first_av, last_av):
            kAP = kh[b]
            qAP = qh[b]
            u = b * 2 + hh
            for j, kt in enumerate(kts):
                tk.op("t", lambda t, j=j, kt=kt, qc=qc, hh=hh, kAP=kAP, qAP=qAP: t.matmul(
                    ps_s[gslot][:, j * 512:(j + 1) * 512],
                    kAP[hh * 64:(hh + 1) * 64, kt * 128:(kt + 1) * 128],
                    qAP[hh * 64:(hh + 1) * 64, qc * 512:(qc + 1) * 512],
                    start=True, stop=True),
                    reads=(f"kh{b}", f"qh{b}"), writes=(f"pss{gslot}",))
            for j, kt in enumerate(kts):
                if kt >= 4 * qc:
                    jj = kt - 4 * qc
                    tk.op("v", lambda v, j=j, jj=jj: v.tensor_add(
                        ps_s[gslot][:, j * 512:(j + 1) * 512],
                        ps_s[gslot][:, j * 512:(j + 1) * 512],
                        cstr[:, jj * 512:(jj + 1) * 512]),
                        reads=(f"pss{gslot}", "cstr"), writes=(f"pss{gslot}",))
            nk = len(kts)
            tk.op("s", lambda s, nk=nk, u=u: s.activation(
                pT[:, gslot * GS * 512: gslot * GS * 512 + nk * 512],
                ps_s[gslot][:, 0:nk * 512], AF.Exp, bias=negC[:, u:u + 1]),
                reads=(f"pss{gslot}", "negC"), writes=(f"pTg{gslot}",))

        def emit_av(b, hh, qc, g0, kts, gslot, first_av, last_av):
            oslot = qc % 2
            for j, kt in enumerate(kts):
                tk.op("t", lambda t, j=j, kt=kt, b=b, hh=hh,
                      fst=(first_av and j == 0), lst=(last_av and j == len(kts) - 1): t.matmul(
                          ps_o[oslot][0:65, :], vt[b][hh][:, kt * 65:(kt + 1) * 65],
                          pT[:, gslot * GS * 512 + j * 512: gslot * GS * 512 + (j + 1) * 512],
                          start=fst, stop=lst),
                      reads=(f"vt{b}{hh}", f"pTg{gslot}"), writes=(f"pso{oslot}",))
            if last_av:
                tk.op("v", lambda v, oslot=oslot: v.tensor_copy(
                    oout[0:65, oslot * 512:(oslot + 1) * 512], ps_o[oslot][0:65, :]),
                    reads=(f"pso{oslot}",), writes=(f"oouts{oslot}",))
                for cc in range(4):
                    tk.op("t", lambda t, oslot=oslot, cc=cc: t.transpose(
                        ps_o[oslot][:, cc * 65:(cc + 1) * 65],
                        oout[0:65, oslot * 512 + cc * 128: oslot * 512 + (cc + 1) * 128],
                        identf[0:65, 0:65]),
                        reads=(f"oouts{oslot}", "identf"), writes=(f"pso{oslot}",))
                tk.op("v", lambda v, oslot=oslot: v.tensor_copy(
                    oot2[:, oslot * 260:(oslot + 1) * 260], ps_o[oslot][:, 0:260]),
                    reads=(f"pso{oslot}",), writes=(f"oot2s{oslot}",))
                for cc in range(4):
                    gt = b * seq + qc * 512 + cc * 128
                    jd = gt // TOK
                    off = gt % TOK
                    tk.op("y", lambda y, jd=jd, off=off, hh=hh, oslot=oslot, cc=cc:
                          y.dma_start(
                              out=ctb2[jd * TOK + off: jd * TOK + off + 128,
                                       hh * 65:(hh + 1) * 65],
                              in_=oot2[:, oslot * 260 + cc * 65: oslot * 260 + (cc + 1) * 65]),
                          reads=(f"oot2s{oslot}",), writes=(f"outd{(qc + hh) % 4}",),
                          dma=True)

        units = []
        for b in range(nb):
            for hh in range(2):
                for qc in range(nqc):
                    ktmax = 4 * (qc + 1)
                    g0s = list(range(0, ktmax, GS))
                    for gidx, g0 in enumerate(g0s):
                        kts = list(range(g0, min(g0 + GS, ktmax)))
                        units.append((b, hh, qc, g0, kts, gidx == 0,
                                      gidx == len(g0s) - 1))
        for gi, (b, hh, qc, g0, kts, fa, la) in enumerate(units):
            emit_unit(b, hh, qc, g0, kts, gi % 2, fa, la)
            if gi >= 1:
                pb, phh, pqc, pg0, pkts, pfa, pla = units[gi - 1]
                emit_av(pb, phh, pqc, pg0, pkts, (gi - 1) % 2, pfa, pla)
        b, hh, qc, g0, kts, fa, la = units[-1]
        emit_av(b, hh, qc, g0, kts, (len(units) - 1) % 2, fa, la)

        tk.op("g", lambda g: g.collective_compute(
            "AllToAll", ALU.bypass, replica_groups=[list(range(NCORES))],
            ins=[ctb2[:, :]], outs=[rcv[:, :]]),
            reads=("outd0", "outd1", "outd2", "outd3"), writes=("rcvd",))
    tk.barrier()



def emit_l1_ffn(nc, tk, io, n_own=TOK, pfx="d_"):
    M, INNER = 64, 16

    x2s = io["x2s"]
    Win, bin_ = io["Win"], io["bin_"]
    W1, b1, W2, b2 = io["W1"], io["b1"], io["W2"], io["b2"]
    Wout, bout = io["Wout"], io["bout"]
    x4o = io["x4o"]
    rcv = io["rcv"]

    ctx = ExitStack()
    with ctx:
        def sb(name, shape, dtype):
            return ctx.enter_context(nc.sbuf_tensor(pfx + name, shape, dtype))

        identbf = sb("identbf", [128, 128], BF16)
        identf = sb("identf", [128, 128], F32)
        ones1 = sb("ones1", [1, 128], F32)
        epsc = sb("epsc", [128, 1], F32)
        win_sb = sb("win_sb", [128, 8 * M], BF16)
        wout_sb = sb("wout_sb", [M, C], BF16)
        w12 = sb("w12", [M, 3 * INNER + 2], F32)
        boutb = sb("boutb", [128, C], F32)
        brow = sb("brow", [1, C], F32)
        x3b = sb("x3b", [128, 8 * C], F32)
        atile = sb("atile", [128, 2 * 8 * 130], F32)
        xlt = sb("xlt", [128, 2 * C], BF16)
        xT = sb("xT", [128, 2 * 8 * 512], BF16)
        s_f = sb("s_f", [M, 2 * 512], F32)
        h_f = sb("h_f", [M, 4 * 512], F32)
        g_f = sb("g_f", [M, 2 * 512], F32)
        g_bf = sb("g_bf", [M, 2 * 512], BF16)
        o_t = sb("o_t", [128, 2 * C], F32)
        linv = sb("linv", [128, 2 * 16], F32)
        st4 = sb("st4", [128, 2 * 16], F32)

        with nc.psum_tensor(pfx + "ps_s", [64, 2 * 512], F32) as ps_s, \
             nc.psum_tensor(pfx + "ps_out", [128, 2 * 1024], F32) as ps_out, \
             nc.psum_tensor(pfx + "ps_b", [128, 512], F32) as ps_b, \
             nc.psum_tensor(pfx + "psT", [128, 128], BF16) as psT:

            tk.op("g", lambda g: g.memset(identf[:], 1.0), writes=("identf",))
            tk.op("g", lambda g: g.affine_select(identf[:], identf[:], pattern=[[-1, 128]],
                                                 compare_op=ALU.is_equal, fill=0.0, base=0,
                                                 channel_multiplier=1),
                  reads=("identf",), writes=("identf",))
            tk.op("v", lambda v: v.tensor_copy(identbf[:], identf[:]),
                  reads=("identf",), writes=("identbf",))
            tk.op("g", lambda g: g.memset(ones1[:], 1.0), writes=("ones1",))
            tk.op("g", lambda g: g.memset(epsc[:], 1e-5), writes=("epsc",))
            tk.op("g", lambda g: g.dma_start(out=win_sb[:].rearrange("p (kc m) -> p kc m", kc=8),
                                             in_=Win[:, :].rearrange("(kc p) m -> p kc m", p=128)),
                  writes=("win_sb",), dma=True)
            tk.op("g", lambda g: g.dma_start(out=wout_sb[:, :], in_=Wout[:, :]),
                  writes=("wout_sb",), dma=True)
            tk.op("y", lambda y: y.dma_start(out=w12[:, 0:INNER], in_=W1[:, :]),
                  writes=("w12",), dma=True)
            tk.op("y", lambda y: y.dma_start(out=w12[:, INNER:2 * INNER], in_=b1[:, :]),
                  writes=("w12",), dma=True)
            tk.op("y", lambda y: y.dma_start(out=w12[:, 2 * INNER:3 * INNER], in_=W2[:, :]),
                  writes=("w12",), dma=True)
            tk.op("y", lambda y: y.dma_start(
                out=w12[:, 3 * INNER:3 * INNER + 1],
                in_=bin_[:].rearrange("(m o) -> m o", o=1)),
                writes=("w12",), dma=True)
            tk.op("y", lambda y: y.dma_start(
                out=w12[:, 3 * INNER + 1:3 * INNER + 2],
                in_=b2[:].rearrange("(m o) -> m o", o=1)),
                writes=("w12",), dma=True)
            tk.op("y", lambda y: y.dma_start(out=brow[0:1, :],
                                             in_=bout[:].rearrange("(o n) -> o n", o=1)),
                  writes=("brow",), dma=True)
            for i in range(2):
                tk.op("t", lambda t, i=i: t.matmul(ps_b[:, :], ones1[0:1, :],
                                                   brow[0:1, i * 512:(i + 1) * 512],
                                                   start=True, stop=True),
                      reads=("ones1", "brow"), writes=("ps_b",))
                tk.op("s", lambda s, i=i: s.copy(boutb[:, i * 512:(i + 1) * 512], ps_b[:, :]),
                      reads=("ps_b",), writes=("boutb",))

            binc = w12[:, 3 * INNER:3 * INNER + 1]
            b2c = w12[:, 3 * INNER + 1:3 * INNER + 2]

            for ch in range(cdiv(n_own, 512)):
                cs = ch % 2
                t0 = ch * 512
                cw = min(512, n_own - t0)
                for jt in range(cw // 128):
                    ti = 4 * ch + jt
                    ts = ti % 2
                    tc = t0 + jt * 128
                    x3s = x3b[:, (cs * 4 + jt) * C:(cs * 4 + jt + 1) * C]
                    x3r = f"x3{cs}{jt}"
                    tk.op("y", lambda y, tc=tc, x3s=x3s: y.dma_start(
                        out=x3s, in_=x2s[tc:tc + 128, :]), writes=(x3r,), dma=True)
                    ats = tuple(f"at{ts}{r}" for r in range(8))
                    for r in range(8):
                        asl = atile[:, (ts * 8 + r) * 130:(ts * 8 + r + 1) * 130]
                        eng = "y" if r % 2 == 0 else "g"
                        tk.op(eng, lambda e, r=r, tc=tc, asl=asl: e.dma_start(
                            out=asl, in_=rcv[r * TOK + tc: r * TOK + tc + 128, :]),
                            reads=("rcvd",), writes=(ats[r],), dma=True)
                    a3 = atile[:, ts * 1040:(ts + 1) * 1040].rearrange(
                        "p (r c) -> p r c", c=130)
                    for hh in range(2):
                        dst = linv[:, ts * 16 + hh * 8: ts * 16 + (hh + 1) * 8].rearrange(
                            "p (r o) -> p r o", o=1)
                        lsrc = a3[:, :, hh * 65 + 64: hh * 65 + 65]
                        tk.op("v", lambda v, dst=dst, lsrc=lsrc: v.reciprocal(dst, lsrc),
                              reads=ats, writes=(f"linv{ts}",))
                    for r in range(8):
                        asl = atile[:, (ts * 8 + r) * 130:(ts * 8 + r + 1) * 130]
                        for hh in range(2):
                            lcol = linv[:, ts * 16 + hh * 8 + r: ts * 16 + hh * 8 + r + 1]
                            tk.op("v", lambda v, r=r, hh=hh, asl=asl, lcol=lcol, x3s=x3s:
                                  v.scalar_tensor_tensor(
                                      x3s[:, (2 * r + hh) * 64:(2 * r + hh + 1) * 64],
                                      asl[:, hh * 65: hh * 65 + 64], lcol,
                                      x3s[:, (2 * r + hh) * 64:(2 * r + hh + 1) * 64],
                                      op0=ALU.mult, op1=ALU.add),
                                  reads=(ats[r], f"linv{ts}", x3r), writes=(x3r,))
                    xlo = xlt[:, ts * C:(ts + 1) * C]
                    xlr = f"xlt{ts}"
                    emit_ln_tile(tk, x3s, x3r, xlo, xlr,
                                 st4[:, ts * 16:(ts + 1) * 16], epsc, sfx=str(ts))
                    for c in range(8):
                        s_ap = xlo[:, c * 128:(c + 1) * 128]
                        d_ap = xT[:, cs * 4096 + c * 512 + jt * 128:
                                  cs * 4096 + c * 512 + (jt + 1) * 128]
                        tk.op("t", lambda t, s_ap=s_ap: t.transpose(psT[:], s_ap, identbf[:]),
                              reads=(xlr, "identbf"), writes=("psT",))
                        tk.op("s", lambda s, d_ap=d_ap: s.copy(d_ap, psT[:]),
                              reads=("psT",), writes=(f"xT{cs}",))
                for kc in range(8):
                    tk.op("t", lambda t, kc=kc, cw=cw, cs=cs: t.matmul(
                        ps_s[:, cs * 512: cs * 512 + cw], win_sb[:, kc * M:(kc + 1) * M],
                        xT[:, cs * 4096 + kc * 512: cs * 4096 + kc * 512 + cw],
                        start=(kc == 0), stop=(kc == 7)),
                        reads=("win_sb", f"xT{cs}"), writes=(f"ps_s{cs}",))
                tk.op("s", lambda s, cw=cw, cs=cs: s.activation(
                    s_f[:, cs * 512: cs * 512 + cw], ps_s[:, cs * 512: cs * 512 + cw],
                    AF.Identity, bias=binc, scale=1.0),
                    reads=(f"ps_s{cs}", "w12"), writes=(f"s_f{cs}",))
                for i in range(INNER):
                    hs = cs * 2 + (i % 2)
                    w1c = w12[:, i:i + 1]
                    b1c = w12[:, INNER + i:INNER + i + 1]
                    w2c = w12[:, 2 * INNER + i:2 * INNER + i + 1]
                    tk.op("s", lambda s, w1c=w1c, b1c=b1c, cw=cw, cs=cs, hs=hs: s.activation(
                        h_f[:, hs * 512: hs * 512 + cw], s_f[:, cs * 512: cs * 512 + cw],
                        AF.Relu, bias=b1c, scale=w1c),
                        reads=(f"s_f{cs}", "w12"), writes=(f"h_f{hs}",))
                    if i == 0:
                        tk.op("v", lambda v, w2c=w2c, cw=cw, cs=cs, hs=hs: v.tensor_scalar(
                            g_f[:, cs * 512: cs * 512 + cw], h_f[:, hs * 512: hs * 512 + cw],
                            w2c, None, op0=ALU.mult),
                            reads=(f"h_f{hs}", "w12"), writes=(f"g_f{cs}",))
                    else:
                        tk.op("v", lambda v, w2c=w2c, cw=cw, cs=cs, hs=hs: v.scalar_tensor_tensor(
                            g_f[:, cs * 512: cs * 512 + cw], h_f[:, hs * 512: hs * 512 + cw],
                            w2c, g_f[:, cs * 512: cs * 512 + cw],
                            op0=ALU.mult, op1=ALU.add),
                            reads=(f"h_f{hs}", f"g_f{cs}", "w12"), writes=(f"g_f{cs}",))
                tk.op("v", lambda v, cw=cw, cs=cs: v.tensor_scalar(
                    g_bf[:, cs * 512: cs * 512 + cw], g_f[:, cs * 512: cs * 512 + cw],
                    b2c, None, op0=ALU.add),
                    reads=(f"g_f{cs}", "w12"), writes=(f"g_bf{cs}",))
                for jt in range(cw // 128):
                    ts = jt % 2
                    tc = t0 + jt * 128
                    x3s = x3b[:, (cs * 4 + jt) * C:(cs * 4 + jt + 1) * C]
                    x3r = f"x3{cs}{jt}"
                    for nn in range(2):
                        tk.op("t", lambda t, jt=jt, nn=nn, cs=cs: t.matmul(
                            ps_out[:, cs * 1024 + nn * 512: cs * 1024 + (nn + 1) * 512],
                            g_bf[:, cs * 512 + jt * 128: cs * 512 + (jt + 1) * 128],
                            wout_sb[:, nn * 512:(nn + 1) * 512],
                            start=True, stop=True),
                            reads=(f"g_bf{cs}", "wout_sb"), writes=(f"ps_out{cs}",))
                    osl = o_t[:, ts * C:(ts + 1) * C]
                    orr = f"o_t{ts}"
                    tk.op("v", lambda v, osl=osl, cs=cs: v.tensor_add(
                        osl, ps_out[:, cs * 1024:(cs + 1) * 1024], boutb[:, :]),
                        reads=(f"ps_out{cs}", "boutb"), writes=(orr,))
                    tk.op("v", lambda v, osl=osl, x3s=x3s: v.tensor_add(osl, osl, x3s),
                          reads=(orr, x3r), writes=(orr,))
                    tk.op("y", lambda y, tc=tc, osl=osl: y.dma_start(
                        out=x4o[tc:tc + 128, :], in_=osl),
                        reads=(orr,), writes=("x4_out",), dma=True)
            tk.op("y", lambda y: y.nop(nofuse=True), reads=("x4_out",))
    tk.barrier()



def build_megakernel():
    nc = bass.Bass("TRN2", target_bir_lowering=False)
    M, INNER = 64, 16
    io = dict(
        xo=nc.dram_tensor("xo", [TOK, C], F32, kind="ExternalInput"),
        xh=nc.dram_tensor("xh", [HALO, C], F32, kind="ExternalInput"),
        Wqk=nc.dram_tensor("Wqk", [C, 2 * C], F32, kind="ExternalInput"),
        bqk=nc.dram_tensor("bqk", [2 * C], F32, kind="ExternalInput"),
        Wgate=nc.dram_tensor("Wgate", [C, NHEAD], F32, kind="ExternalInput"),
        bgate=nc.dram_tensor("bgate", [NHEAD], F32, kind="ExternalInput"),
        Wproj=nc.dram_tensor("Wproj", [C, C], F32, kind="ExternalInput"),
        bproj=nc.dram_tensor("bproj", [C], F32, kind="ExternalInput"),
        W_r=nc.dram_tensor("W_r", [C, C], F32, kind="ExternalInput"),
        W_k=nc.dram_tensor("W_k", [C, C], F32, kind="ExternalInput"),
        W_v=nc.dram_tensor("W_v", [C, C], F32, kind="ExternalInput"),
        mu_r=nc.dram_tensor("mu_r", [C], F32, kind="ExternalInput"),
        mu_k=nc.dram_tensor("mu_k", [C], F32, kind="ExternalInput"),
        Wq2=nc.dram_tensor("Wq2", [2, HD, C], F32, kind="ExternalInput"),
        bq2=nc.dram_tensor("bq2", [2, HD], F32, kind="ExternalInput"),
        Wk2=nc.dram_tensor("Wk2", [2, HD, C], F32, kind="ExternalInput"),
        bk2=nc.dram_tensor("bk2", [2, HD], F32, kind="ExternalInput"),
        Wv2=nc.dram_tensor("Wv2", [2, HD, C], F32, kind="ExternalInput"),
        bv2=nc.dram_tensor("bv2", [2, HD], F32, kind="ExternalInput"),
        Win=nc.dram_tensor("Win", [C, M], F32, kind="ExternalInput"),
        bin_=nc.dram_tensor("bin_", [M], F32, kind="ExternalInput"),
        W1=nc.dram_tensor("W1", [M, INNER], F32, kind="ExternalInput"),
        b1=nc.dram_tensor("b1", [M, INNER], F32, kind="ExternalInput"),
        W2=nc.dram_tensor("W2", [M, INNER], F32, kind="ExternalInput"),
        b2=nc.dram_tensor("b2", [M], F32, kind="ExternalInput"),
        Wout=nc.dram_tensor("Wout", [M, C], F32, kind="ExternalInput"),
        bout=nc.dram_tensor("bout", [C], F32, kind="ExternalInput"),
        x4o=nc.dram_tensor("xout", [TOK, C], F32, kind="ExternalOutput"),
    )
    x1d = nc.dram_tensor("x1d", [TOK + 1, C], F32)
    x2d = nc.dram_tensor("x2d", [TOK, C], F32)
    io["ldram"] = nc.dram_tensor("ldram", [1, 16 * (TOK + 128)], F32)
    io["x1o"] = x1d[1:TOK + 1, :]
    io["x1p"] = x1d[0:1, :]
    io["x2o"] = x2d
    io["x2src"] = x2d
    io["x2s"] = x2d
    io["ctb1"] = nc.dram_tensor("ctb1", [8 * 128, TOK], BF16)
    io["g1g"] = nc.dram_tensor("g1g", [NCORES * 8 * 128, TOK], BF16, addr_space="Shared")
    io["ctb2"] = nc.dram_tensor("ctb2", [8 * TOK, 130], F32)
    io["rcv"] = nc.dram_tensor("rcv", [8 * TOK, 130], F32)

    ctx = ExitStack()
    with ctx:
        tk = TK(nc, ctx)
        block = ctx.enter_context(nc.Block())
        emit_l0_attn(nc, tk, io, n_own=TOK, pfx="a_")
        emit_l0_ffn(nc, tk, io, n_own=TOK, pfx="b_")
        emit_l1_attn(nc, tk, io, seq=T, nb=B, pfx="c_")
        emit_l1_ffn(nc, tk, io, n_own=TOK, pfx="d_")
        tk.emit(block)
    return nc


def kernel(L0_attn_x, L0_attn_Wqk, L0_attn_bqk, L0_attn_Wgate, L0_attn_bgate,
           L0_attn_Wproj, L0_attn_bproj, L0_ffn_W_r, L0_ffn_W_k, L0_ffn_W_v,
           L0_ffn_mu_r, L0_ffn_mu_k, L1_attn_Wq, L1_attn_bq, L1_attn_Wk,
           L1_attn_bk, L1_attn_Wv, L1_attn_bv, L1_ffn_Win, L1_ffn_bin_,
           L1_ffn_W1, L1_ffn_b1, L1_ffn_W2, L1_ffn_b2, L1_ffn_Wout, L1_ffn_bout):
    cores = list(range(NCORES))
    xf = np.ascontiguousarray(np.asarray(L0_attn_x, dtype=np.float32).reshape(B * T, C))

    def cc(a):
        return np.ascontiguousarray(a, dtype=np.float32)

    nc = build_megakernel()
    maps = []
    for c in cores:
        q = c % 4
        sl = slice(2 * c, 2 * c + 2)
        maps.append(dict(
            xo=cc(xf[c * TOK:(c + 1) * TOK]),
            xh=(np.zeros((HALO, C), np.float32) if q == 0
                else cc(xf[c * TOK - HALO:c * TOK])),
            Wqk=cc(L0_attn_Wqk), bqk=cc(L0_attn_bqk),
            Wgate=cc(L0_attn_Wgate), bgate=cc(L0_attn_bgate),
            Wproj=cc(L0_attn_Wproj), bproj=cc(L0_attn_bproj),
            W_r=cc(L0_ffn_W_r), W_k=cc(L0_ffn_W_k), W_v=cc(L0_ffn_W_v),
            mu_r=cc(L0_ffn_mu_r), mu_k=cc(L0_ffn_mu_k),
            Wq2=cc(L1_attn_Wq[sl]), bq2=cc(L1_attn_bq[sl]),
            Wk2=cc(L1_attn_Wk[sl]), bk2=cc(L1_attn_bk[sl]),
            Wv2=cc(L1_attn_Wv[sl]), bv2=cc(L1_attn_bv[sl]),
            Win=cc(L1_ffn_Win), bin_=cc(L1_ffn_bin_),
            W1=cc(L1_ffn_W1), b1=cc(L1_ffn_b1),
            W2=cc(L1_ffn_W2), b2=cc(L1_ffn_b2),
            Wout=cc(L1_ffn_Wout), bout=cc(L1_ffn_bout)))
    res = run_bass_kernel_spmd(nc, maps, core_ids=cores)
    out = np.concatenate([np.asarray(r["xout"]) for r in res.results], axis=0)
    return out.reshape(B, T, C)


# version 19 (final)
# speedup vs baseline: 1.0257x; 1.0030x over previous
import os
os.environ.setdefault("TRN_ALLOW_ICI_ALL_TO_ALL", "1")

import math
from contextlib import ExitStack

import numpy as np
import ml_dtypes

import concourse.bass as bass
import concourse.mybir as mybir
from concourse.bass_utils import run_bass_kernel_spmd

F32 = mybir.dt.float32
BF16 = mybir.dt.bfloat16
AF = mybir.ActivationFunctionType
ALU = mybir.AluOpType
AX = mybir.AxisListType.X

NCORES = 8
B, T, C = 2, 8192, 1024
TOK = T * B // NCORES
HALO = 1024
NHEAD, HD = 16, 64
NEG = -1.0e30


ENGS = ("t", "s", "v", "g", "y")
NDMACH = 6


class TK:
    def __init__(self, nc, ctx):
        self.nc = nc
        self.sems = {}
        self.cnt = {}
        self.ops = {e: [] for e in ENGS}
        self.hw = {e: {} for e in ENGS}
        self.regions = {}
        self.rr = {e: 0 for e in ENGS}
        for e in ENGS:
            self.sems[e] = ctx.enter_context(nc.semaphore(f"sem_{e}"))
            self.cnt[e] = 0
            for i in range(NDMACH):
                k = f"{e}d{i}"
                self.sems[k] = ctx.enter_context(nc.semaphore(f"sem_{k}"))
                self.cnt[k] = 0

    def _reg(self, name):
        if name not in self.regions:
            self.regions[name] = {"w": [], "r": []}
        return self.regions[name]

    def op(self, eng, fn, reads=(), writes=(), dma=False):
        waits = {}

        def add(ev):
            if ev is None:
                return
            k, v = ev
            if waits.get(k, -1) < v:
                waits[k] = v

        for r in reads:
            for ev in self._reg(r)["w"]:
                add(ev)
        for w in writes:
            reg = self._reg(w)
            for ev in reg["w"]:
                add(ev)
            for ev in reg["r"]:
                add(ev)
        key = eng
        if dma:
            ch = self.rr[eng] % NDMACH
            self.rr[eng] += 1
            key = f"{eng}d{ch}"
            if self.cnt[key] > 0:
                add((key, self.cnt[key]))
        fw = []
        for k, v in waits.items():
            if k == eng == "t":
                continue
            if self.hw[eng].get(k, 0) >= v:
                continue
            self.hw[eng][k] = v
            fw.append((k, v))
        self.cnt[key] += 16 if dma else 1
        ev = (key, self.cnt[key])
        self.ops[eng].append((fw, fn, key))
        for w in writes:
            reg = self._reg(w)
            reg["w"] = [ev]
            reg["r"] = []
        for r in reads:
            self._reg(r)["r"].append(ev)
        return ev

    def barrier(self):
        snap = dict(self.cnt)
        for e in ENGS:
            fw = []
            for k, v in snap.items():
                if v > 0 and k != e:
                    self.hw[e][k] = max(self.hw[e].get(k, 0), v)
                    fw.append((k, v))
            self.cnt[e] += 1
            self.ops[e].append((fw, lambda eng: eng.nop(nofuse=True), e))
        bevs = [(e, self.cnt[e]) for e in ENGS]
        for reg in self.regions.values():
            if reg["w"] or reg["r"]:
                reg["w"] = list(bevs)
                reg["r"] = []

    def emit(self, block):
        def mk(name):
            def body(eng):
                for fw, fn, key in self.ops[name]:
                    for k, v in fw:
                        eng.wait_ge(self.sems[k], v)
                    inst = fn(eng)
                    inst.then_inc(self.sems[key], 16 if "d" in key else 1)
            return body
        block.tensor(mk("t"))
        block.scalar(mk("s"))
        block.vector(mk("v"))
        block.gpsimd(mk("g"))
        block.sync(mk("y"))


def cdiv(a, b):
    return (a + b - 1) // b


def window_of(h):
    return max(4, 1024 >> h)


def nkt_of(h):
    return cdiv(window_of(h) - 1 + 128, 128)



def emit_l0_attn(nc, tk, io, n_own=TOK, pfx="a_"):
    n_all = HALO + n_own
    n_q = n_own + 128
    nt_all = n_all // 128
    nqt = n_q // 128
    nqw = n_q + 384

    xo, xh = io["xo"], io["xh"]
    Wqk, bqk = io["Wqk"], io["bqk"]
    Wgate, bgate = io["Wgate"], io["bgate"]
    Wproj, bproj = io["Wproj"], io["bproj"]
    x1o, x1p = io["x1o"], io["x1p"]
    ldram = io["ldram"]

    ctx = ExitStack()
    with ctx:
        def sb(name, shape, dtype):
            h = ctx.enter_context(nc.sbuf_tensor(pfx + name, shape, dtype))
            return h

        identbf = sb("identbf", [128, 128], BF16)
        identf = sb("identf", [128, 128], F32)
        ones1 = sb("ones1", [1, 128], F32)
        sel = sb("sel", [16, 8 * 128], F32)
        stripE = sb("stripE", [128, 9 * 128], BF16)
        stripO = sb("stripO", [128, 9 * 128], BF16)
        xtile = sb("xtile", [128, 2 * C], F32)
        xlnt = sb("xlnt", [128, 2 * C], BF16)
        xlnT = sb("xlnT", [128, 8 * n_all], BF16)
        wq_s = sb("wq_s", [128, 2 * 1024], BF16)
        wk_s = sb("wk_s", [128, 2 * 1024], BF16)
        wproj_sb = sb("wproj_sb", [128, 8 * 1024], BF16)
        wgate_sb = sb("wgate_sb", [128, 8 * 16], BF16)
        bq8 = sb("bq8", [128, 16], F32)
        bgate_sb = sb("bgate_sb", [16, 1], F32)
        brow = sb("brow", [1, 1024], F32)
        bprojb = sb("bprojb", [128, C], F32)
        hvb = sb("hvb", [128, 1], F32)
        epsc = sb("epsc", [128, 1], F32)
        hvm = sb("hvm", [128, 1], F32)
        qpk = sb("qpk", [128, 2 * nqw], BF16)
        kpk = sb("kpk", [128, 2 * n_all], BF16)
        vb0 = sb("vb0", [128, nt_all * 65], BF16)
        vb1 = sb("vb1", [128, nt_all * 65], BF16)
        pT = sb("pT", [128, 2 * 4 * 128], BF16)
        gout = sb("gout", [128, 8 * n_q], BF16)
        gexp = sb("gexp", [16, n_q], F32)
        gsum = sb("gsum", [1, n_q], F32)
        lscr = sb("lscr", [16, 2 * n_q], F32)
        lfm = lscr[0:16, 0:n_q]
        st_scr = sb("st_scr", [128, 16 * nt_all + 4], F32)
        x0t = xtile[:, 0:C]
        x1t = xtile[:, C:2 * C]

        with nc.psum_tensor(pfx + "psT", [128, 1024 + 128], BF16) as psT, \
             nc.psum_tensor(pfx + "ps_b", [128, 512], F32) as ps_b, \
             nc.psum_tensor(pfx + "ps_g", [128, 512], F32) as ps_g:

            tk.op("g", lambda g: g.memset(identf[:], 1.0), writes=("identf",))
            tk.op("g", lambda g: g.affine_select(identf[:], identf[:], pattern=[[-1, 128]],
                                                 compare_op=ALU.is_equal, fill=0.0, base=0,
                                                 channel_multiplier=1),
                  reads=("identf",), writes=("identf",))
            tk.op("v", lambda v: v.tensor_copy(identbf[:], identf[:]),
                  reads=("identf",), writes=("identbf",))
            tk.op("g", lambda g: g.memset(ones1[:], 1.0), writes=("ones1",))
            tk.op("g", lambda g: g.memset(sel[:], 1.0), writes=("sel",))
            s3 = sel[:].rearrange("h (p c) -> h p c", p=8)
            tk.op("g", lambda g: g.affine_select(s3, s3, pattern=[[128, 8], [1, 128]],
                                                 compare_op=ALU.is_ge, fill=0.0, base=0,
                                                 channel_multiplier=-64),
                  reads=("sel",), writes=("sel",))
            tk.op("g", lambda g: g.affine_select(s3, s3, pattern=[[-128, 8], [-1, 128]],
                                                 compare_op=ALU.is_ge, fill=0.0, base=63,
                                                 channel_multiplier=64),
                  reads=("sel",), writes=("sel",))
            tk.op("g", lambda g: g.memset(epsc[:], 1e-5), writes=("epsc",))
            tk.op("g", lambda g: g.memset(vb0[:], 1.0), writes=("vb0",))
            tk.op("g", lambda g: g.memset(vb1[:], 1.0), writes=("vb1",))

            for kc in range(8):
                tk.op("g", lambda g, kc=kc: g.dma_start(
                    out=wproj_sb[:, kc * 1024:(kc + 1) * 1024], in_=Wproj[kc * 128:(kc + 1) * 128, :]),
                    writes=("wproj_sb",), dma=True)
                tk.op("g", lambda g, kc=kc: g.dma_start(
                    out=wgate_sb[:, kc * 16:(kc + 1) * 16], in_=Wgate[kc * 128:(kc + 1) * 128, :]),
                    writes=("wgate_sb",), dma=True)
            for cc in range(8):
                tk.op("y", lambda y, cc=cc: y.dma_start(
                    out=bq8[:, cc:cc + 1],
                    in_=bqk[cc * 128:(cc + 1) * 128].rearrange("(p o) -> p o", o=1)),
                    writes=("bq8",), dma=True)
                tk.op("y", lambda y, cc=cc: y.dma_start(
                    out=bq8[:, 8 + cc:8 + cc + 1],
                    in_=bqk[1024 + cc * 128:1024 + (cc + 1) * 128].rearrange("(p o) -> p o", o=1)),
                    writes=("bq8",), dma=True)
            tk.op("v", lambda v: v.tensor_scalar_mul(bq8[:, 0:8], bq8[:, 0:8], 0.125),
                  reads=("bq8",), writes=("bq8",))
            tk.op("y", lambda y: y.dma_start(out=bgate_sb[:], in_=bgate[:].rearrange("(n o) -> n o", o=1)),
                  writes=("bgate_sb",), dma=True)
            tk.op("y", lambda y: y.dma_start(out=brow[0:1, 0:1024],
                                             in_=bproj[:].rearrange("(o n) -> o n", o=1)),
                  writes=("brow",), dma=True)
            for i in range(2):
                tk.op("t", lambda t, i=i: t.matmul(ps_b[:, 0:512], ones1[0:1, :],
                                                   brow[0:1, i * 512:(i + 1) * 512],
                                                   start=True, stop=True),
                      reads=("ones1", "brow"), writes=("ps_b",))
                tk.op("s", lambda s, i=i: s.copy(bprojb[:, i * 512:(i + 1) * 512], ps_b[:, 0:512]),
                      reads=("ps_b",), writes=("bprojb",))

            for i in range(nt_all):
                slot = i % 2
                xsl = xtile[:, slot * C:(slot + 1) * C]
                xr = f"xt{slot}"
                src = xh[i * 128:(i + 1) * 128, :] if i < 8 else xo[(i - 8) * 128:(i - 7) * 128, :]
                tk.op("y", lambda y, src=src, xsl=xsl: y.dma_start(out=xsl, in_=src),
                      writes=(xr,), dma=True)
                if i == 0:
                    col = 16 * nt_all
                    tk.op("v", lambda v, xsl=xsl, col=col: v.reduce_max(
                        st_scr[:, col:col + 1], xsl, axis=AX, apply_absolute_value=True),
                        reads=(xr,), writes=("hvv",))
                    tk.op("s", lambda s, col=col: s.activation(
                        st_scr[0:1, col + 1:col + 2], st_scr[0:1, col:col + 1], AF.Sign),
                        reads=("hvv",), writes=("hvv",))
                    tk.op("t", lambda t, col=col: t.matmul(
                        ps_b[:, 0:1], ones1[0:1, :], st_scr[0:1, col + 1:col + 2],
                        start=True, stop=True),
                        reads=("ones1", "hvv"), writes=("ps_b",))
                    tk.op("s", lambda s: s.copy(hvb[:], ps_b[:, 0:1]),
                          reads=("ps_b",), writes=("hvb",))
                    tk.op("v", lambda v: v.tensor_scalar(hvm[:], hvb[:], -1.0, -NEG,
                                                         op0=ALU.add, op1=ALU.mult),
                          reads=("hvb",), writes=("hvm",))
                xl = xlnt[:, slot * C:(slot + 1) * C]
                xlr = f"xl{slot}"
                emit_ln_tile(tk, xsl, xr, xl, xlr, st_scr[:, 16 * i:16 * (i + 1)], epsc,
                             sfx=str(slot))
                for c in range(8):
                    pp = (i * 8 + c) % 2
                    psl = psT[:, pp * 1024: pp * 1024 + 128]
                    s_ap = xl[:, c * 128:(c + 1) * 128]
                    d_ap = xlnT[:, c * n_all + 128 * i: c * n_all + 128 * (i + 1)]
                    tk.op("t", lambda t, s_ap=s_ap, psl=psl: t.transpose(psl, s_ap, identbf[:]),
                          reads=(xlr, "identbf"), writes=(f"psT{pp}",))
                    tk.op("s", lambda s, d_ap=d_ap, psl=psl: s.copy(d_ap, psl),
                          reads=(f"psT{pp}",), writes=("xlnT",))

            toff = HALO - 128
            tk.op("g", lambda g: g.memset(st_scr[0:16, 16 * nt_all + 2:16 * nt_all + 3], 1.0),
                  writes=("ones16",))
            for i in range(cdiv(n_q, 512)):
                wd = min(512, n_q - i * 512)
                for kc in range(8):
                    tk.op("t", lambda t, i=i, wd=wd, kc=kc: t.matmul(
                        ps_g[0:16, 0:wd], wgate_sb[:, kc * 16:(kc + 1) * 16],
                        xlnT[:, kc * n_all + toff + i * 512: kc * n_all + toff + i * 512 + wd],
                        start=(kc == 0), stop=(kc == 7)),
                        reads=("wgate_sb", "xlnT"), writes=("ps_g",))
                tk.op("s", lambda s, i=i, wd=wd: s.activation(
                    gexp[:, i * 512:i * 512 + wd], ps_g[0:16, 0:wd], AF.Exp,
                    bias=bgate_sb[:], scale=1.0),
                    reads=("ps_g", "bgate_sb"), writes=("gexpr",))
                tk.op("t", lambda t, i=i, wd=wd: t.matmul(
                    ps_g[0:1, 0:wd], st_scr[0:16, 16 * nt_all + 2:16 * nt_all + 3],
                    gexp[:, i * 512:i * 512 + wd], start=True, stop=True),
                    reads=("ones16", "gexpr"), writes=("ps_g",))
                tk.op("s", lambda s, i=i, wd=wd: s.copy(gsum[:, i * 512:i * 512 + wd],
                                                        ps_g[0:1, 0:wd]),
                      reads=("ps_g",), writes=("gsumr",))

        tk.barrier()

        with nc.psum_tensor(pfx + "ps_qk", [128, 512], F32) as ps_qk, \
             nc.psum_tensor(pfx + "ps_qk2", [128, 512], F32) as ps_qk2, \
             nc.psum_tensor(pfx + "psTv", [128, 128], BF16) as psTv, \
             nc.psum_tensor(pfx + "ps_s0", [128, 4 * 128], F32) as ps_s0, \
             nc.psum_tensor(pfx + "ps_s1", [128, 4 * 128], F32) as ps_s1, \
             nc.psum_tensor(pfx + "ps_o0", [65, 128], F32) as ps_o0, \
             nc.psum_tensor(pfx + "ps_o1", [65, 128], F32) as ps_o1:
            ps_s = [ps_s0, ps_s1]
            ps_o = [ps_o0, ps_o1]

            def gen_prep(p):
                slot = p % 2
                wqr, wkr = f"wq{slot}", f"wk{slot}"
                qsrc = Wqk[:, p * 128:(p + 1) * 128].rearrange("(kc p2) c -> p2 kc c", p2=128)
                ksrc = Wqk[:, 1024 + p * 128:1024 + (p + 1) * 128].rearrange(
                    "(kc p2) c -> p2 kc c", p2=128)
                qdst = wq_s[:, slot * 1024:(slot + 1) * 1024].rearrange("q (kc c) -> q kc c", kc=8)
                kdst = wk_s[:, slot * 1024:(slot + 1) * 1024].rearrange("q (kc c) -> q kc c", kc=8)
                yield ("g", (lambda g, qsrc=qsrc, qdst=qdst: g.dma_start(out=qdst, in_=qsrc)),
                       (), (wqr,), True)
                yield ("g", (lambda g, ksrc=ksrc, kdst=kdst: g.dma_start(out=kdst, in_=ksrc)),
                       (), (wkr,), True)
                for i in range(cdiv(nqw, 512)):
                    wd = min(512, nqw - i * 512)
                    for kc in range(8):
                        yield ("t", (lambda t, i=i, wd=wd, kc=kc, slot=slot: t.matmul(
                            ps_qk[:, 0:wd],
                            wq_s[:, slot * 1024 + kc * 128: slot * 1024 + (kc + 1) * 128],
                            xlnT[:, kc * n_all + HALO - 512 + i * 512:
                                 kc * n_all + HALO - 512 + i * 512 + wd],
                            start=(kc == 0), stop=(kc == 7))),
                            (wqr, "xlnT"), ("ps_qk",), False)
                    yield ("s", (lambda s, i=i, wd=wd, p=p, slot=slot: s.activation(
                        qpk[:, slot * nqw + i * 512: slot * nqw + i * 512 + wd],
                        ps_qk[:, 0:wd], AF.Identity,
                        bias=bq8[:, p:p + 1], scale=0.125)),
                        ("ps_qk", "bq8"), (f"qpk{slot}",), False)
                for i in range(cdiv(n_all, 512)):
                    wd = min(512, n_all - i * 512)
                    for kc in range(8):
                        yield ("t", (lambda t, i=i, wd=wd, kc=kc, slot=slot: t.matmul(
                            ps_qk2[:, 0:wd],
                            wk_s[:, slot * 1024 + kc * 128: slot * 1024 + (kc + 1) * 128],
                            xlnT[:, kc * n_all + i * 512: kc * n_all + i * 512 + wd],
                            start=(kc == 0), stop=(kc == 7))),
                            (wkr, "xlnT"), ("ps_qk2",), False)
                    yield ("s", (lambda s, i=i, wd=wd, p=p, slot=slot: s.activation(
                        kpk[:, slot * n_all + i * 512: slot * n_all + i * 512 + wd],
                        ps_qk2[:, 0:wd], AF.Identity,
                        bias=bq8[:, 8 + p:8 + p + 1], scale=1.0)),
                        ("ps_qk2", "bq8"), (f"kpk{slot}",), False)

            def gen_tail(p):
                slot = p % 2
                for strip, stripn, h in ((stripE, "stripE", 2 * p), (stripO, "stripO", 2 * p + 1)):
                    nk = nkt_of(h)
                    w = window_of(h)
                    stripr = stripn
                    sl = strip[:, 0:nk * 128]
                    sl3 = sl.rearrange("q (r i) -> q r i", r=nk)
                    yield ("g", (lambda g, sl=sl: g.memset(sl, 0.0)), (), (stripr,), False)
                    yield ("g", (lambda g, sl3=sl3, nk=nk: g.affine_select(
                        sl3, sl3, pattern=[[128, nk], [1, 128]], compare_op=ALU.is_ge,
                        fill=NEG, base=0, channel_multiplier=-1)),
                        (stripr,), (stripr,), False)
                    yield ("g", (lambda g, sl3=sl3, nk=nk, w=w: g.affine_select(
                        sl3, sl3, pattern=[[-128, nk], [-1, 128]], compare_op=ALU.is_ge,
                        fill=NEG, base=w - 1, channel_multiplier=1)),
                        (stripr,), (stripr,), False)
                for i in range(nt_all):
                    yield ("t", (lambda t, i=i, slot=slot: t.transpose(
                        psTv[:], kpk[:, slot * n_all + i * 128: slot * n_all + (i + 1) * 128],
                        identbf[:])),
                        (f"kpk{slot}", "identbf"), ("psTv",), False)
                    yield ("v", (lambda v, i=i: v.tensor_copy(
                        vb0[:, i * 65: i * 65 + 64], psTv[:, 0:64])),
                        ("psTv",), ("vb0",), False)
                    yield ("v", (lambda v, i=i: v.tensor_copy(
                        vb1[:, i * 65: i * 65 + 64], psTv[:, 64:128])),
                        ("psTv",), ("vb1",), False)

            def gen_units(p, g0):
                gidx = g0
                for j in range(nqt):
                    qs = -128 + 128 * j
                    for sub in range(2):
                        h = 2 * p + sub
                        nk = nkt_of(h)
                        nke = min(nk, (qs + HALO) // 128 + 1)
                        wsh = window_of(h)
                        rl = list(range(nke))
                        parts = [rl[z:z + 4] for z in range(0, nke, 4)]
                        for pi, rpart in enumerate(parts):
                            yield dict(p=p, slot=p % 2, j=j, sub=sub, h=h, wsh=wsh,
                                       qs=qs, rpart=rpart, oslot=gidx % 2,
                                       av_first=(pi == 0),
                                       av_last=(pi == len(parts) - 1))
                        gidx += 1

            def emit_u(u, uslot):
                p, slot, j, sub = u["p"], u["slot"], u["j"], u["sub"]
                qs, wsh, rpart = u["qs"], u["wsh"], u["rpart"]
                qr, kr = f"qpk{slot}", f"kpk{slot}"
                strip = stripE if sub == 0 else stripO
                stripr = "stripE" if sub == 0 else "stripO"
                for idx, r in enumerate(rpart):
                    ks = qs - 128 * r
                    kcol = slot * n_all + ks + HALO
                    qcol = slot * nqw + qs + 512
                    tk.op("t", lambda t, idx=idx, kcol=kcol, qcol=qcol, sub=sub: t.matmul(
                        ps_s[uslot][:, idx * 128:(idx + 1) * 128],
                        kpk[sub * 64:(sub + 1) * 64, kcol:kcol + 128],
                        qpk[sub * 64:(sub + 1) * 64, qcol:qcol + 128],
                        start=True, stop=True),
                        reads=(kr, qr), writes=(f"pss{uslot}",))
                for idx, r in enumerate(rpart):
                    ks = qs - 128 * r
                    lo, hi = 128 * r - 127, 128 * r + 127
                    nw = not (lo >= 0 and hi <= wsh - 1)
                    nh = ks < 0
                    if not (nw or nh):
                        continue
                    ssl = strip[:, r * 128:(r + 1) * 128]
                    dst = ps_s[uslot][:, idx * 128:(idx + 1) * 128]
                    if nw and nh:
                        tk.op("v", lambda v, dst=dst, ssl=ssl: v.scalar_tensor_tensor(
                            dst, dst, hvm[:], ssl, op0=ALU.add, op1=ALU.add),
                            reads=(f"pss{uslot}", "hvm", stripr), writes=(f"pss{uslot}",))
                    elif nw:
                        tk.op("v", lambda v, dst=dst, ssl=ssl: v.tensor_add(dst, dst, ssl),
                              reads=(f"pss{uslot}", stripr), writes=(f"pss{uslot}",))
                    else:
                        tk.op("v", lambda v, dst=dst: v.tensor_scalar_add(dst, dst, hvm[:]),
                              reads=(f"pss{uslot}", "hvm"), writes=(f"pss{uslot}",))
                nr = len(rpart)
                tk.op("s", lambda s, nr=nr: s.activation(
                    pT[:, uslot * 512: uslot * 512 + nr * 128],
                    ps_s[uslot][:, 0:nr * 128], AF.Exp),
                    reads=(f"pss{uslot}",), writes=(f"pTu{uslot}",))

            def emit_av(u, uslot):
                p, slot, j, sub, h = u["p"], u["slot"], u["j"], u["sub"], u["h"]
                qs, rpart, oslot = u["qs"], u["rpart"], u["oslot"]
                vbuf = vb0 if sub == 0 else vb1
                vr = f"vb{sub}"
                nr = len(rpart)
                for idx, r in enumerate(rpart):
                    kidx = (qs - 128 * r + HALO) // 128
                    tk.op("t", lambda t, idx=idx, kidx=kidx, vbuf=vbuf, slot=slot,
                          oslot=oslot,
                          fst=(u["av_first"] and idx == 0),
                          lst=(u["av_last"] and idx == nr - 1): t.matmul(
                              ps_o[oslot][0:65, :],
                              vbuf[:, kidx * 65: kidx * 65 + 65],
                              pT[:, uslot * 512 + idx * 128: uslot * 512 + (idx + 1) * 128],
                              start=fst, stop=lst),
                          reads=(vr, f"pTu{uslot}"), writes=(f"pso{oslot}",))
                if u["av_last"]:
                    go = gout[sub * 64:(sub + 1) * 64,
                              p * n_q + j * 128: p * n_q + (j + 1) * 128]
                    tk.op("s", lambda s, go=go, oslot=oslot: s.copy(
                        go, ps_o[oslot][0:64, :]),
                        reads=(f"pso{oslot}",), writes=(f"gout{p}",))
                    lsl = lscr[0:1, sub * n_q + j * 128: sub * n_q + (j + 1) * 128]
                    tk.op("v", lambda v, lsl=lsl, oslot=oslot: v.tensor_copy(
                        lsl, ps_o[oslot][64:65, :]),
                        reads=(f"pso{oslot}",), writes=("lstr",))
                    if j == nqt - 1 and sub == 1:
                        tk.op("y", lambda y, p=p: y.dma_start(
                            out=ldram[0:1, 2 * p * n_q:(2 * p + 2) * n_q],
                            in_=lscr[0:1, 0:2 * n_q]),
                            reads=("lstr",), writes=("ldr",), dma=True)

            def emit_item(item):
                tk.op(item[0], item[1], reads=item[2], writes=item[3], dma=item[4])

            for item in gen_prep(0):
                emit_item(item)
            for item in gen_tail(0):
                emit_item(item)
            pending = None
            uslot = 0
            gcount = 0
            for p in range(8):
                nxt = gen_prep(p + 1) if p + 1 < 8 else iter(())
                units = list(gen_units(p, gcount))
                gcount += sum(1 for u in units if u["av_first"])
                nper = cdiv(160, len(units))
                for u in units:
                    emit_u(u, uslot)
                    if pending is not None:
                        emit_av(*pending)
                    pending = (u, uslot)
                    uslot ^= 1
                    for _ in range(nper):
                        item = next(nxt, None)
                        if item is None:
                            break
                        emit_item(item)
                for item in nxt:
                    emit_item(item)
                emit_av(*pending)
                pending = None
                if p + 1 < 8:
                    for item in gen_tail(p + 1):
                        emit_item(item)

        tk.barrier()

        with nc.psum_tensor(pfx + "ps_f", [128, 512], F32) as ps_f, \
             nc.psum_tensor(pfx + "ps_p", [128, 2 * 1024], F32) as ps_p:

            tk.op("y", lambda y: y.dma_start(
                out=lfm[:, :],
                in_=ldram[0:1, :].rearrange("o (h q) -> (o h) q", h=16)),
                reads=("ldr",), writes=("lfmr",), dma=True)
            for i in range(cdiv(n_q, 512)):
                wd = min(512, n_q - i * 512)
                tk.op("t", lambda t, i=i, wd=wd: t.matmul(
                    ps_f[0:16, 0:wd], ones1[0:1, 0:16], gsum[:, i * 512:i * 512 + wd],
                    start=True, stop=True),
                    reads=("ones1", "gsumr"), writes=("ps_f",))
                sl = lfm[:, i * 512:i * 512 + wd]
                tk.op("v", lambda v, sl=sl, i=i, wd=wd: v.scalar_tensor_tensor(
                    sl, sl, 1e-20, ps_f[0:16, 0:wd], op0=ALU.add, op1=ALU.mult),
                    reads=("lfmr", "ps_f"), writes=("lfmr",))
            tk.op("s", lambda s: s.activation(lfm[:, :], lfm[:, :], AF.Ln),
                  reads=("lfmr",), writes=("lfmr",))
            tk.op("s", lambda s: s.activation(lfm[:, :], lfm[:, :], AF.Exp, scale=-1.0),
                  reads=("lfmr",), writes=("lfmr",))
            tk.op("v", lambda v: v.tensor_mul(gexp[:, :], gexp[:, :], lfm[:, :]),
                  reads=("gexpr", "lfmr"), writes=("gexpr",))
            for p in range(8):
                for i in range(cdiv(n_q, 512)):
                    wd = min(512, n_q - i * 512)
                    tk.op("t", lambda t, p=p, i=i, wd=wd: t.matmul(
                        ps_f[:, 0:wd], sel[:, p * 128:(p + 1) * 128],
                        gexp[:, i * 512:i * 512 + wd], start=True, stop=True),
                        reads=("sel", "gexpr"), writes=("ps_f",))
                    gsl = gout[:, p * n_q + i * 512: p * n_q + i * 512 + wd]
                    tk.op("v", lambda v, gsl=gsl, wd=wd: v.tensor_mul(gsl, gsl, ps_f[:, 0:wd]),
                          reads=(f"gout{p}", "ps_f"), writes=(f"gout{p}",))
            for j in range(nqt):
                jp = j % 2
                for p in range(8):
                    for nn in range(2):
                        tk.op("t", lambda t, j=j, p=p, nn=nn, jp=jp: t.matmul(
                            ps_p[:, jp * 1024 + nn * 512: jp * 1024 + (nn + 1) * 512],
                            gout[:, p * n_q + j * 128: p * n_q + (j + 1) * 128],
                            wproj_sb[:, p * 1024 + nn * 512: p * 1024 + (nn + 1) * 512],
                            start=(p == 0), stop=(p == 7)),
                            reads=(f"gout{p}", "wproj_sb"), writes=(f"ps_p{jp}",))
                if j == 0:
                    src = xh[896:1024, :]
                else:
                    src = xo[(j - 1) * 128:j * 128, :]
                tk.op("y", lambda y, src=src: y.dma_start(out=x0t, in_=src),
                      writes=("x0t",), dma=True)
                tk.op("v", lambda v, jp=jp: v.tensor_add(
                    x1t, ps_p[:, jp * 1024:(jp + 1) * 1024], bprojb[:, :]),
                      reads=(f"ps_p{jp}", "bprojb"), writes=("x1t",))
                tk.op("v", lambda v: v.tensor_add(x1t, x1t, x0t),
                      reads=("x1t", "x0t"), writes=("x1t",))
                if j == 0:
                    tk.op("v", lambda v: v.tensor_scalar_mul(x1t, x1t, hvb[:]),
                          reads=("x1t", "hvb"), writes=("x1t",))
                    tk.op("y", lambda y: y.dma_start(out=x1p[0:1, :], in_=x1t[127:128, :]),
                          reads=("x1t",), writes=("x1p_out",), dma=True)
                else:
                    tk.op("y", lambda y, j=j: y.dma_start(
                        out=x1o[(j - 1) * 128:j * 128, :], in_=x1t),
                        reads=("x1t",), writes=("x1o_out",), dma=True)
    tk.barrier()


def emit_l0_ffn(nc, tk, io, n_own=TOK, pfx="b_"):
    nt = n_own // 128
    ncol = n_own + 1

    x1o, x1p = io["x1o"], io["x1p"]
    W_r, W_k, W_v = io["W_r"], io["W_k"], io["W_v"]
    mu_r, mu_k = io["mu_r"], io["mu_k"]
    x2o = io["x2o"]
    ctb1 = io["ctb1"]

    ctx = ExitStack()
    with ctx:
        def sb(name, shape, dtype):
            return ctx.enter_context(nc.sbuf_tensor(pfx + name, shape, dtype))

        identbf = sb("identbf", [128, 128], BF16)
        identf = sb("identf", [128, 128], F32)
        epsc = sb("epsc", [128, 1], F32)
        xtile = sb("xtile", [128, 2 * C], F32)
        xlnt = sb("xlnt", [128, 2 * C], BF16)
        xlnT = sb("xlnT", [128, 8 * ncol], BF16)
        xrT = sb("xrT", [128, 2 * 8 * 512], BF16)
        xkT = sb("xkT", [128, 2 * 8 * 512], BF16)
        dT = sb("dT", [128, 2 * 512], BF16)
        wrT = sb("wrT", [128, 8 * C], BF16)
        wkT = sb("wkT", [128, 8 * C], BF16)
        wvT = sb("wvT", [128, 8 * C], BF16)
        wld = sb("wld", [128, 2 * C], BF16)
        mur = sb("mur", [128, 16], F32)
        kact = sb("kact", [128, 8 * 512], BF16)
        krelu = sb("krelu", [128, 2 * 512], F32)
        r_tm = sb("r_tm", [128, 8 * C], F32)
        st_scr = sb("st_scr", [128, 16 * (nt + 1)], F32)
        xlt3 = xlnt
        st43 = sb("st43", [128, 2 * 16], F32)
        xT3 = sb("xT3", [128, 2 * 1024], BF16)

        with nc.psum_tensor(pfx + "psT", [128, 1024 + 128], BF16) as psT:
            tk.op("g", lambda g: g.memset(identf[:], 1.0), writes=("identf",))
            tk.op("g", lambda g: g.affine_select(identf[:], identf[:], pattern=[[-1, 128]],
                                                 compare_op=ALU.is_equal, fill=0.0, base=0,
                                                 channel_multiplier=1),
                  reads=("identf",), writes=("identf",))
            tk.op("v", lambda v: v.tensor_copy(identbf[:], identf[:]),
                  reads=("identf",), writes=("identbf",))
            tk.op("g", lambda g: g.memset(epsc[:], 1e-5), writes=("epsc",))
            for cc in range(8):
                tk.op("y", lambda y, cc=cc: y.dma_start(
                    out=mur[:, cc:cc + 1],
                    in_=mu_r[cc * 128:(cc + 1) * 128].rearrange("(p o) -> p o", o=1)),
                    writes=("mur",), dma=True)
                tk.op("y", lambda y, cc=cc: y.dma_start(
                    out=mur[:, 8 + cc:8 + cc + 1],
                    in_=mu_k[cc * 128:(cc + 1) * 128].rearrange("(p o) -> p o", o=1)),
                    writes=("mur",), dma=True)
            for wi, (W, wT, wtr) in enumerate(((W_r, wrT, "wrT"), (W_k, wkT, "wkT"),
                                               (W_v, wvT, "wvT"))):
                for dc in range(8):
                    ws = (wi * 8 + dc) % 2
                    tk.op("g", lambda g, W=W, dc=dc, ws=ws: g.dma_start(
                        out=wld[:, ws * C:(ws + 1) * C], in_=W[dc * 128:(dc + 1) * 128, :]),
                        writes=(f"wld{ws}",), dma=True)
                    for cc in range(8):
                        pp = (dc * 8 + cc) % 2
                        psl = psT[:, pp * 1024: pp * 1024 + 128]
                        s_ap = wld[:, ws * C + cc * 128: ws * C + (cc + 1) * 128]
                        d_ap = wT[:, cc * 1024 + dc * 128: cc * 1024 + (dc + 1) * 128]
                        tk.op("t", lambda t, s_ap=s_ap, psl=psl: t.transpose(psl, s_ap, identbf[:]),
                              reads=(f"wld{ws}", "identbf"), writes=(f"psT{pp}",))
                        tk.op("s", lambda s, d_ap=d_ap, psl=psl: s.copy(d_ap, psl),
                              reads=(f"psT{pp}",), writes=(wtr,))
            for i in range(nt + 1):
                slot = i % 2
                xsl = xtile[:, slot * C:(slot + 1) * C]
                xr_ = f"xt{slot}"
                if i < nt:
                    rows = 128
                    src = x1o[i * 128:(i + 1) * 128, :]
                else:
                    rows = 1
                    src = x1p[0:1, :]
                xsl_r = xsl[0:rows, :] if rows == 1 else xsl
                tk.op("y", lambda y, src=src, xsl_r=xsl_r: y.dma_start(out=xsl_r, in_=src),
                      writes=(xr_,), dma=True)
                xl = xlnt[:, slot * C:(slot + 1) * C]
                xlr = f"xl{slot}"
                xs = xsl[0:rows, :]
                xlo = xl[0:rows, :]
                emit_ln_tile(tk, xs, xr_, xlo, xlr, st_scr[:, 16 * i:16 * (i + 1)], epsc,
                             rows=rows, sfx=str(slot))
                for c in range(8):
                    pp = (i * 8 + c) % 2
                    s_ap = xl[0:rows, c * 128:(c + 1) * 128]
                    col0 = c * ncol + (1 + 128 * i if i < nt else 0)
                    d_ap = xlnT[:, col0:col0 + rows]
                    psl = psT[:, pp * 1024: pp * 1024 + rows]
                    tk.op("t", lambda t, s_ap=s_ap, rows=rows, psl=psl: t.transpose(
                        psl, s_ap, identbf[0:rows, 0:rows]),
                          reads=(xlr, "identbf"), writes=(f"psT{pp}",))
                    tk.op("s", lambda s, d_ap=d_ap, psl=psl: s.copy(d_ap, psl),
                          reads=(f"psT{pp}",), writes=("xlnT",))

        tk.barrier()

        with nc.psum_tensor(pfx + "ps_r", [128, 1024], F32) as ps_r, \
             nc.psum_tensor(pfx + "ps_k", [128, 2 * 512], F32) as ps_k, \
             nc.psum_tensor(pfx + "ps_kv", [128, 1024], F32) as ps_kv, \
             nc.psum_tensor(pfx + "psT3", [128, 1024 + 128], BF16) as psT3:
            for ch in range(cdiv(n_own, 512)):
                t0 = ch * 512
                cw = min(512, n_own - t0)
                cp = ch % 2
                for cc in range(8):
                    cur = xlnT[:, cc * ncol + 1 + t0: cc * ncol + 1 + t0 + cw]
                    prev = xlnT[:, cc * ncol + t0: cc * ncol + t0 + cw]
                    dsl = dT[:, cp * 512: cp * 512 + cw]
                    tk.op("v", lambda v, cur=cur, prev=prev, dsl=dsl: v.tensor_sub(
                        dsl, cur, prev),
                        reads=("xlnT",), writes=(f"dT{cp}",))
                    tk.op("v", lambda v, cc=cc, prev=prev, dsl=dsl, cp=cp, cw=cw:
                          v.scalar_tensor_tensor(
                              xrT[:, (cp * 8 + cc) * 512: (cp * 8 + cc) * 512 + cw], dsl,
                              mur[:, cc:cc + 1], prev, op0=ALU.mult, op1=ALU.add),
                          reads=(f"dT{cp}", "mur", "xlnT"), writes=(f"xrT{cp}",))
                    tk.op("v", lambda v, cc=cc, prev=prev, dsl=dsl, cp=cp, cw=cw:
                          v.scalar_tensor_tensor(
                              xkT[:, (cp * 8 + cc) * 512: (cp * 8 + cc) * 512 + cw], dsl,
                              mur[:, 8 + cc:8 + cc + 1], prev, op0=ALU.mult, op1=ALU.add),
                          reads=(f"dT{cp}", "mur", "xlnT"), writes=(f"xkT{cp}",))
                for dc in range(8):
                    kp = dc % 2
                    for cc in range(8):
                        tk.op("t", lambda t, dc=dc, cc=cc, cp=cp, cw=cw, kp=kp: t.matmul(
                            ps_k[:, kp * 512: kp * 512 + cw],
                            wkT[:, cc * 1024 + dc * 128: cc * 1024 + (dc + 1) * 128],
                            xkT[:, (cp * 8 + cc) * 512: (cp * 8 + cc) * 512 + cw],
                            start=(cc == 0), stop=(cc == 7)),
                            reads=("wkT", f"xkT{cp}"), writes=(f"ps_k{kp}",))
                    tk.op("s", lambda s, cw=cw, kp=kp: s.activation(
                        krelu[:, kp * 512: kp * 512 + cw], ps_k[:, kp * 512: kp * 512 + cw],
                        AF.Relu),
                        reads=(f"ps_k{kp}",), writes=(f"krelu{kp}",))
                    tk.op("s", lambda s, dc=dc, cw=cw, kp=kp: s.activation(
                        kact[:, dc * 512:dc * 512 + cw], krelu[:, kp * 512: kp * 512 + cw],
                        AF.Square),
                        reads=(f"krelu{kp}",), writes=("kact",))
                for jt in range(cw // 128):
                    tcol = t0 + jt * 128
                    for cc in range(8):
                        for nn in range(2):
                            tk.op("t", lambda t, cc=cc, nn=nn, tcol=tcol, cp=cp, t0=t0: t.matmul(
                                ps_r[:, nn * 512:(nn + 1) * 512],
                                xrT[:, (cp * 8 + cc) * 512 + tcol - t0:
                                    (cp * 8 + cc) * 512 + tcol - t0 + 128],
                                wrT[:, cc * 1024 + nn * 512: cc * 1024 + (nn + 1) * 512],
                                start=(cc == 0), stop=(cc == 7)),
                                reads=(f"xrT{cp}", "wrT"), writes=("ps_r",))
                    tk.op("s", lambda s, jt=jt, cp=cp: s.activation(
                        r_tm[:, (cp * 4 + jt) * C:(cp * 4 + jt + 1) * C], ps_r[:, :],
                        AF.Sigmoid),
                        reads=("ps_r",), writes=(f"rtm{cp}{jt}",))
                for jt in range(cw // 128):
                    tcol = t0 + jt * 128
                    for dc in range(8):
                        for nn in range(2):
                            tk.op("t", lambda t, dc=dc, nn=nn, tcol=tcol, t0=t0: t.matmul(
                                ps_kv[:, nn * 512:(nn + 1) * 512],
                                kact[:, dc * 512 + tcol - t0: dc * 512 + tcol - t0 + 128],
                                wvT[:, dc * 1024 + nn * 512: dc * 1024 + (nn + 1) * 512],
                                start=(dc == 0), stop=(dc == 7)),
                                reads=("kact", "wvT"), writes=("ps_kv",))
                    xsl = xtile[:, (jt % 2) * C:(jt % 2 + 1) * C]
                    xr_ = f"xt{jt % 2}"
                    rr = f"rtm{cp}{jt}"
                    rsl = r_tm[:, (cp * 4 + jt) * C:(cp * 4 + jt + 1) * C]
                    tk.op("y", lambda y, tcol=tcol, xsl=xsl: y.dma_start(
                        out=xsl, in_=x1o[tcol:tcol + 128, :]),
                        writes=(xr_,), dma=True)
                    tk.op("v", lambda v, rsl=rsl: v.tensor_mul(rsl, rsl, ps_kv[:, :]),
                        reads=(rr, "ps_kv"), writes=(rr,))
                    tk.op("v", lambda v, rsl=rsl, xsl=xsl: v.tensor_add(rsl, rsl, xsl),
                        reads=(rr, xr_), writes=(rr,))
                    tk.op("y", lambda y, rsl=rsl, tcol=tcol: y.dma_start(
                        out=x2o[tcol:tcol + 128, :], in_=rsl),
                        reads=(rr,), writes=("x2_out",), dma=True)
                    ti = ch * 4 + jt
                    ts = ti % 2
                    x2sl = rsl
                    xlo3 = xlt3[:, ts * C:(ts + 1) * C]
                    emit_ln_tile(tk, x2sl, rr, xlo3, f"xl3{ts}",
                                 st43[:, ts * 16:(ts + 1) * 16], epsc, sfx=f"3x{ts}")
                    for c in range(8):
                        pp = c % 2
                        psl = psT3[:, pp * 1024: pp * 1024 + 128]
                        s_ap = xlo3[:, c * 128:(c + 1) * 128]
                        d_ap = xT3[:, ts * 1024 + c * 128: ts * 1024 + (c + 1) * 128]
                        tk.op("t", lambda t, s_ap=s_ap, psl=psl: t.transpose(psl, s_ap, identbf[:]),
                              reads=(f"xl3{ts}", "identbf"), writes=(f"psT3{pp}",))
                        tk.op("v", lambda v, d_ap=d_ap, psl=psl: v.tensor_copy(d_ap, psl),
                              reads=(f"psT3{pp}",), writes=(f"xT3{ts}",))
                    gdst = ctb1.rearrange("(kc p) t -> p kc t", p=128)[:, :, ti * 128:(ti + 1) * 128]
                    gsrc = xT3[:, ts * 1024:(ts + 1) * 1024].rearrange("p (kc t) -> p kc t", kc=8)
                    tk.op("y", lambda y, gdst=gdst, gsrc=gsrc: y.dma_start(out=gdst, in_=gsrc),
                          reads=(f"xT3{ts}",), writes=("g1d",), dma=True)
    tk.barrier()


def emit_ln_tile(tk, xs, xr_, xlo, xlr, stats, epsc, rows=128, sfx=""):
    st = "st" + sfx
    b01 = stats[0:rows, 0:12]
    mv = stats[0:rows, 12:14]
    mn = stats[0:rows, 12:13]
    vr = stats[0:rows, 13:14]
    sC = stats[0:rows, 14:15]
    sD = stats[0:rows, 15:16]
    tk.op("v", lambda v: v.bn_stats(stats[0:rows, 0:6], xs[:, 0:C // 2]),
          reads=(xr_,), writes=(st,))
    tk.op("v", lambda v: v.bn_stats(stats[0:rows, 6:12], xs[:, C // 2:C]),
          reads=(xr_,), writes=(st,))
    tk.op("v", lambda v: v.bn_aggr(mv, b01), reads=(st,), writes=(st,))
    tk.op("s", lambda s: s.activation(sC, vr, AF.Sqrt, bias=epsc[0:rows, :]),
          reads=(st, "epsc"), writes=(st,))
    tk.op("v", lambda v: v.reciprocal(sC, sC), reads=(st,), writes=(st,))
    tk.op("v", lambda v: v.tensor_mul(sD, mn, sC), reads=(st,), writes=(st,))
    tk.op("v", lambda v: v.tensor_scalar(xlo, xs, sC, sD, op0=ALU.mult, op1=ALU.subtract),
          reads=(xr_, st), writes=(xlr,))



def emit_l1_attn(nc, tk, io, seq=T, nb=B, pfx="c_"):
    ntt = seq // 128
    nqc = seq // 512
    GS = 3

    Wq2, bq2 = io["Wq2"], io["bq2"]
    Wk2, bk2 = io["Wk2"], io["bk2"]
    Wv2, bv2 = io["Wv2"], io["bv2"]
    x2src = io["x2src"]
    ctb1 = io["ctb1"]
    g1g = io["g1g"]
    ctb2 = io["ctb2"]
    rcv = io["rcv"]

    def g1read(kc, gt0, w):
        r = gt0 // TOK
        off = gt0 % TOK
        return g1g[(r * 8 + kc) * 128:(r * 8 + kc + 1) * 128, off:off + w]

    ctx = ExitStack()
    with ctx:
        def sb(name, shape, dtype):
            return ctx.enter_context(nc.sbuf_tensor(pfx + name, shape, dtype))

        identbf = sb("identbf", [128, 128], BF16)
        identf = sb("identf", [128, 128], F32)
        ones1 = sb("ones1", [1, 128], F32)
        onesc = sb("onesc", [128, 1], BF16)
        wload = sb("wload", [64, C], BF16)
        wqT = sb("wqT", [128, 8 * 128], BF16)
        wkT = sb("wkT", [128, 8 * 128], BF16)
        wvT = sb("wvT", [128, 8 * 128], BF16)
        bqc = sb("bqc", [128, 2], F32)
        bvc = sb("bvc", [128, 1], F32)
        g1sb = sb("g1sb", [128, 2 * 8 * 512], BF16)
        qh = [sb(f"qh{b}", [128, seq], BF16) for b in range(nb)]
        kh = [sb(f"kh{b}", [128, seq], BF16) for b in range(nb)]
        vt = [[sb(f"vt{b}{h}", [128, ntt * 65], BF16) for h in range(2)]
              for b in range(nb)]
        vfm = sb("vfm", [128, 2 * 512], BF16)
        pT = sb("pT", [128, 2 * GS * 512], BF16)
        cstr = sb("cstr", [128, 4 * 512], BF16)
        sqs = sb("sqs", [128, 512], BF16)
        nmx = sb("nmx", [1, nb * 2 * 2 * nqc], F32)
        negC = sb("negC", [128, nb * 2], F32)
        oout = sb("oout", [65, 2 * 512], F32)
        oot2 = sb("oot2", [128, 2 * 260], F32)
        st4 = sb("st4", [128, 4], F32)

        with nc.psum_tensor(pfx + "psT", [128, 1024 + 64], BF16) as psT:
            tk.op("g", lambda g: g.memset(identf[:], 1.0), writes=("identf",))
            tk.op("g", lambda g: g.affine_select(identf[:], identf[:], pattern=[[-1, 128]],
                                                 compare_op=ALU.is_equal, fill=0.0, base=0,
                                                 channel_multiplier=1),
                  reads=("identf",), writes=("identf",))
            tk.op("v", lambda v: v.tensor_copy(identbf[:], identf[:]),
                  reads=("identf",), writes=("identbf",))
            tk.op("g", lambda g: g.memset(ones1[:], 1.0), writes=("ones1",))
            tk.op("g", lambda g: g.memset(onesc[:], 1.0), writes=("onesc",))
            tk.op("g", lambda g: g.memset(cstr[:], 0.0), writes=("cstr",))
            c3 = cstr[:].rearrange("p (d i) -> p d i", d=4)
            tk.op("g", lambda g, c3=c3: g.affine_select(
                c3, c3, pattern=[[-128, 4], [1, 512]], compare_op=ALU.is_ge,
                fill=NEG, base=0, channel_multiplier=-1),
                reads=("cstr",), writes=("cstr",))
            for b in range(nb):
                for h in range(2):
                    tk.op("g", lambda g, b=b, h=h: g.memset(vt[b][h][:], 1.0),
                          writes=(f"vt{b}{h}",))

            for W, wT, wtr in ((Wq2, wqT, "wqT"), (Wk2, wkT, "wkT"), (Wv2, wvT, "wvT")):
                for hh in range(2):
                    tk.op("g", lambda g, W=W, hh=hh: g.dma_start(
                        out=wload[:, :], in_=W[hh, :, :]), writes=("wload",), dma=True)
                    for kc in range(8):
                        pp = kc % 2
                        psl = psT[:, pp * 1024: pp * 1024 + 64]
                        s_ap = wload[:, kc * 128:(kc + 1) * 128]
                        d_ap = wT[:, kc * 128 + hh * 64: kc * 128 + (hh + 1) * 64]
                        tk.op("t", lambda t, s_ap=s_ap, psl=psl: t.transpose(
                            psl, s_ap, identbf[0:64, 0:64]),
                            reads=("wload", "identbf"), writes=(f"psT{pp}",))
                        tk.op("s", lambda s, d_ap=d_ap, psl=psl: s.copy(d_ap, psl),
                              reads=(f"psT{pp}",), writes=(wtr,))
            tk.op("y", lambda y: y.dma_start(
                out=bqc[:, 0:1], in_=bq2[:, :].rearrange("h (d o) -> (h d) o", o=1)),
                writes=("bqc",), dma=True)
            tk.op("y", lambda y: y.dma_start(
                out=bqc[:, 1:2], in_=bk2[:, :].rearrange("h (d o) -> (h d) o", o=1)),
                writes=("bqc",), dma=True)
            tk.op("v", lambda v: v.tensor_scalar_mul(bqc[:, 0:1], bqc[:, 0:1], 0.125),
                  reads=("bqc",), writes=("bqc",))
            tk.op("y", lambda y: y.dma_start(
                out=bvc[:, 0:1], in_=bv2[:, :].rearrange("h (d o) -> (h d) o", o=1)),
                writes=("bvc",), dma=True)

            tk.op("g", lambda g: g.collective_compute(
                "AllGather", ALU.bypass, replica_groups=[list(range(NCORES))],
                ins=[ctb1[:, :]], outs=[g1g[:, :]]),
                reads=("g1d",), writes=("g1gd",))

        tk.barrier()

        with nc.psum_tensor(pfx + "ps_q", [128, 1024], F32) as ps_q, \
             nc.psum_tensor(pfx + "psT2", [128, 1024 + 128], BF16) as psT2, \
             nc.psum_tensor(pfx + "ps_n", [64, 1024], F32) as ps_n:

            for b in range(nb):
                for i in range(nqc):
                    slot = i % 2
                    gsl = g1sb[:, slot * 4096:(slot + 1) * 4096]
                    for kc in range(8):
                        gr = f"g1s{slot}k{kc}"
                        gsrc = g1read(kc, b * seq + i * 512, 512)
                        eng = "y" if kc < 4 else "g"
                        tk.op(eng, lambda e, gsrc=gsrc, kc=kc, gsl=gsl: e.dma_start(
                            out=gsl[:, kc * 512:(kc + 1) * 512], in_=gsrc),
                            reads=("g1gd",), writes=(gr,), dma=True)
                    grs = tuple(f"g1s{slot}k{kc}" for kc in range(8))
                    for kc in range(8):
                        tk.op("t", lambda t, kc=kc, gsl=gsl: t.matmul(
                            ps_q[:, 0:512], wqT[:, kc * 128:(kc + 1) * 128],
                            gsl[:, kc * 512:(kc + 1) * 512],
                            start=(kc == 0), stop=(kc == 7)),
                            reads=("wqT", grs[kc]), writes=("psq0",))
                    tk.op("s", lambda s, b=b, i=i: s.activation(
                        qh[b][:, i * 512:(i + 1) * 512], ps_q[:, 0:512],
                        AF.Identity, bias=bqc[:, 0:1], scale=0.125),
                        reads=("psq0", "bqc"), writes=(f"qh{b}",))
                    for kc in range(8):
                        tk.op("t", lambda t, kc=kc, gsl=gsl: t.matmul(
                            ps_q[:, 512:1024], wkT[:, kc * 128:(kc + 1) * 128],
                            gsl[:, kc * 512:(kc + 1) * 512],
                            start=(kc == 0), stop=(kc == 7)),
                            reads=("wkT", grs[kc]), writes=("psq1",))
                    tk.op("s", lambda s, b=b, i=i: s.activation(
                        kh[b][:, i * 512:(i + 1) * 512], ps_q[:, 512:1024],
                        AF.Identity, bias=bqc[:, 1:2], scale=1.0),
                        reads=("psq1", "bqc"), writes=(f"kh{b}",))
                    for kc in range(8):
                        tk.op("t", lambda t, kc=kc, gsl=gsl: t.matmul(
                            ps_q[:, 0:512], wvT[:, kc * 128:(kc + 1) * 128],
                            gsl[:, kc * 512:(kc + 1) * 512],
                            start=(kc == 0), stop=(kc == 7)),
                            reads=("wvT", grs[kc]), writes=("psq0",))
                    tk.op("s", lambda s, slot=slot: s.activation(
                        vfm[:, slot * 512:(slot + 1) * 512], ps_q[:, 0:512],
                        AF.Identity, bias=bvc[:, 0:1], scale=1.0),
                        reads=("psq0", "bvc"), writes=(f"vfm{slot}",))
                    for jt in range(4):
                        ti = i * 4 + jt
                        pp = jt % 2
                        psl = psT2[:, pp * 1024: pp * 1024 + 128]
                        tk.op("t", lambda t, slot=slot, jt=jt, psl=psl: t.transpose(
                            psl, vfm[:, slot * 512 + jt * 128: slot * 512 + (jt + 1) * 128],
                            identbf[:]),
                            reads=(f"vfm{slot}", "identbf"), writes=(f"psT2{pp}",))
                        tk.op("v", lambda v, b=b, ti=ti, psl=psl: v.tensor_copy(
                            vt[b][0][:, ti * 65: ti * 65 + 64], psl[:, 0:64]),
                            reads=(f"psT2{pp}",), writes=(f"vt{b}0",))
                        tk.op("v", lambda v, b=b, ti=ti, psl=psl: v.tensor_copy(
                            vt[b][1][:, ti * 65: ti * 65 + 64], psl[:, 64:128]),
                            reads=(f"psT2{pp}",), writes=(f"vt{b}1",))
                    for srcbuf, srcr, base in ((qh[b], f"qh{b}", 0), (kh[b], f"kh{b}", 1)):
                        sl = srcbuf[:, i * 512:(i + 1) * 512]
                        tk.op("v", lambda v, sl=sl: v.tensor_mul(sqs[:, :], sl, sl),
                              reads=(srcr,), writes=("sqs",))
                        tk.op("t", lambda t: t.matmul(
                            ps_n[0:1, 0:512], onesc[0:64, :], sqs[0:64, :],
                            start=True, stop=True),
                            reads=("onesc", "sqs"), writes=("ps_n",))
                        tk.op("t", lambda t: t.matmul(
                            ps_n[0:1, 512:1024], onesc[64:128, :], sqs[64:128, :],
                            start=True, stop=True),
                            reads=("onesc", "sqs"), writes=("ps_n",))
                        for hh in range(2):
                            col = ((b * 2 + hh) * 2 + base) * nqc + i
                            tk.op("v", lambda v, hh=hh, col=col: v.reduce_max(
                                nmx[0:1, col:col + 1], ps_n[0:1, hh * 512:(hh + 1) * 512],
                                axis=AX),
                                reads=("ps_n",), writes=("nmx",))
            for b in range(nb):
                for hh in range(2):
                    u = b * 2 + hh
                    qm = nmx[0:1, (u * 2 + 0) * nqc: (u * 2 + 0) * nqc + nqc]
                    km = nmx[0:1, (u * 2 + 1) * nqc: (u * 2 + 1) * nqc + nqc]
                    tk.op("v", lambda v, qm=qm: v.reduce_max(st4[0:1, 0:1], qm, axis=AX),
                          reads=("nmx",), writes=("stb",))
                    tk.op("v", lambda v, km=km: v.reduce_max(st4[0:1, 1:2], km, axis=AX),
                          reads=("nmx",), writes=("stb",))
                    tk.op("v", lambda v: v.tensor_mul(st4[0:1, 0:1], st4[0:1, 0:1],
                                                      st4[0:1, 1:2]),
                          reads=("stb",), writes=("stb",))
                    tk.op("s", lambda s: s.activation(st4[0:1, 0:1], st4[0:1, 0:1], AF.Sqrt),
                          reads=("stb",), writes=("stb",))
                    tk.op("t", lambda t: t.matmul(ps_q[:, 0:1], ones1[0:1, :],
                                                  st4[0:1, 0:1], start=True, stop=True),
                          reads=("ones1", "stb"), writes=("psq0",))
                    tk.op("v", lambda v, u=u: v.tensor_scalar_mul(
                        negC[:, u:u + 1], ps_q[:, 0:1], -1.0),
                        reads=("psq0",), writes=("negC",))

        tk.barrier()

        ps_s = [ctx.enter_context(nc.psum_tensor(pfx + f"ps_s{s}", [128, GS * 512], F32))
                for s in range(2)]
        ps_o = [ctx.enter_context(nc.psum_tensor(pfx + f"ps_o{s}", [128, 512], F32))
                for s in range(2)]

        def emit_unit(b, hh, qc, g0, kts, gslot, first_av, last_av):
            kAP = kh[b]
            qAP = qh[b]
            u = b * 2 + hh
            for j, kt in enumerate(kts):
                tk.op("t", lambda t, j=j, kt=kt, qc=qc, hh=hh, kAP=kAP, qAP=qAP: t.matmul(
                    ps_s[gslot][:, j * 512:(j + 1) * 512],
                    kAP[hh * 64:(hh + 1) * 64, kt * 128:(kt + 1) * 128],
                    qAP[hh * 64:(hh + 1) * 64, qc * 512:(qc + 1) * 512],
                    start=True, stop=True),
                    reads=(f"kh{b}", f"qh{b}"), writes=(f"pss{gslot}",))
            for j, kt in enumerate(kts):
                if kt >= 4 * qc:
                    jj = kt - 4 * qc
                    tk.op("v", lambda v, j=j, jj=jj: v.tensor_add(
                        ps_s[gslot][:, j * 512:(j + 1) * 512],
                        ps_s[gslot][:, j * 512:(j + 1) * 512],
                        cstr[:, jj * 512:(jj + 1) * 512]),
                        reads=(f"pss{gslot}", "cstr"), writes=(f"pss{gslot}",))
            nk = len(kts)
            tk.op("s", lambda s, nk=nk, u=u: s.activation(
                pT[:, gslot * GS * 512: gslot * GS * 512 + nk * 512],
                ps_s[gslot][:, 0:nk * 512], AF.Exp, bias=negC[:, u:u + 1]),
                reads=(f"pss{gslot}", "negC"), writes=(f"pTg{gslot}",))

        def emit_av(b, hh, qc, g0, kts, gslot, first_av, last_av):
            oslot = qc % 2
            for j, kt in enumerate(kts):
                tk.op("t", lambda t, j=j, kt=kt, b=b, hh=hh,
                      fst=(first_av and j == 0), lst=(last_av and j == len(kts) - 1): t.matmul(
                          ps_o[oslot][0:65, :], vt[b][hh][:, kt * 65:(kt + 1) * 65],
                          pT[:, gslot * GS * 512 + j * 512: gslot * GS * 512 + (j + 1) * 512],
                          start=fst, stop=lst),
                      reads=(f"vt{b}{hh}", f"pTg{gslot}"), writes=(f"pso{oslot}",))
            if last_av:
                tk.op("v", lambda v, oslot=oslot: v.tensor_copy(
                    oout[0:65, oslot * 512:(oslot + 1) * 512], ps_o[oslot][0:65, :]),
                    reads=(f"pso{oslot}",), writes=(f"oouts{oslot}",))
                for cc in range(4):
                    tk.op("t", lambda t, oslot=oslot, cc=cc: t.transpose(
                        ps_o[oslot][:, cc * 65:(cc + 1) * 65],
                        oout[0:65, oslot * 512 + cc * 128: oslot * 512 + (cc + 1) * 128],
                        identf[0:65, 0:65]),
                        reads=(f"oouts{oslot}", "identf"), writes=(f"pso{oslot}",))
                tk.op("v", lambda v, oslot=oslot: v.tensor_copy(
                    oot2[:, oslot * 260:(oslot + 1) * 260], ps_o[oslot][:, 0:260]),
                    reads=(f"pso{oslot}",), writes=(f"oot2s{oslot}",))
                for cc in range(4):
                    gt = b * seq + qc * 512 + cc * 128
                    jd = gt // TOK
                    off = gt % TOK
                    tk.op("y", lambda y, jd=jd, off=off, hh=hh, oslot=oslot, cc=cc:
                          y.dma_start(
                              out=ctb2[jd * TOK + off: jd * TOK + off + 128,
                                       hh * 65:(hh + 1) * 65],
                              in_=oot2[:, oslot * 260 + cc * 65: oslot * 260 + (cc + 1) * 65]),
                          reads=(f"oot2s{oslot}",), writes=(f"outd{(qc + hh) % 4}",),
                          dma=True)

        units = []
        for b in range(nb):
            for hh in range(2):
                for qc in range(nqc):
                    ktmax = 4 * (qc + 1)
                    g0s = list(range(0, ktmax, GS))
                    for gidx, g0 in enumerate(g0s):
                        kts = list(range(g0, min(g0 + GS, ktmax)))
                        units.append((b, hh, qc, g0, kts, gidx == 0,
                                      gidx == len(g0s) - 1))
        for gi, (b, hh, qc, g0, kts, fa, la) in enumerate(units):
            emit_unit(b, hh, qc, g0, kts, gi % 2, fa, la)
            if gi >= 1:
                pb, phh, pqc, pg0, pkts, pfa, pla = units[gi - 1]
                emit_av(pb, phh, pqc, pg0, pkts, (gi - 1) % 2, pfa, pla)
        b, hh, qc, g0, kts, fa, la = units[-1]
        emit_av(b, hh, qc, g0, kts, (len(units) - 1) % 2, fa, la)

        tk.op("g", lambda g: g.collective_compute(
            "AllToAll", ALU.bypass, replica_groups=[list(range(NCORES))],
            ins=[ctb2[:, :]], outs=[rcv[:, :]]),
            reads=("outd0", "outd1", "outd2", "outd3"), writes=("rcvd",))
    tk.barrier()



def emit_l1_ffn(nc, tk, io, n_own=TOK, pfx="d_"):
    M, INNER = 64, 16

    x2s = io["x2s"]
    Win, bin_ = io["Win"], io["bin_"]
    W1, b1, W2, b2 = io["W1"], io["b1"], io["W2"], io["b2"]
    Wout, bout = io["Wout"], io["bout"]
    x4o = io["x4o"]
    rcv = io["rcv"]

    ctx = ExitStack()
    with ctx:
        def sb(name, shape, dtype):
            return ctx.enter_context(nc.sbuf_tensor(pfx + name, shape, dtype))

        identbf = sb("identbf", [128, 128], BF16)
        identf = sb("identf", [128, 128], F32)
        ones1 = sb("ones1", [1, 128], F32)
        epsc = sb("epsc", [128, 1], F32)
        win_sb = sb("win_sb", [128, 8 * M], BF16)
        wout_sb = sb("wout_sb", [M, C], BF16)
        w12 = sb("w12", [M, 3 * INNER + 2], F32)
        boutb = sb("boutb", [128, C], F32)
        brow = sb("brow", [1, C], F32)
        x3b = sb("x3b", [128, 8 * C], F32)
        atile = sb("atile", [128, 2 * 8 * 130], F32)
        xlt = sb("xlt", [128, 2 * C], BF16)
        xT = sb("xT", [128, 2 * 8 * 512], BF16)
        s_f = sb("s_f", [M, 2 * 512], F32)
        h_f = sb("h_f", [M, 4 * 512], F32)
        g_f = sb("g_f", [M, 2 * 512], F32)
        g_bf = sb("g_bf", [M, 2 * 512], BF16)
        o_t = sb("o_t", [128, 2 * C], F32)
        linv = sb("linv", [128, 2 * 16], F32)
        st4 = sb("st4", [128, 2 * 16], F32)

        with nc.psum_tensor(pfx + "ps_s", [64, 2 * 512], F32) as ps_s, \
             nc.psum_tensor(pfx + "ps_out", [128, 2 * 1024], F32) as ps_out, \
             nc.psum_tensor(pfx + "ps_b", [128, 512], F32) as ps_b, \
             nc.psum_tensor(pfx + "psT", [128, 128], BF16) as psT:

            tk.op("g", lambda g: g.memset(identf[:], 1.0), writes=("identf",))
            tk.op("g", lambda g: g.affine_select(identf[:], identf[:], pattern=[[-1, 128]],
                                                 compare_op=ALU.is_equal, fill=0.0, base=0,
                                                 channel_multiplier=1),
                  reads=("identf",), writes=("identf",))
            tk.op("v", lambda v: v.tensor_copy(identbf[:], identf[:]),
                  reads=("identf",), writes=("identbf",))
            tk.op("g", lambda g: g.memset(ones1[:], 1.0), writes=("ones1",))
            tk.op("g", lambda g: g.memset(epsc[:], 1e-5), writes=("epsc",))
            tk.op("g", lambda g: g.dma_start(out=win_sb[:].rearrange("p (kc m) -> p kc m", kc=8),
                                             in_=Win[:, :].rearrange("(kc p) m -> p kc m", p=128)),
                  writes=("win_sb",), dma=True)
            tk.op("g", lambda g: g.dma_start(out=wout_sb[:, :], in_=Wout[:, :]),
                  writes=("wout_sb",), dma=True)
            tk.op("y", lambda y: y.dma_start(out=w12[:, 0:INNER], in_=W1[:, :]),
                  writes=("w12",), dma=True)
            tk.op("y", lambda y: y.dma_start(out=w12[:, INNER:2 * INNER], in_=b1[:, :]),
                  writes=("w12",), dma=True)
            tk.op("y", lambda y: y.dma_start(out=w12[:, 2 * INNER:3 * INNER], in_=W2[:, :]),
                  writes=("w12",), dma=True)
            tk.op("y", lambda y: y.dma_start(
                out=w12[:, 3 * INNER:3 * INNER + 1],
                in_=bin_[:].rearrange("(m o) -> m o", o=1)),
                writes=("w12",), dma=True)
            tk.op("y", lambda y: y.dma_start(
                out=w12[:, 3 * INNER + 1:3 * INNER + 2],
                in_=b2[:].rearrange("(m o) -> m o", o=1)),
                writes=("w12",), dma=True)
            tk.op("y", lambda y: y.dma_start(out=brow[0:1, :],
                                             in_=bout[:].rearrange("(o n) -> o n", o=1)),
                  writes=("brow",), dma=True)
            for i in range(2):
                tk.op("t", lambda t, i=i: t.matmul(ps_b[:, :], ones1[0:1, :],
                                                   brow[0:1, i * 512:(i + 1) * 512],
                                                   start=True, stop=True),
                      reads=("ones1", "brow"), writes=("ps_b",))
                tk.op("s", lambda s, i=i: s.copy(boutb[:, i * 512:(i + 1) * 512], ps_b[:, :]),
                      reads=("ps_b",), writes=("boutb",))

            binc = w12[:, 3 * INNER:3 * INNER + 1]
            b2c = w12[:, 3 * INNER + 1:3 * INNER + 2]

            for ch in range(cdiv(n_own, 512)):
                cs = ch % 2
                t0 = ch * 512
                cw = min(512, n_own - t0)
                for jt in range(cw // 128):
                    ti = 4 * ch + jt
                    ts = ti % 2
                    tc = t0 + jt * 128
                    x3s = x3b[:, (cs * 4 + jt) * C:(cs * 4 + jt + 1) * C]
                    x3r = f"x3{cs}{jt}"
                    tk.op("y", lambda y, tc=tc, x3s=x3s: y.dma_start(
                        out=x3s, in_=x2s[tc:tc + 128, :]), writes=(x3r,), dma=True)
                    ats = tuple(f"at{ts}{r}" for r in range(8))
                    for r in range(8):
                        asl = atile[:, (ts * 8 + r) * 130:(ts * 8 + r + 1) * 130]
                        eng = "y" if r % 2 == 0 else "g"
                        tk.op(eng, lambda e, r=r, tc=tc, asl=asl: e.dma_start(
                            out=asl, in_=rcv[r * TOK + tc: r * TOK + tc + 128, :]),
                            reads=("rcvd",), writes=(ats[r],), dma=True)
                    a3 = atile[:, ts * 1040:(ts + 1) * 1040].rearrange(
                        "p (r c) -> p r c", c=130)
                    for hh in range(2):
                        dst = linv[:, ts * 16 + hh * 8: ts * 16 + (hh + 1) * 8].rearrange(
                            "p (r o) -> p r o", o=1)
                        lsrc = a3[:, :, hh * 65 + 64: hh * 65 + 65]
                        tk.op("v", lambda v, dst=dst, lsrc=lsrc: v.reciprocal(dst, lsrc),
                              reads=ats, writes=(f"linv{ts}",))
                    for r in range(8):
                        asl = atile[:, (ts * 8 + r) * 130:(ts * 8 + r + 1) * 130]
                        for hh in range(2):
                            lcol = linv[:, ts * 16 + hh * 8 + r: ts * 16 + hh * 8 + r + 1]
                            tk.op("v", lambda v, r=r, hh=hh, asl=asl, lcol=lcol, x3s=x3s:
                                  v.scalar_tensor_tensor(
                                      x3s[:, (2 * r + hh) * 64:(2 * r + hh + 1) * 64],
                                      asl[:, hh * 65: hh * 65 + 64], lcol,
                                      x3s[:, (2 * r + hh) * 64:(2 * r + hh + 1) * 64],
                                      op0=ALU.mult, op1=ALU.add),
                                  reads=(ats[r], f"linv{ts}", x3r), writes=(x3r,))
                    xlo = xlt[:, ts * C:(ts + 1) * C]
                    xlr = f"xlt{ts}"
                    emit_ln_tile(tk, x3s, x3r, xlo, xlr,
                                 st4[:, ts * 16:(ts + 1) * 16], epsc, sfx=str(ts))
                    for c in range(8):
                        s_ap = xlo[:, c * 128:(c + 1) * 128]
                        d_ap = xT[:, cs * 4096 + c * 512 + jt * 128:
                                  cs * 4096 + c * 512 + (jt + 1) * 128]
                        tk.op("t", lambda t, s_ap=s_ap: t.transpose(psT[:], s_ap, identbf[:]),
                              reads=(xlr, "identbf"), writes=("psT",))
                        tk.op("s", lambda s, d_ap=d_ap: s.copy(d_ap, psT[:]),
                              reads=("psT",), writes=(f"xT{cs}",))
                for kc in range(8):
                    tk.op("t", lambda t, kc=kc, cw=cw, cs=cs: t.matmul(
                        ps_s[:, cs * 512: cs * 512 + cw], win_sb[:, kc * M:(kc + 1) * M],
                        xT[:, cs * 4096 + kc * 512: cs * 4096 + kc * 512 + cw],
                        start=(kc == 0), stop=(kc == 7)),
                        reads=("win_sb", f"xT{cs}"), writes=(f"ps_s{cs}",))
                tk.op("s", lambda s, cw=cw, cs=cs: s.activation(
                    s_f[:, cs * 512: cs * 512 + cw], ps_s[:, cs * 512: cs * 512 + cw],
                    AF.Identity, bias=binc, scale=1.0),
                    reads=(f"ps_s{cs}", "w12"), writes=(f"s_f{cs}",))
                for i in range(INNER):
                    hs = cs * 2 + (i % 2)
                    w1c = w12[:, i:i + 1]
                    b1c = w12[:, INNER + i:INNER + i + 1]
                    w2c = w12[:, 2 * INNER + i:2 * INNER + i + 1]
                    tk.op("s", lambda s, w1c=w1c, b1c=b1c, cw=cw, cs=cs, hs=hs: s.activation(
                        h_f[:, hs * 512: hs * 512 + cw], s_f[:, cs * 512: cs * 512 + cw],
                        AF.Relu, bias=b1c, scale=w1c),
                        reads=(f"s_f{cs}", "w12"), writes=(f"h_f{hs}",))
                    if i == 0:
                        tk.op("v", lambda v, w2c=w2c, cw=cw, cs=cs, hs=hs: v.tensor_scalar(
                            g_f[:, cs * 512: cs * 512 + cw], h_f[:, hs * 512: hs * 512 + cw],
                            w2c, None, op0=ALU.mult),
                            reads=(f"h_f{hs}", "w12"), writes=(f"g_f{cs}",))
                    else:
                        tk.op("v", lambda v, w2c=w2c, cw=cw, cs=cs, hs=hs: v.scalar_tensor_tensor(
                            g_f[:, cs * 512: cs * 512 + cw], h_f[:, hs * 512: hs * 512 + cw],
                            w2c, g_f[:, cs * 512: cs * 512 + cw],
                            op0=ALU.mult, op1=ALU.add),
                            reads=(f"h_f{hs}", f"g_f{cs}", "w12"), writes=(f"g_f{cs}",))
                tk.op("v", lambda v, cw=cw, cs=cs: v.tensor_scalar(
                    g_bf[:, cs * 512: cs * 512 + cw], g_f[:, cs * 512: cs * 512 + cw],
                    b2c, None, op0=ALU.add),
                    reads=(f"g_f{cs}", "w12"), writes=(f"g_bf{cs}",))
                for jt in range(cw // 128):
                    ts = jt % 2
                    tc = t0 + jt * 128
                    x3s = x3b[:, (cs * 4 + jt) * C:(cs * 4 + jt + 1) * C]
                    x3r = f"x3{cs}{jt}"
                    for nn in range(2):
                        tk.op("t", lambda t, jt=jt, nn=nn, cs=cs: t.matmul(
                            ps_out[:, cs * 1024 + nn * 512: cs * 1024 + (nn + 1) * 512],
                            g_bf[:, cs * 512 + jt * 128: cs * 512 + (jt + 1) * 128],
                            wout_sb[:, nn * 512:(nn + 1) * 512],
                            start=True, stop=True),
                            reads=(f"g_bf{cs}", "wout_sb"), writes=(f"ps_out{cs}",))
                    osl = o_t[:, ts * C:(ts + 1) * C]
                    orr = f"o_t{ts}"
                    tk.op("v", lambda v, osl=osl, cs=cs: v.tensor_add(
                        osl, ps_out[:, cs * 1024:(cs + 1) * 1024], boutb[:, :]),
                        reads=(f"ps_out{cs}", "boutb"), writes=(orr,))
                    tk.op("v", lambda v, osl=osl, x3s=x3s: v.tensor_add(osl, osl, x3s),
                          reads=(orr, x3r), writes=(orr,))
                    tk.op("y", lambda y, tc=tc, osl=osl: y.dma_start(
                        out=x4o[tc:tc + 128, :], in_=osl),
                        reads=(orr,), writes=("x4_out",), dma=True)
            tk.op("y", lambda y: y.nop(nofuse=True), reads=("x4_out",))
    tk.barrier()



def build_megakernel():
    nc = bass.Bass("TRN2", target_bir_lowering=False)
    M, INNER = 64, 16
    io = dict(
        xo=nc.dram_tensor("xo", [TOK, C], F32, kind="ExternalInput"),
        xh=nc.dram_tensor("xh", [HALO, C], F32, kind="ExternalInput"),
        Wqk=nc.dram_tensor("Wqk", [C, 2 * C], F32, kind="ExternalInput"),
        bqk=nc.dram_tensor("bqk", [2 * C], F32, kind="ExternalInput"),
        Wgate=nc.dram_tensor("Wgate", [C, NHEAD], F32, kind="ExternalInput"),
        bgate=nc.dram_tensor("bgate", [NHEAD], F32, kind="ExternalInput"),
        Wproj=nc.dram_tensor("Wproj", [C, C], F32, kind="ExternalInput"),
        bproj=nc.dram_tensor("bproj", [C], F32, kind="ExternalInput"),
        W_r=nc.dram_tensor("W_r", [C, C], F32, kind="ExternalInput"),
        W_k=nc.dram_tensor("W_k", [C, C], F32, kind="ExternalInput"),
        W_v=nc.dram_tensor("W_v", [C, C], F32, kind="ExternalInput"),
        mu_r=nc.dram_tensor("mu_r", [C], F32, kind="ExternalInput"),
        mu_k=nc.dram_tensor("mu_k", [C], F32, kind="ExternalInput"),
        Wq2=nc.dram_tensor("Wq2", [2, HD, C], F32, kind="ExternalInput"),
        bq2=nc.dram_tensor("bq2", [2, HD], F32, kind="ExternalInput"),
        Wk2=nc.dram_tensor("Wk2", [2, HD, C], F32, kind="ExternalInput"),
        bk2=nc.dram_tensor("bk2", [2, HD], F32, kind="ExternalInput"),
        Wv2=nc.dram_tensor("Wv2", [2, HD, C], F32, kind="ExternalInput"),
        bv2=nc.dram_tensor("bv2", [2, HD], F32, kind="ExternalInput"),
        Win=nc.dram_tensor("Win", [C, M], F32, kind="ExternalInput"),
        bin_=nc.dram_tensor("bin_", [M], F32, kind="ExternalInput"),
        W1=nc.dram_tensor("W1", [M, INNER], F32, kind="ExternalInput"),
        b1=nc.dram_tensor("b1", [M, INNER], F32, kind="ExternalInput"),
        W2=nc.dram_tensor("W2", [M, INNER], F32, kind="ExternalInput"),
        b2=nc.dram_tensor("b2", [M], F32, kind="ExternalInput"),
        Wout=nc.dram_tensor("Wout", [M, C], F32, kind="ExternalInput"),
        bout=nc.dram_tensor("bout", [C], F32, kind="ExternalInput"),
        x4o=nc.dram_tensor("xout", [TOK, C], F32, kind="ExternalOutput"),
    )
    x1d = nc.dram_tensor("x1d", [TOK + 1, C], F32)
    x2d = nc.dram_tensor("x2d", [TOK, C], F32)
    io["ldram"] = nc.dram_tensor("ldram", [1, 16 * (TOK + 128)], F32)
    io["x1o"] = x1d[1:TOK + 1, :]
    io["x1p"] = x1d[0:1, :]
    io["x2o"] = x2d
    io["x2src"] = x2d
    io["x2s"] = x2d
    io["ctb1"] = nc.dram_tensor("ctb1", [8 * 128, TOK], BF16)
    io["g1g"] = nc.dram_tensor("g1g", [NCORES * 8 * 128, TOK], BF16, addr_space="Shared")
    io["ctb2"] = nc.dram_tensor("ctb2", [8 * TOK, 130], F32)
    io["rcv"] = nc.dram_tensor("rcv", [8 * TOK, 130], F32)

    ctx = ExitStack()
    with ctx:
        tk = TK(nc, ctx)
        block = ctx.enter_context(nc.Block())
        emit_l0_attn(nc, tk, io, n_own=TOK, pfx="a_")
        emit_l0_ffn(nc, tk, io, n_own=TOK, pfx="b_")
        emit_l1_attn(nc, tk, io, seq=T, nb=B, pfx="c_")
        emit_l1_ffn(nc, tk, io, n_own=TOK, pfx="d_")
        tk.emit(block)
    return nc


def kernel(L0_attn_x, L0_attn_Wqk, L0_attn_bqk, L0_attn_Wgate, L0_attn_bgate,
           L0_attn_Wproj, L0_attn_bproj, L0_ffn_W_r, L0_ffn_W_k, L0_ffn_W_v,
           L0_ffn_mu_r, L0_ffn_mu_k, L1_attn_Wq, L1_attn_bq, L1_attn_Wk,
           L1_attn_bk, L1_attn_Wv, L1_attn_bv, L1_ffn_Win, L1_ffn_bin_,
           L1_ffn_W1, L1_ffn_b1, L1_ffn_W2, L1_ffn_b2, L1_ffn_Wout, L1_ffn_bout):
    cores = list(range(NCORES))
    xf = np.ascontiguousarray(np.asarray(L0_attn_x, dtype=np.float32).reshape(B * T, C))

    def cc(a):
        return np.ascontiguousarray(a, dtype=np.float32)

    nc = build_megakernel()
    maps = []
    for c in cores:
        q = c % 4
        sl = slice(2 * c, 2 * c + 2)
        maps.append(dict(
            xo=cc(xf[c * TOK:(c + 1) * TOK]),
            xh=(np.zeros((HALO, C), np.float32) if q == 0
                else cc(xf[c * TOK - HALO:c * TOK])),
            Wqk=cc(L0_attn_Wqk), bqk=cc(L0_attn_bqk),
            Wgate=cc(L0_attn_Wgate), bgate=cc(L0_attn_bgate),
            Wproj=cc(L0_attn_Wproj), bproj=cc(L0_attn_bproj),
            W_r=cc(L0_ffn_W_r), W_k=cc(L0_ffn_W_k), W_v=cc(L0_ffn_W_v),
            mu_r=cc(L0_ffn_mu_r), mu_k=cc(L0_ffn_mu_k),
            Wq2=cc(L1_attn_Wq[sl]), bq2=cc(L1_attn_bq[sl]),
            Wk2=cc(L1_attn_Wk[sl]), bk2=cc(L1_attn_bk[sl]),
            Wv2=cc(L1_attn_Wv[sl]), bv2=cc(L1_attn_bv[sl]),
            Win=cc(L1_ffn_Win), bin_=cc(L1_ffn_bin_),
            W1=cc(L1_ffn_W1), b1=cc(L1_ffn_b1),
            W2=cc(L1_ffn_W2), b2=cc(L1_ffn_b2),
            Wout=cc(L1_ffn_Wout), bout=cc(L1_ffn_bout)))
    res = run_bass_kernel_spmd(nc, maps, core_ids=cores)
    out = np.concatenate([np.asarray(r["xout"]) for r in res.results], axis=0)
    return out.reshape(B, T, C)
